# version 11
# baseline (speedup 1.0000x reference)
import math
from contextlib import ExitStack

import numpy as np
import concourse.bass as bass
import concourse.mybir as mybir
from concourse.bass_utils import run_bass_kernel_spmd

F32 = mybir.dt.float32
F16 = mybir.dt.float16
I32 = mybir.dt.int32
ALU = mybir.AluOpType
AF = mybir.ActivationFunctionType
AX = mybir.AxisListType

P = 128
D = 1024
NH = 16
HD = 64
S = 4096
TOWN = 2048
TEXTRA = 128
TCAT = S + TEXTRA
TQ = TOWN + TEXTRA
NT_CAT = TCAT // P
NT_STAT = S // P
NT_OWN = TOWN // P
DC = D // P
HFF = 4096
HFC = HFF // P
EPS = 1e-5
N_CORES = 8
B = 4

BUILD_PHASES = 99
DEBUG = False


class Sem:
    def __init__(self, es, nc, name):
        self.h = es.enter_context(nc.semaphore(name))
        self.n = 0


class Emitter:

    def __init__(self, nc, es):
        self.nc = nc
        self.es = es
        self.prog = {k: [] for k in ("sync", "vector", "scalar", "tensor",
                                     "gpsimd")}
        self.sems = {}

    def sem(self, name):
        if name not in self.sems:
            self.sems[name] = Sem(self.es, self.nc, name)
        return self.sems[name]

    def op(self, eng, f, sem=None, k=1):
        if sem is not None:
            s = self.sem(sem) if isinstance(sem, str) else sem
            s.n += k
            h = s.h

            def run(e, f=f, h=h, k=k):
                f(e).then_inc(h, k)
        else:
            def run(e, f=f):
                f(e)
        self.prog[eng].append(run)

    def wait(self, eng, sem, val=None):
        s = self.sem(sem) if isinstance(sem, str) else sem
        v = s.n if val is None else val
        self.prog[eng].append(lambda e, h=s.h, v=v: e.wait_ge(h, v))

    def barrier(self, eng_wait, sems):
        for sname in sems:
            self.wait(eng_wait, sname)

    def wait_all(self, eng):
        for s in self.sems.values():
            self.wait(eng, s)

    def dma(self, eng, f, sem):
        self.wait(eng, sem)
        self.op(eng, f, sem, 16)

    def phase_barrier(self):
        for eng in self.prog:
            for s in self.sems.values():
                self.wait(eng, s)


def ceil_div(a, b):
    return (a + b - 1) // b


def build_program():
    nc = bass.Bass("TRN2", target_bir_lowering=False)

    XC = nc.dram_tensor("XC", [TCAT, D], F32, kind="ExternalInput").ap()
    WQT = nc.dram_tensor("WQT", [D, D], F32, kind="ExternalInput").ap()
    WKT = nc.dram_tensor("WKT", [D, D], F32, kind="ExternalInput").ap()
    WVT = nc.dram_tensor("WVT", [D, D], F32, kind="ExternalInput").ap()
    WOT = nc.dram_tensor("WOT", [D, D], F32, kind="ExternalInput").ap()
    W1B = nc.dram_tensor("W1B", [HFC, P, D], F32, kind="ExternalInput").ap()
    W2B = nc.dram_tensor("W2B", [DC, P, HFF], F32, kind="ExternalInput").ap()
    B1T = nc.dram_tensor("B1T", [P, HFC], F32, kind="ExternalInput").ap()
    B2T = nc.dram_tensor("B2T", [P, DC], F32, kind="ExternalInput").ap()
    SCL = nc.dram_tensor("SCL", [1, 1], F32, kind="ExternalInput").ap()
    IDF32 = nc.dram_tensor("IDF32", [P, P], F32, kind="ExternalInput").ap()
    IDF16 = nc.dram_tensor("IDF16", [P, P], F16, kind="ExternalInput").ap()
    OUT = nc.dram_tensor("OUT", [TOWN, D], F32, kind="ExternalOutput").ap()

    _sk = "ExternalOutput" if DEBUG else "Internal"
    KH = nc.dram_tensor("KHs", [D, S], F16, kind=_sk).ap()
    KL = nc.dram_tensor("KLs", [D, S], F16, kind=_sk).ap()
    VSP = nc.dram_tensor("VSPs", [NH // 2, P, 32 * 130], F32, kind=_sk).ap()
    QH = nc.dram_tensor("QHs", [D, TQ], F16, kind=_sk).ap()
    QL = nc.dram_tensor("QLs", [D, TQ], F16, kind=_sk).ap()
    X1SP = nc.dram_tensor("X1SPs", [TOWN, D], F32, kind="Internal").ap()
    W1H = nc.dram_tensor("W1Hs", [HFC, D, P], F16, kind="Internal").ap()
    W1L = nc.dram_tensor("W1Ls", [HFC, D, P], F16, kind="Internal").ap()
    W2H = nc.dram_tensor("W2Hs", [DC, HFF, P], F16, kind="Internal").ap()
    W2L = nc.dram_tensor("W2Ls", [DC, HFF, P], F16, kind="Internal").ap()
    CCI = nc.dram_tensor("CCIs", [1, 8], F32, kind="Internal")
    CCO = nc.dram_tensor("CCOs", [1, 8], F32, kind="Internal")
    LRSPH = nc.dram_tensor("LRSPH", [8, D], F16, kind="Internal").ap()
    LRSPL = nc.dram_tensor("LRSPL", [8, D], F16, kind="Internal").ap()
    PRSP = nc.dram_tensor("PRSP", [4, 512], F32, kind="Internal").ap()
    RRSP = nc.dram_tensor("RRSP", [4, 512], F32, kind="Internal").ap()

    dbg = {}
    if DEBUG:
        dbg["KT"] = nc.dram_tensor("DBG_KT", [P, DC * TCAT], F16,
                                   kind="ExternalOutput").ap()
        dbg["ST"] = nc.dram_tensor("DBG_ST", [P, 16], F32,
                                   kind="ExternalOutput").ap()
        dbg["Q"] = nc.dram_tensor("DBG_Q", [D, TQ], F32,
                                  kind="ExternalOutput").ap()
        dbg["AO"] = nc.dram_tensor("DBG_AO", [TQ, D], F32,
                                   kind="ExternalOutput").ap()
        dbg["X1"] = nc.dram_tensor("DBG_X1", [TQ, D], F32,
                                   kind="ExternalOutput").ap()

    with ExitStack() as es:
        e = es.enter_context
        em = Emitter(nc, es)

        xa = [e(nc.sbuf_tensor(f"xa{i}", [P, D], F32)) for i in range(2)]
        scr = e(nc.sbuf_tensor("scr", [P, D], F32))
        scr2 = e(nc.sbuf_tensor("scr2", [P, D], F32))
        nmean = e(nc.sbuf_tensor("nmean", [P, NT_CAT], F32))
        rstd = e(nc.sbuf_tensor("rstd", [P, NT_CAT], F32))
        vssum = e(nc.sbuf_tensor("vssum", [P, NT_CAT], F32))
        hmax = e(nc.sbuf_tensor("hmax", [P, NT_CAT], F32))
        hmin = e(nc.sbuf_tensor("hmin", [P, NT_CAT], F32))
        s1c = e(nc.sbuf_tensor("s1c", [P, NT_CAT], F32))
        s2c = e(nc.sbuf_tensor("s2c", [P, NT_CAT], F32))
        ksq = e(nc.sbuf_tensor("ksq", [P, NT_CAT], F32))
        invsx = e(nc.sbuf_tensor("invsx", [P, NT_CAT], F32))
        sxc = e(nc.sbuf_tensor("sxc", [P, NT_CAT], F32))
        tl1 = e(nc.sbuf_tensor("tl1", [P, 1], F32))
        tl2 = e(nc.sbuf_tensor("tl2", [P, 1], F32))
        tl3 = e(nc.sbuf_tensor("tl3", [P, 1], F32))
        tl4 = e(nc.sbuf_tensor("tl4", [P, 1], F32))
        glob = e(nc.sbuf_tensor("glob", [P, 16], F32))
        ones1 = e(nc.sbuf_tensor("ones1", [1, P], F32))
        kescl = e(nc.sbuf_tensor("kescl", [P, NT_STAT], F32))
        wacc = e(nc.sbuf_tensor("wacc", [P, 1], F32))
        ident32 = e(nc.sbuf_tensor("ident32", [P, P], F32))
        ident16 = e(nc.sbuf_tensor("ident16", [P, P], F16))
        psum = [e(nc.psum_tensor(f"ps{i}", [P, 512], F32)) for i in range(8)]
        fp_es = ExitStack()
        fe = fp_es.enter_context
        h1b = fe(nc.sbuf_tensor("h1b", [P, D], F32))
        yb = fe(nc.sbuf_tensor("yb", [P, D], F32))
        eb = fe(nc.sbuf_tensor("eb", [P, D], F32))
        cb = fe(nc.sbuf_tensor("cb", [P, D], F32))
        db = fe(nc.sbuf_tensor("db", [P, D], F32))
        k1b = fe(nc.sbuf_tensor("k1b", [P, D], F32))
        ki32 = fe(nc.sbuf_tensor("ki32", [P, D], I32))
        kf16 = fe(nc.sbuf_tensor("kf16", [P, D], F16))
        kT = fe(nc.sbuf_tensor("kT", [P, DC * TCAT], F16))
        tern = fe(nc.sbuf_tensor("tern", [P, DC * D], F16))
        wbuf = [fe(nc.sbuf_tensor(f"wbuf{i}", [P, D], F32)) for i in range(2)]
        pevac = fe(nc.sbuf_tensor("pevac", [P, 512], F32))
        pv16a = fe(nc.sbuf_tensor("pv16a", [P, 512], F16))
        pv16b = fe(nc.sbuf_tensor("pv16b", [P, 512], F16))
        vstg = [fe(nc.sbuf_tensor(f"vstg{i}", [P, 520], F32))
                for i in range(2)]
        vscl = fe(nc.sbuf_tensor("vscl", [P, NT_STAT], F32))
        qscl = fe(nc.sbuf_tensor("qscl", [P, NT_CAT], F32))
        qnsq = fe(nc.sbuf_tensor("qnsq", [P, 17], F32))
        vtmp = fe(nc.sbuf_tensor("vtmp", [P, P], F32))
        qstg = fe(nc.sbuf_tensor("qstg", [P, 512], F32))
        qstg2 = fe(nc.sbuf_tensor("qstg2", [P, 512], F32))

        em.op("vector", lambda v: v.memset(glob[:], 0.0), "vA")
        em.op("vector", lambda v: v.memset(ones1[:], 1.0), "vA")
        em.dma("sync", lambda sy: sy.dma_start(out=ident32[:], in_=IDF32),
               "dma")
        em.dma("sync", lambda sy: sy.dma_start(out=ident16[:], in_=IDF16),
               "dma")
        em.wait("tensor", "dma")

        def emit_round_t(vsem, src, dst_f32, T):
            _ki, _eb, _cb, _db, _k1 = (T["ki32"], T["eb"], T["cb"], T["db"],
                                       T["k1b"])
            em.wait("vector", vsem)
            em.op("vector", lambda v: v.tensor_copy(_ki[:], src[:]), vsem)
            em.wait("vector", vsem)
            em.op("vector", lambda v: v.scalar_tensor_tensor(
                out=_eb[:], in0=src[:], scalar=1.0, in1=_ki[:],
                op0=ALU.mult, op1=ALU.subtract), vsem)
            em.wait("vector", vsem)
            em.op("vector", lambda v: v.tensor_single_scalar(
                out=_cb[:], in_=_eb[:], scalar=0.5, op=ALU.is_gt), vsem)
            em.op("vector", lambda v: v.tensor_single_scalar(
                out=_db[:], in_=_eb[:], scalar=-0.5, op=ALU.is_lt), vsem)
            em.wait("vector", vsem)
            em.op("vector", lambda v: v.scalar_tensor_tensor(
                out=_k1[:], in0=_cb[:], scalar=1.0, in1=_ki[:],
                op0=ALU.mult, op1=ALU.add), vsem)
            em.wait("vector", vsem)
            em.op("vector", lambda v: v.scalar_tensor_tensor(
                out=dst_f32[:], in0=_db[:], scalar=-1.0, in1=_k1[:],
                op0=ALU.mult, op1=ALU.add), vsem)

        def emit_round(vsem, src, dst_f32):
            emit_round_t(vsem, src, dst_f32,
                         dict(ki32=ki32, eb=eb, cb=cb, db=db, k1b=k1b))

        def emit_round2(vsem, src, out_f16, T, extra_waits=()):
            _ki, _eb, _db, _k1 = T["ki32"], T["eb"], T["db"], T["k1b"]
            em.wait("vector", vsem)
            em.op("vector", lambda v: v.tensor_copy(_ki[:], src[:]), vsem)
            em.wait("vector", vsem)
            em.op("vector", lambda v: v.scalar_tensor_tensor(
                out=_eb[:], in0=src[:], scalar=1.0, in1=_ki[:],
                op0=ALU.mult, op1=ALU.subtract), vsem)
            em.wait("vector", vsem)
            em.op("vector", lambda v: v.tensor_single_scalar(
                out=_db[:], in_=_eb[:], scalar=-0.5, op=ALU.is_lt), vsem)
            em.op("vector", lambda v: v.scalar_tensor_tensor(
                out=_k1[:], in0=_eb[:], scalar=0.5, in1=_ki[:],
                op0=ALU.is_gt, op1=ALU.add), vsem)
            em.wait("vector", vsem)
            for w_ in extra_waits:
                em.wait("vector", w_)
            em.op("vector", lambda v: v.scalar_tensor_tensor(
                out=out_f16[:], in0=_db[:], scalar=-1.0, in1=_k1[:],
                op0=ALU.mult, op1=ALU.add), vsem)

        def cross_part_reduce(src_col, dst_ap, op, psum_bank):
            em.wait_all("tensor")
            em.wait_all("vector")
            em.op("vector", lambda v: v.tensor_copy(scr[:, 4:5], src_col),
                  "vP")
            em.wait("tensor", "vP")
            em.op("tensor", lambda t: t.transpose(
                psum_bank[0:1, 0:P], scr[:, 4:5], ident32[:]), "tP")
            em.wait("vector", "tP")
            em.op("vector", lambda v: v.tensor_reduce(
                out=dst_ap, in_=psum_bank[0:1, 0:P], axis=AX.X, op=op), "vP")

        def bcast_col(src11, dst_col, psum_bank):
            em.wait_all("tensor")
            em.wait_all("vector")
            em.op("tensor", lambda t: t.matmul(
                psum_bank[:, 0:1], ones1[:], src11, start=True, stop=True),
                "tP")
            em.wait("vector", "tP")
            em.op("vector", lambda v: v.tensor_copy(dst_col, psum_bank[:, 0:1]),
                  "vP")

        def recip_refined(val_col, out_col, iters=1):
            em.wait("vector", "vP")
            em.op("vector", lambda v: v.reciprocal(tl3[:], val_col), "vP")
            for _ in range(iters):
                em.wait("vector", "vP")
                em.op("vector", lambda v: v.tensor_tensor(
                    out=tl4[:], in0=val_col, in1=tl3[:], op=ALU.mult), "vP")
                em.wait("vector", "vP")
                em.op("vector", lambda v: v.tensor_scalar(
                    out=tl4[:], in0=tl4[:], scalar1=-1.0, scalar2=2.0,
                    op0=ALU.mult, op1=ALU.add), "vP")
                em.wait("vector", "vP")
                em.op("vector", lambda v: v.tensor_tensor(
                    out=tl3[:], in0=tl3[:], in1=tl4[:], op=ALU.mult), "vP")
            em.wait("vector", "vP")
            em.op("vector", lambda v: v.tensor_copy(out_col, tl3[:]), "vP")

        h_vA, h_scA = [], []
        for i in range(NT_CAT):
            buf = xa[i % 2]
            if i >= 2:
                em.wait("sync", "vA", h_vA[i - 2])
                em.wait("sync", "scA", h_scA[i - 2])
            em.dma("sync", lambda sy, b=buf, i=i: sy.dma_start(
                out=b[:], in_=XC[i * P:(i + 1) * P, :]), "dma")
            em.wait("vector", "dma")
            em.op("vector", lambda v, b=buf, i=i: v.tensor_reduce(
                out=hmax[:, i:i + 1], in_=b[:], axis=AX.X, op=ALU.max), "vA")
            em.op("vector", lambda v, b=buf, i=i: v.tensor_reduce(
                out=hmin[:, i:i + 1], in_=b[:], axis=AX.X, op=ALU.min), "vA")
            h_vA.append(em.sem("vA").n)
            em.wait("scalar", "dma")
            em.wait("scalar", "scA")
            em.op("scalar", lambda sc, b=buf, i=i: sc.activation(
                scr[:], b[:], AF.Identity, bias=0.0, scale=1.0,
                accum_out=nmean[:, i:i + 1]), "scA")
            em.op("scalar", lambda sc, b=buf, i=i: sc.activation(
                scr2[:], b[:], AF.Square, bias=0.0, scale=1.0,
                accum_out=vssum[:, i:i + 1]), "scA")
            h_scA.append(em.sem("scA").n)
        NC_ = NT_CAT
        em.wait("vector", "vA")
        em.wait("vector", "scA")
        em.op("vector", lambda v: v.tensor_scalar_mul(
            nmean[:, 0:NC_], nmean[:, 0:NC_], -1.0 / D), "vA")
        em.wait("vector", "vA")
        em.op("vector", lambda v: v.tensor_tensor(
            out=s2c[:, 0:NC_], in0=nmean[:, 0:NC_], in1=nmean[:, 0:NC_],
            op=ALU.mult), "vA")
        em.wait("vector", "vA")
        em.op("vector", lambda v: v.scalar_tensor_tensor(
            out=s1c[:, 0:NC_], in0=vssum[:, 0:NC_], scalar=1.0 / D,
            in1=s2c[:, 0:NC_], op0=ALU.mult, op1=ALU.subtract), "vA")
        em.wait("vector", "vA")
        em.op("vector", lambda v: v.tensor_scalar_add(
            s1c[:, 0:NC_], s1c[:, 0:NC_], EPS), "vA")
        em.wait("scalar", "vA")
        em.wait("scalar", "scA")
        em.op("scalar", lambda sc: sc.activation(
            scr[:, 0:NC_], s1c[:, 0:NC_], AF.Sqrt, bias=0.0, scale=1.0),
            "scA")
        em.wait("vector", "scA")
        em.op("vector", lambda v: v.reciprocal(rstd[:, 0:NC_],
                                               scr[:, 0:NC_]), "vA")
        for _ in range(2):
            em.wait("vector", "vA")
            em.op("vector", lambda v: v.tensor_tensor(
                out=s2c[:, 0:NC_], in0=rstd[:, 0:NC_], in1=rstd[:, 0:NC_],
                op=ALU.mult), "vA")
            em.wait("vector", "vA")
            em.op("vector", lambda v: v.tensor_tensor(
                out=s2c[:, 0:NC_], in0=s2c[:, 0:NC_], in1=s1c[:, 0:NC_],
                op=ALU.mult), "vA")
            em.wait("vector", "vA")
            em.op("vector", lambda v: v.tensor_scalar(
                out=s2c[:, 0:NC_], in0=s2c[:, 0:NC_], scalar1=-0.5,
                scalar2=1.5, op0=ALU.mult, op1=ALU.add), "vA")
            em.wait("vector", "vA")
            em.op("vector", lambda v: v.tensor_tensor(
                out=rstd[:, 0:NC_], in0=rstd[:, 0:NC_], in1=s2c[:, 0:NC_],
                op=ALU.mult), "vA")
        em.wait("vector", "vA")
        em.op("vector", lambda v: v.tensor_tensor(
            out=s2c[:, 0:NC_], in0=nmean[:, 0:NC_], in1=nmean[:, 0:NC_],
            op=ALU.mult), "vA")
        em.wait("vector", "vA")
        em.op("vector", lambda v: v.scalar_tensor_tensor(
            out=s2c[:, 0:NC_], in0=s2c[:, 0:NC_], scalar=-float(D),
            in1=vssum[:, 0:NC_], op0=ALU.mult, op1=ALU.add), "vA")
        em.wait("vector", "vA")
        em.op("vector", lambda v: v.tensor_tensor(
            out=scr[:, 0:NC_], in0=rstd[:, 0:NC_], in1=rstd[:, 0:NC_],
            op=ALU.mult), "vA")
        em.wait("vector", "vA")
        em.op("vector", lambda v: v.tensor_tensor(
            out=s2c[:, 0:NC_], in0=s2c[:, 0:NC_], in1=scr[:, 0:NC_],
            op=ALU.mult), "vA")
        em.wait("vector", "vA")
        em.op("vector", lambda v: v.tensor_reduce(
            out=tl2[:], in_=s2c[:, 0:NT_STAT], axis=AX.X, op=ALU.add), "vA")
        em.wait("vector", "vA")
        em.op("vector", lambda v: v.tensor_copy(scr[:, 1:2], tl2[:]), "vA")
        em.wait("tensor", "vA")
        em.wait("tensor", "dma")
        em.op("tensor", lambda t: t.transpose(
            psum[1][0:1, 0:P], scr[:, 1:2], ident32[:]), "tT")
        em.wait("vector", "tT")
        em.op("vector", lambda v: v.tensor_reduce(
            out=glob[0:1, 7:8], in_=psum[1][0:1, 0:P], axis=AX.X,
            op=ALU.add), "vA")
        em.wait("vector", "vA")
        em.op("vector", lambda v: v.tensor_scalar_mul(
            glob[0:1, 7:8], glob[0:1, 7:8], 1.0 / (S * D)), "vA")
        bcast_col(glob[0:1, 7:8], glob[:, 5:6], psum[2])
        em.wait("vector", "vP")
        em.op("vector", lambda v: v.tensor_scalar_add(
            tl1[:], glob[:, 5:6], EPS), "vA")
        em.wait("scalar", "vA")
        em.op("scalar", lambda sc: sc.activation(
            tl2[:], tl1[:], AF.Sqrt, bias=0.0, scale=1.0), "scA")
        em.wait("vector", "scA")
        em.op("vector", lambda v: v.reciprocal(tl3[:], tl2[:]), "vA")
        for _ in range(2):
            em.wait("vector", "vA")
            em.op("vector", lambda v: v.tensor_tensor(
                out=tl4[:], in0=tl3[:], in1=tl3[:], op=ALU.mult), "vA")
            em.wait("vector", "vA")
            em.op("vector", lambda v: v.tensor_tensor(
                out=tl4[:], in0=tl4[:], in1=tl1[:], op=ALU.mult), "vA")
            em.wait("vector", "vA")
            em.op("vector", lambda v: v.tensor_scalar(
                out=tl4[:], in0=tl4[:], scalar1=-0.5, scalar2=1.5,
                op0=ALU.mult, op1=ALU.add), "vA")
            em.wait("vector", "vA")
            em.op("vector", lambda v: v.tensor_tensor(
                out=tl3[:], in0=tl3[:], in1=tl4[:], op=ALU.mult), "vA")
        em.wait("vector", "vA")
        em.op("vector", lambda v: v.tensor_copy(glob[:, 2:3], tl3[:]), "vA")
        em.wait("vector", "vA")
        em.op("vector", lambda v: v.scalar_tensor_tensor(
            out=hmax[:, 0:NC_], in0=hmax[:, 0:NC_], scalar=1.0,
            in1=nmean[:, 0:NC_], op0=ALU.mult, op1=ALU.add), "vA")
        em.op("vector", lambda v: v.scalar_tensor_tensor(
            out=hmin[:, 0:NC_], in0=hmin[:, 0:NC_], scalar=-1.0,
            in1=nmean[:, 0:NC_], op0=ALU.mult, op1=ALU.subtract), "vA")
        em.wait("vector", "vA")
        em.op("vector", lambda v: v.tensor_tensor(
            out=hmax[:, 0:NC_], in0=hmax[:, 0:NC_], in1=hmin[:, 0:NC_],
            op=ALU.max), "vA")
        em.wait("vector", "vA")
        em.op("vector", lambda v: v.tensor_tensor(
            out=hmax[:, 0:NC_], in0=hmax[:, 0:NC_], in1=rstd[:, 0:NC_],
            op=ALU.mult), "vA")
        em.wait("vector", "vA")
        em.op("vector", lambda v: v.tensor_scalar_mul(
            hmax[:, 0:NC_], hmax[:, 0:NC_], glob[:, 2:3]), "vA")
        em.wait("vector", "vA")
        em.op("vector", lambda v: v.tensor_scalar_max(
            hmax[:, 0:NC_], hmax[:, 0:NC_], 1e-5), "vA")
        em.wait("vector", "vA")
        em.op("vector", lambda v: v.tensor_scalar_mul(
            invsx[:, 0:NC_], hmax[:, 0:NC_], 1.0 / 127.0), "vA")
        em.op("vector", lambda v: v.reciprocal(sxc[:, 0:NC_],
                                               hmax[:, 0:NC_]), "vA")
        em.wait("vector", "vA")
        em.op("vector", lambda v: v.tensor_tensor(
            out=scr[:, 0:NC_], in0=hmax[:, 0:NC_], in1=sxc[:, 0:NC_],
            op=ALU.mult), "vA")
        em.wait("vector", "vA")
        em.op("vector", lambda v: v.tensor_scalar(
            out=scr[:, 0:NC_], in0=scr[:, 0:NC_], scalar1=-1.0, scalar2=2.0,
            op0=ALU.mult, op1=ALU.add), "vA")
        em.wait("vector", "vA")
        em.op("vector", lambda v: v.tensor_tensor(
            out=sxc[:, 0:NC_], in0=sxc[:, 0:NC_], in1=scr[:, 0:NC_],
            op=ALU.mult), "vA")
        em.wait("vector", "vA")
        em.op("vector", lambda v: v.tensor_scalar_mul(
            sxc[:, 0:NC_], sxc[:, 0:NC_], 127.0), "vA")
        em.wait("vector", "vA")
        em.op("vector", lambda v: v.tensor_scalar_mul(
            s1c[:, 0:NC_], sxc[:, 0:NC_], glob[:, 2:3]), "vA")
        em.wait("vector", "vA")
        em.op("vector", lambda v: v.tensor_tensor(
            out=s1c[:, 0:NC_], in0=s1c[:, 0:NC_], in1=rstd[:, 0:NC_],
            op=ALU.mult), "vA")

        em.wait("sync", "dma")
        em.wait("sync", "vA")
        em.wait("sync", "scA")
        em.wait("vector", "vA")
        em.wait("vector", "vP")
        h_vB = []
        T_front = dict(ki32=ki32, eb=eb, db=db, k1b=k1b)
        for i in range(NT_CAT):
            buf = xa[i % 2]
            if i >= 2:
                em.wait("sync", "vB", h_vB[i - 2])
            em.dma("sync", lambda sy, b=buf, i=i: sy.dma_start(
                out=b[:], in_=XC[i * P:(i + 1) * P, :]), "dmb")
            em.wait("vector", "dmb")
            em.op("vector", lambda v, b=buf, i=i: v.tensor_scalar(
                out=yb[:], in0=b[:], scalar1=nmean[:, i:i + 1],
                scalar2=s1c[:, i:i + 1], op0=ALU.add, op1=ALU.mult), "vB")
            h_vB.append(em.sem("vB").n)
            emit_round2("vB", yb, kf16, T_front,
                        extra_waits=("scB", "tT"))
            em.wait("scalar", "vB")
            em.wait("scalar", "scB")
            em.op("scalar", lambda sc, i=i: sc.activation(
                scr2[:], kf16[:], AF.Square, bias=0.0, scale=1.0,
                accum_out=ksq[:, i:i + 1]), "scB")
            em.wait("tensor", "vB")
            em.wait("tensor", "scB")
            if i == 0:
                em.wait("tensor", "vA")
                em.wait("tensor", "vP")
            pts = [psum[c % 4][:, (c // 4) * 64:(c // 4) * 64 + 64]
                   .bitcast(F16) for c in range(DC)]
            for c in range(DC):
                em.op("tensor", lambda t, c=c: t.transpose(
                    pts[c], kf16[:, c * P:(c + 1) * P], ident16[:]), "tT")
            em.wait("scalar", "tT")
            for c in range(DC):
                em.op("scalar", lambda sc, c=c, i=i: sc.activation(
                    kT[:, c * TCAT + i * P: c * TCAT + (i + 1) * P], pts[c],
                    AF.Identity, bias=0.0, scale=1.0), "scB")
        if DEBUG:
            em.wait("sync", "vB")
            em.wait("sync", "scB")
            em.dma("sync", lambda sy: sy.dma_start(
                out=dbg["KT"], in_=kT[:]), "dst")

        def ternarize_t(WT, sw_glob_col, tern, wbuf, yb, T):
            em.wait("vector", "tEnd")
            for c in range(DC):
                wb = wbuf[c % 2]
                em.wait("sync", "vP")
                em.dma("sync", lambda sy, wb=wb, c=c: sy.dma_start(
                    out=wb[:], in_=WT[c * P:(c + 1) * P, :]), "dmw")
                em.wait("vector", "dmw")
                em.op("vector", lambda v, wb=wb: v.tensor_reduce(
                    out=tl1[:], in_=wb[:], axis=AX.X, op=ALU.add,
                    apply_absolute_value=True), "vP")
                em.wait("vector", "vP")
                if c == 0:
                    em.op("vector", lambda v: v.tensor_copy(wacc[:], tl1[:]),
                          "vP")
                else:
                    em.op("vector", lambda v: v.tensor_tensor(
                        out=wacc[:], in0=wacc[:], in1=tl1[:], op=ALU.add),
                        "vP")
            cross_part_reduce(wacc[:], glob[0:1, 8:9], ALU.add, psum[6])
            em.wait("vector", "vP")
            em.op("vector", lambda v: v.tensor_scalar(
                out=glob[0:1, 8:9], in0=glob[0:1, 8:9],
                scalar1=1.0 / (D * D), scalar2=1e-5, op0=ALU.mult,
                op1=ALU.max), "vP")
            bcast_col(glob[0:1, 8:9], glob[:, sw_glob_col:sw_glob_col + 1],
                      psum[6])
            em.wait("vector", "vP")
            em.op("vector", lambda v: v.tensor_scalar_mul(
                glob[:, 13:14], glob[:, sw_glob_col:sw_glob_col + 1], 0.5),
                "vP")
            em.op("vector", lambda v: v.tensor_scalar_mul(
                glob[:, 14:15], glob[:, sw_glob_col:sw_glob_col + 1], -0.5),
                "vP")
            em.wait("vector", "vP")
            for c in range(DC):
                wb = wbuf[c % 2]
                em.wait("sync", "vP")
                em.dma("sync", lambda sy, wb=wb, c=c: sy.dma_start(
                    out=wb[:], in_=WT[c * P:(c + 1) * P, :]), "dmw")
                em.wait("vector", "dmw")
                em.op("vector", lambda v, wb=wb: v.tensor_scalar(
                    out=T["cb"][:], in0=wb[:], scalar1=glob[:, 13:14],
                    scalar2=1.0, op0=ALU.is_gt, op1=ALU.mult), "vP")
                em.op("vector", lambda v, wb=wb: v.tensor_scalar(
                    out=T["db"][:], in0=wb[:], scalar1=glob[:, 14:15],
                    scalar2=1.0, op0=ALU.is_lt, op1=ALU.mult), "vP")
                em.wait("vector", "vP")
                em.op("vector", lambda v, c=c, tern=tern:
                      v.scalar_tensor_tensor(
                          out=tern[:, c * D:(c + 1) * D], in0=T["cb"][:],
                          scalar=1.0, in1=T["db"][:], op0=ALU.mult,
                          op1=ALU.subtract), "vP")

        if BUILD_PHASES >= 2:
            NKT = S // 512
            QT_W = [512, 512, 512, 512, 128]
            QT_OFF = [0, 512, 1024, 1536, 4096]

            T_fp = dict(ki32=ki32, eb=eb, cb=cb, db=db, k1b=k1b)
            ternarize_t(WVT, 6, tern, wbuf, yb, T_fp)
            em.wait("vector", "vP")
            for i in range(NT_STAT):
                em.op("vector", lambda v, i=i: v.tensor_tensor(
                    out=vscl[:, i:i + 1], in0=invsx[:, i:i + 1],
                    in1=glob[:, 6:7], op=ALU.mult), "vP")
            for b in range(2):
                for kc in range(4):
                    em.op("vector", lambda v, b=b, kc=kc: v.memset(
                        vstg[b][:, kc * 130 + 64:kc * 130 + 65], 1.0), "vP")
                    em.op("vector", lambda v, b=b, kc=kc: v.memset(
                        vstg[b][:, kc * 130 + 129:kc * 130 + 130], 1.0), "vP")
            em.wait("tensor", "vB")
            em.wait("tensor", "scB")
            h_vV_w = {0: None, 1: None}
            for oc in range(DC):
                em.wait("tensor", "vP")
                tv_mm = {}
                for w in range(2):
                    if h_vV_w[w] is not None:
                        em.wait("tensor", "vV", h_vV_w[w])
                    for c in range(DC):
                        for tt in range(w * 4, w * 4 + 4):
                            em.op("tensor", lambda t, oc=oc, c=c, tt=tt:
                                  t.matmul(
                                      psum[tt][:],
                                      tern[:, c * D + oc * P:
                                           c * D + oc * P + P],
                                      kT[:, c * TCAT + tt * 512:
                                         c * TCAT + tt * 512 + 512],
                                      start=(c == 0), stop=(c == DC - 1)),
                                  "tV" if c == DC - 1 else None)
                    tv_mm[w] = em.sem("tV").n
                for w in range(2):
                    tv_need = tv_mm[w]
                    for tt in range(w * 4, w * 4 + 4):
                        vs = vstg[tt % 2]
                        em.wait("vector", "tV", tv_need)
                        em.op("vector", lambda v, tt=tt: v.tensor_copy(
                            pevac[:], psum[tt][:]), "vV")
                        em.wait("tensor", "vV")
                        for kc in range(4):
                            em.op("tensor", lambda t, tt=tt, kc=kc:
                                  t.transpose(
                                      psum[tt][:, kc * P:(kc + 1) * P],
                                      pevac[:, kc * P:(kc + 1) * P],
                                      ident32[:]), "tV")
                        tv_need = em.sem("tV").n
                        em.wait("vector", "tV", tv_need)
                        em.wait("vector", "dsp")
                        for kc in range(4):
                            col = tt * 4 + kc
                            em.wait("vector", "vV")
                            em.op("vector", lambda v, vs=vs, tt=tt, kc=kc,
                                  col=col: v.tensor_scalar_mul(
                                      vs[:, kc * 130 + 0:kc * 130 + 64],
                                      psum[tt][:, kc * P:kc * P + 64],
                                      vscl[:, col:col + 1]), "vV")
                            em.op("vector", lambda v, vs=vs, tt=tt, kc=kc,
                                  col=col: v.tensor_scalar_mul(
                                      vs[:, kc * 130 + 65:kc * 130 + 129],
                                      psum[tt][:, kc * P + 64:(kc + 1) * P],
                                      vscl[:, col:col + 1]), "vV")
                        em.wait("gpsimd", "vV")
                        em.dma("gpsimd", lambda g, vs=vs, oc=oc, tt=tt:
                               g.dma_start(
                                   out=VSP[oc, :, tt * 4 * 130:
                                           (tt + 1) * 4 * 130],
                                   in_=vs[:]), "dsp")
                    h_vV_w[w] = em.sem("vV").n

            em.op("tensor", lambda t: t.nop(), "tEnd")
            ternarize_t(WKT, 5, tern, wbuf, yb, T_fp)
            h_vK_w = {0: None, 1: None}
            for oc in range(DC):
                em.wait("tensor", "vP")
                tk_mm = {}
                for w in range(2):
                    if h_vK_w[w] is not None:
                        em.wait("tensor", "vK", h_vK_w[w])
                    for c in range(DC):
                        for tt in range(w * 4, w * 4 + 4):
                            em.op("tensor", lambda t, oc=oc, c=c, tt=tt:
                                  t.matmul(
                                      psum[tt][:],
                                      tern[:, c * D + oc * P:
                                           c * D + oc * P + P],
                                      kT[:, c * TCAT + tt * 512:
                                         c * TCAT + tt * 512 + 512],
                                      start=(c == 0), stop=(c == DC - 1)),
                                  "tK" if c == DC - 1 else None)
                    tk_mm[w] = em.sem("tK").n
                for w in range(2):
                    for tt in range(w * 4, w * 4 + 4):
                        em.wait("vector", "tK", tk_mm[w])
                        em.wait("vector", "dsk")
                        em.op("vector", lambda v, tt=tt: v.tensor_scalar_mul(
                            pv16a[:], psum[tt][:], 0.25), "vK")
                        em.wait("vector", "vK")
                        em.op("vector", lambda v, tt=tt:
                              v.scalar_tensor_tensor(
                                  out=pv16b[:], in0=psum[tt][:], scalar=0.25,
                                  in1=pv16a[:], op0=ALU.mult,
                                  op1=ALU.subtract), "vK")
                        em.wait("gpsimd", "vK")
                        em.dma("gpsimd", lambda g, oc=oc, tt=tt: g.dma_start(
                            out=KH[oc * P:(oc + 1) * P,
                                   tt * 512:(tt + 1) * 512],
                            in_=pv16a[:]), "dsk")
                        em.dma("gpsimd", lambda g, oc=oc, tt=tt: g.dma_start(
                            out=KL[oc * P:(oc + 1) * P,
                                   tt * 512:(tt + 1) * 512],
                            in_=pv16b[:]), "dsk")
                    h_vK_w[w] = em.sem("vK").n

            em.op("tensor", lambda t: t.nop(), "tEnd")
            ternarize_t(WQT, 4, tern, wbuf, yb, T_fp)
            em.wait("vector", "vP")
            for i in list(range(16)) + [32]:
                em.op("vector", lambda v, i=i: v.tensor_scalar_mul(
                    qscl[:, i:i + 1], invsx[:, i:i + 1], 0.25), "vP")
            em.op("vector", lambda v: v.memset(qnsq[:], 0.0), "vQn")
            h_vQ_w = {0: None, 1: None}
            QW_TTS = {0: (0, 1, 2), 1: (3, 4)}
            for oc in range(DC):
                em.wait("tensor", "vP")
                tq_mm = {}
                for w in range(2):
                    if h_vQ_w[w] is not None:
                        em.wait("tensor", "vQ", h_vQ_w[w])
                    for c in range(DC):
                        for tt in QW_TTS[w]:
                            em.op("tensor", lambda t, oc=oc, c=c, tt=tt:
                                  t.matmul(
                                      psum[tt][:, 0:QT_W[tt]],
                                      tern[:, c * D + oc * P:
                                           c * D + oc * P + P],
                                      kT[:, c * TCAT + QT_OFF[tt]:
                                         c * TCAT + QT_OFF[tt] + QT_W[tt]],
                                      start=(c == 0), stop=(c == DC - 1)),
                                  "tQ" if c == DC - 1 else None)
                    tq_mm[w] = em.sem("tQ").n
                for w in range(2):
                    tq_need = tq_mm[w]
                    for tt in QW_TTS[w]:
                        W = QT_W[tt]
                        nkk = W // P
                        em.wait("vector", "tQ", tq_need)
                        em.op("vector", lambda v, tt=tt, W=W: v.tensor_copy(
                            qstg[:, 0:W], psum[tt][:, 0:W]), "vQ")
                        em.wait("tensor", "vQ")
                        for kk in range(nkk):
                            em.op("tensor", lambda t, tt=tt, kk=kk:
                                  t.transpose(
                                      psum[tt][:, kk * P:(kk + 1) * P],
                                      qstg[:, kk * P:(kk + 1) * P],
                                      ident32[:]), "tQ")
                        tq_need = em.sem("tQ").n
                        em.wait("vector", "tQ", tq_need)
                        for kk in range(nkk):
                            col = (tt * 4 + kk) if tt < 4 else 32
                            qtile = (tt * 4 + kk) if tt < 4 else 16
                            em.op("vector", lambda v, tt=tt, kk=kk, col=col:
                                  v.tensor_scalar_mul(
                                      qstg2[:, kk * P:(kk + 1) * P],
                                      psum[tt][:, kk * P:(kk + 1) * P],
                                      qscl[:, col:col + 1]), "vQ")
                            em.wait("scalar", "vQ")
                            em.op("scalar", lambda sc, kk=kk: sc.activation(
                                scr2[:, 0:P], qstg2[:, kk * P:(kk + 1) * P],
                                AF.Square, bias=0.0, scale=1.0,
                                accum_out=tl2[:]), "scQ")
                            em.wait("vector", "scQ")
                            em.wait("vector", "vQn")
                            em.op("vector", lambda v, qtile=qtile:
                                  v.tensor_tensor(
                                      out=qnsq[:, qtile:qtile + 1],
                                      in0=qnsq[:, qtile:qtile + 1],
                                      in1=tl2[:], op=ALU.add), "vQn")
                        em.wait("tensor", "vQ")
                        for kk in range(nkk):
                            em.op("tensor", lambda t, tt=tt, kk=kk:
                                  t.transpose(
                                      psum[tt][:, kk * P:(kk + 1) * P],
                                      qstg2[:, kk * P:(kk + 1) * P],
                                      ident32[:]), "tQ")
                        tq_need = em.sem("tQ").n
                        em.wait("vector", "tQ", tq_need)
                        em.wait("vector", "dsq")
                        em.wait("vector", "dsk")
                        em.op("vector", lambda v, tt=tt, W=W: v.tensor_copy(
                            pv16a[:, 0:W], psum[tt][:, 0:W]), "vQ")
                        em.wait("vector", "vQ")
                        em.op("vector", lambda v, tt=tt, W=W:
                              v.scalar_tensor_tensor(
                                  out=pv16b[:, 0:W], in0=psum[tt][:, 0:W],
                                  scalar=1.0, in1=pv16a[:, 0:W],
                                  op0=ALU.mult, op1=ALU.subtract), "vQ")
                        tok0 = tt * 512 if tt < 4 else 2048
                        em.wait("gpsimd", "vQ")
                        em.dma("gpsimd", lambda g, oc=oc, tok0=tok0, W=W:
                               g.dma_start(
                                   out=QH[oc * P:(oc + 1) * P,
                                          tok0:tok0 + W],
                                   in_=pv16a[:, 0:W]), "dsq")
                        em.dma("gpsimd", lambda g, oc=oc, tok0=tok0, W=W:
                               g.dma_start(
                                   out=QL[oc * P:(oc + 1) * P,
                                          tok0:tok0 + W],
                                   in_=pv16b[:, 0:W]), "dsq")
                    h_vQ_w[w] = em.sem("vQ").n

            em.op("tensor", lambda t: t.nop(), "tEnd")
            em.wait("vector", "vP")
            em.op("vector", lambda v: v.tensor_tensor(
                out=tl1[:], in0=glob[:, 4:5], in1=glob[:, 5:6], op=ALU.mult),
                "vP")
            em.wait("vector", "vP")
            em.op("vector", lambda v: v.tensor_scalar_mul(
                tl1[:], tl1[:], 2.0), "vP")
            em.wait("vector", "vP")
            for i in range(NT_STAT):
                em.op("vector", lambda v, i=i: v.tensor_tensor(
                    out=kescl[:, i:i + 1], in0=invsx[:, i:i + 1], in1=tl1[:],
                    op=ALU.mult), "vP")
            em.wait("scalar", "vB")
            em.wait("scalar", "scB")
            em.wait("scalar", "scQ")
            em.op("scalar", lambda sc: sc.activation(
                scr2[:, 0:NT_STAT], ksq[:, 0:NT_STAT], AF.Sqrt, bias=0.0,
                scale=1.0), "scP")
            em.wait("vector", "scP")
            em.op("vector", lambda v: v.memset(eb[:, 0:NT_STAT], 0.0), "vP")
            em.wait("vector", "vP")
            for i in range(NT_STAT):
                em.op("vector", lambda v, i=i: v.tensor_tensor(
                    out=eb[:, i:i + 1], in0=scr2[:, i:i + 1],
                    in1=kescl[:, i:i + 1], op=ALU.mult), "vP")
            em.wait("vector", "vP")
            em.op("vector", lambda v: v.tensor_scalar_mul(
                eb[:, 0:NT_STAT], eb[:, 0:NT_STAT], 0.25), "vP")
            em.wait("vector", "vP")
            em.op("vector", lambda v: v.tensor_reduce(
                out=tl1[:], in_=eb[:, 0:NT_STAT], axis=AX.X, op=ALU.max),
                "vP")
            cross_part_reduce(tl1[:], glob[0:1, 9:10], ALU.max, psum[6])
            em.wait("scalar", "vQn")
            em.wait("scalar", "scP")
            em.wait("scalar", "vP")
            em.op("scalar", lambda sc: sc.activation(
                scr2[:, 0:17], qnsq[:, 0:17], AF.Sqrt, bias=0.0,
                scale=1.0), "scP")
            em.wait("vector", "scP")
            em.op("vector", lambda v: v.tensor_reduce(
                out=tl1[:], in_=scr2[:, 0:17], axis=AX.X, op=ALU.max), "vP")
            cross_part_reduce(tl1[:], glob[0:1, 10:11], ALU.max, psum[6])
            em.wait("vector", "vP")
            em.op("vector", lambda v: v.tensor_tensor(
                out=glob[0:1, 11:12], in0=glob[0:1, 9:10],
                in1=glob[0:1, 10:11], op=ALU.mult), "vP")
            em.wait("vector", "vP")
            em.op("vector", lambda v: v.tensor_scalar_mul(
                glob[0:1, 11:12], glob[0:1, 11:12], -1.01), "vP")
            bcast_col(glob[0:1, 11:12], glob[:, 3:4], psum[6])

            if DEBUG:
                em.wait("sync", "vP")
                em.dma("sync", lambda sy: sy.dma_start(
                    out=dbg["ST"], in_=glob[:]), "dst")

        if BUILD_PHASES >= 3:
            em.phase_barrier()
            fp_es.close()
            mid_es = ExitStack()
            me = mid_es.enter_context
            x1x = me(nc.sbuf_tensor("x1x", [P, D], F32))
            oqscl = me(nc.sbuf_tensor("oqscl", [P, 17], F32))
            lorT = me(nc.sbuf_tensor("lorT", [P, DC * 16], F32))
            lorTh = me(nc.sbuf_tensor("lorTh", [P, DC * 16], F16))
            lorTl = me(nc.sbuf_tensor("lorTl", [P, DC * 16], F16))
            ao_es = ExitStack()
            attnout = ao_es.enter_context(
                nc.sbuf_tensor("attnout", [P, 17 * D], F32))
            at_es = ExitStack()
            ae = at_es.enter_context
            qbh = ae(nc.sbuf_tensor("qbh", [P, DC * 512], F16))
            qbl = ae(nc.sbuf_tensor("qbl", [P, DC * 512], F16))
            kbh = [ae(nc.sbuf_tensor(f"kbh{i}", [P, S], F16))
                   for i in range(2)]
            kbl = [ae(nc.sbuf_tensor(f"kbl{i}", [P, S], F16))
                   for i in range(2)]
            vbuf = [ae(nc.sbuf_tensor(f"vbuf{i}", [P, 32 * 130], F32))
                    for i in range(2)]
            ea32 = [ae(nc.sbuf_tensor(f"ea32{i}", [P, 512], F32))
                    for i in range(2)]
            eb32 = [ae(nc.sbuf_tensor(f"eb32{i}", [P, 512], F32))
                    for i in range(2)]
            ovst = [ae(nc.sbuf_tensor(f"ovst{i}", [P, 512], F32))
                    for i in range(2)]

            QT5_W = [512, 512, 512, 512, 128]
            QT5_T0 = [0, 4, 8, 12, 16]
            h_hp_end = []

            def emit_scores(kc, hp, W, kb, lb):
                pa = psum[2 + (kc % 2)]
                pb = psum[4 + (kc % 2)]
                kcs = slice(kc * P, (kc + 1) * P)
                for (st_, mv, first, last) in (
                        (kb, qbh, True, False), (kb, qbl, False, False),
                        (lb, qbh, False, True)):
                    em.op("tensor", lambda t, st_=st_, mv=mv, kcs=kcs, hp=hp,
                          W=W, pa=pa, first=first, last=last: t.matmul(
                              pa[0:P, 0:W], st_[0:64, kcs],
                              mv[0:64, hp * 512:hp * 512 + W],
                              start=first, stop=last),
                          "tS" if last else None)
                    em.op("tensor", lambda t, st_=st_, mv=mv, kcs=kcs, hp=hp,
                          W=W, pb=pb, first=first, last=last: t.matmul(
                              pb[0:P, 0:W], st_[64:P, kcs],
                              mv[64:P, hp * 512:hp * 512 + W],
                              start=first, stop=last,
                              tile_position=(64, 0)),
                          "tS" if last else None)

            def emit_av(kc, W, vb, bA, bB):
                first = (kc == 0)
                last = (kc == NT_STAT - 1)
                c0 = kc * 130
                ea, eb_ = ea32[kc % 2], eb32[kc % 2]
                em.op("tensor", lambda t, c0=c0, W=W, bA=bA, ea=ea,
                      first=first, last=last: t.matmul(
                          bA[0:65, 0:W], vb[:, c0:c0 + 65], ea[:, 0:W],
                          start=first, stop=last), "tAv")
                em.op("tensor", lambda t, c0=c0, W=W, bB=bB, eb_=eb_,
                      first=first, last=last: t.matmul(
                          bB[0:65, 0:W], vb[:, c0 + 65:c0 + 130],
                          eb_[:, 0:W], start=first, stop=last), "tAv")

            for qt in range(5):
                W = QT5_W[qt]
                off = qt * 512 if qt < 4 else 2048
                if h_hp_end:
                    em.wait("sync", "tAv", h_hp_end[-1])
                for c in range(DC):
                    em.dma("sync", lambda sy, c=c, off=off, W=W: sy.dma_start(
                        out=qbh[:, c * 512:c * 512 + W],
                        in_=QH[c * P:(c + 1) * P, off:off + W]), "dqb")
                    em.dma("sync", lambda sy, c=c, off=off, W=W: sy.dma_start(
                        out=qbl[:, c * 512:c * 512 + W],
                        in_=QL[c * P:(c + 1) * P, off:off + W]), "dqb")
                for hp in range(DC):
                    kb, lb, vb = kbh[hp % 2], kbl[hp % 2], vbuf[hp % 2]
                    bA = psum[0] if hp % 2 == 0 else psum[6]
                    bB = psum[1] if hp % 2 == 0 else psum[7]
                    if len(h_hp_end) >= 2:
                        em.wait("sync", "tAv", h_hp_end[-2])
                    em.dma("sync", lambda sy, kb=kb, hp=hp: sy.dma_start(
                        out=kb[:], in_=KH[hp * P:(hp + 1) * P, :]), "dkb")
                    em.dma("sync", lambda sy, lb=lb, hp=hp: sy.dma_start(
                        out=lb[:], in_=KL[hp * P:(hp + 1) * P, :]), "dkb")
                    em.dma("sync", lambda sy, vb=vb, hp=hp: sy.dma_start(
                        out=vb[:], in_=VSP[hp, :, :]), "dkb")
                    em.wait("tensor", "dkb")
                    em.wait("tensor", "dqb")
                    em.wait("tensor", "vO")
                    em.wait("tensor", "tOv")
                    em.wait("tensor", "scE")
                    h_scE, h_tAv = [], []
                    for kc in range(NT_STAT):
                        if kc >= 2:
                            em.wait("tensor", "scE", h_scE[kc - 2])
                        emit_scores(kc, hp, W, kb, lb)
                        tS_after = em.sem("tS").n
                        if kc >= 1:
                            em.wait("tensor", "scE", h_scE[kc - 1])
                            emit_av(kc - 1, W, vb, bA, bB)
                            h_tAv.append(em.sem("tAv").n)
                        pa = psum[2 + (kc % 2)]
                        pb = psum[4 + (kc % 2)]
                        ea, eb_ = ea32[kc % 2], eb32[kc % 2]
                        em.wait("scalar", "tS", tS_after)
                        if kc == 0:
                            em.wait("scalar", "tAv")
                        elif kc >= 2:
                            em.wait("scalar", "tAv", h_tAv[kc - 2])
                        em.op("scalar", lambda sc, pa=pa, ea=ea, kc=kc, W=W:
                              sc.activation(
                                  ea[:, 0:W], pa[:, 0:W], AF.Exp,
                                  bias=glob[:, 3:4],
                                  scale=kescl[:, kc:kc + 1]), "scE")
                        em.op("scalar", lambda sc, pb=pb, eb_=eb_, kc=kc, W=W:
                              sc.activation(
                                  eb_[:, 0:W], pb[:, 0:W], AF.Exp,
                                  bias=glob[:, 3:4],
                                  scale=kescl[:, kc:kc + 1]), "scE")
                        h_scE.append(em.sem("scE").n)
                    em.wait("tensor", "scE", h_scE[NT_STAT - 1])
                    emit_av(NT_STAT - 1, W, vb, bA, bB)
                    h_hp_end.append(em.sem("tAv").n)
                    nkk = W // P
                    em.wait("vector", "tAv")
                    em.wait("vector", "tOv")
                    for (hx, bk) in ((0, bA), (1, bB)):
                        ov = ovst[hx]
                        em.op("vector", lambda v, ov=ov, bk=bk, W=W:
                              v.tensor_copy(ov[0:65, 0:W],
                                            bk[0:65, 0:W]), "vO")
                    em.wait("tensor", "vO")
                    for (hx, bk) in ((0, bA), (1, bB)):
                        ov = ovst[hx]
                        for kk in range(nkk):
                            em.op("tensor", lambda t, ov=ov, bk=bk, kk=kk:
                                  t.transpose(bk[:, kk * P:kk * P + 65],
                                              ov[0:65, kk * P:(kk + 1) * P],
                                              ident32[0:65, 0:65]), "tOv")
                    nL = 2 * nkk
                    em.wait("vector", "tOv")
                    for (hx, bk) in ((0, bA), (1, bB)):
                        for kk in range(nkk):
                            idx = hx * nkk + kk
                            em.op("vector", lambda v, bk=bk, kk=kk, idx=idx:
                                  v.tensor_copy(scr[:, idx:idx + 1],
                                                bk[:, kk * P + 64:
                                                    kk * P + 65]), "vO")
                    em.wait("vector", "vO")
                    em.op("vector", lambda v, nL=nL: v.reciprocal(
                        scr2[:, 0:nL], scr[:, 0:nL]), "vO")
                    em.wait("vector", "vO")
                    em.op("vector", lambda v, nL=nL: v.tensor_tensor(
                        out=scr[:, 16:16 + nL], in0=scr[:, 0:nL],
                        in1=scr2[:, 0:nL], op=ALU.mult), "vO")
                    em.wait("vector", "vO")
                    em.op("vector", lambda v, nL=nL: v.tensor_scalar(
                        out=scr[:, 16:16 + nL], in0=scr[:, 16:16 + nL],
                        scalar1=-1.0, scalar2=2.0, op0=ALU.mult,
                        op1=ALU.add), "vO")
                    em.wait("vector", "vO")
                    em.op("vector", lambda v, nL=nL: v.tensor_tensor(
                        out=scr2[:, 0:nL], in0=scr2[:, 0:nL],
                        in1=scr[:, 16:16 + nL], op=ALU.mult), "vO")
                    em.wait("vector", "vO")
                    for (hx, bk) in ((0, bA), (1, bB)):
                        for kk in range(nkk):
                            idx = hx * nkk + kk
                            tile = QT5_T0[qt] + kk
                            hcol = tile * D + (hp * 2 + hx) * HD
                            em.op("vector", lambda v, bk=bk, kk=kk, idx=idx,
                                  hcol=hcol: v.tensor_scalar_mul(
                                      attnout[:, hcol:hcol + HD],
                                      bk[:, kk * P:kk * P + HD],
                                      scr2[:, idx:idx + 1]), "vO")

            if DEBUG:
                em.wait("sync", "vO")
                for tile in range(17):
                    em.dma("sync", lambda sy, tile=tile: sy.dma_start(
                        out=dbg["AO"][tile * P:(tile + 1) * P, :],
                        in_=attnout[:, tile * D:(tile + 1) * D]), "dst")

        open_scopes = []
        if BUILD_PHASES >= 4:
            em.phase_barrier()
            at_es.close()
            p4 = ExitStack()
            open_scopes += [ao_es, p4]
            pe4 = p4.enter_context
            koT = pe4(nc.sbuf_tensor("koT", [P, DC * TQ], F16))
            ternO = pe4(nc.sbuf_tensor("ternO", [P, DC * D], F16))
            wbuf4 = [pe4(nc.sbuf_tensor(f"wbuf4{i}", [P, D], F32))
                     for i in range(2)]
            ki32b = pe4(nc.sbuf_tensor("ki32b", [P, D], I32))
            ebb = pe4(nc.sbuf_tensor("ebb", [P, D], F32))
            cbb = pe4(nc.sbuf_tensor("cbb", [P, D], F32))
            dbb = pe4(nc.sbuf_tensor("dbb", [P, D], F32))
            k1bb = pe4(nc.sbuf_tensor("k1bb", [P, D], F32))
            ybb = pe4(nc.sbuf_tensor("ybb", [P, D], F32))
            kfOb = pe4(nc.sbuf_tensor("kfOb", [P, D], F16))
            ostgT = pe4(nc.sbuf_tensor("ostgT", [P, DC * 512], F32))
            o4b = attnout
            oqmx = pe4(nc.sbuf_tensor("oqmx", [P, 17], F32))
            oqmn = pe4(nc.sbuf_tensor("oqmn", [P, 17], F32))
            osx = pe4(nc.sbuf_tensor("osx", [P, 17], F32))
            xst4 = pe4(nc.sbuf_tensor("xst4", [P, 8], F32))
            xss4 = pe4(nc.sbuf_tensor("xss4", [P, 8], F32))
            ccis = pe4(nc.sbuf_tensor("ccis", [1, 8], F32))
            ccos = pe4(nc.sbuf_tensor("ccos", [1, 8], F32))
            T4 = dict(ki32=ki32b, eb=ebb, cb=cbb, db=dbb, k1b=k1bb)

            em.op("vector", lambda v: v.memset(tl1[:], 0.0), "v4")
            em.op("vector", lambda v: v.memset(tl2[:], 0.0), "v4")
            for i in range(NT_OWN):
                em.wait("vector", "v4")
                em.op("vector", lambda v, i=i: v.tensor_reduce(
                    out=tl3[:], in_=attnout[:, i * D:(i + 1) * D], axis=AX.X,
                    op=ALU.add), "v4")
                em.wait("scalar", "sc4")
                em.wait("scalar", "v4")
                em.op("scalar", lambda sc, i=i: sc.activation(
                    scr2[:], attnout[:, i * D:(i + 1) * D], AF.Square,
                    bias=0.0, scale=1.0, accum_out=tl4[:]), "sc4")
                em.wait("vector", "v4")
                em.wait("vector", "sc4")
                em.op("vector", lambda v: v.tensor_tensor(
                    out=tl1[:], in0=tl1[:], in1=tl3[:], op=ALU.add), "v4")
                em.op("vector", lambda v: v.tensor_tensor(
                    out=tl2[:], in0=tl2[:], in1=tl4[:], op=ALU.add), "v4")
            cross_part_reduce(tl1[:], ccis[0:1, 0:1], ALU.add, psum[6])
            cross_part_reduce(tl2[:], ccis[0:1, 1:2], ALU.add, psum[6])
            em.wait("vector", "vP")
            em.op("vector", lambda v: v.memset(ccis[0:1, 2:8], 0.0), "v4")
            em.wait("gpsimd", "v4")
            em.wait("gpsimd", "vP")
            em.op("gpsimd", lambda g: g.dma_start(
                out=CCI.ap(), in_=ccis[:]), "dcc", 16)
            em.wait("gpsimd", "dcc")
            em.op("gpsimd", lambda g: g.collective_compute(
                "AllReduce", ALU.add,
                replica_groups=[[0, 1], [2, 3], [4, 5], [6, 7]],
                ins=[CCI.ap().opt()], outs=[CCO.ap().opt()]), "gcc", 1)
            em.wait("gpsimd", "gcc")
            em.op("gpsimd", lambda g: g.dma_start(
                out=ccos[:], in_=CCO.ap()), "dcc", 16)
            em.wait("vector", "dcc")
            em.op("vector", lambda v: v.tensor_scalar_mul(
                ccos[0:1, 2:4], ccos[0:1, 0:2], 1.0 / (S * D)), "v4")
            em.wait("vector", "v4")
            bcast_col(ccos[0:1, 2:3], glob[:, 9:10], psum[6])
            bcast_col(ccos[0:1, 3:4], glob[:, 10:11], psum[6])
            em.wait("vector", "vP")
            em.op("vector", lambda v: v.tensor_tensor(
                out=tl1[:], in0=glob[:, 9:10], in1=glob[:, 9:10],
                op=ALU.mult), "v4")
            em.wait("vector", "v4")
            em.op("vector", lambda v: v.tensor_tensor(
                out=tl2[:], in0=glob[:, 10:11], in1=tl1[:], op=ALU.subtract),
                "v4")
            em.wait("vector", "v4")
            em.op("vector", lambda v: v.tensor_scalar_add(
                tl1[:], tl2[:], EPS), "v4")
            em.wait("scalar", "v4")
            em.op("scalar", lambda sc: sc.activation(
                tl2[:], tl1[:], AF.Sqrt, bias=0.0, scale=1.0), "sc4")
            em.wait("vector", "sc4")
            em.op("vector", lambda v: v.reciprocal(tl3[:], tl2[:]), "v4")
            for _ in range(2):
                em.wait("vector", "v4")
                em.op("vector", lambda v: v.tensor_tensor(
                    out=tl4[:], in0=tl3[:], in1=tl3[:], op=ALU.mult), "v4")
                em.wait("vector", "v4")
                em.op("vector", lambda v: v.tensor_tensor(
                    out=tl4[:], in0=tl4[:], in1=tl1[:], op=ALU.mult), "v4")
                em.wait("vector", "v4")
                em.op("vector", lambda v: v.tensor_scalar(
                    out=tl4[:], in0=tl4[:], scalar1=-0.5, scalar2=1.5,
                    op0=ALU.mult, op1=ALU.add), "v4")
                em.wait("vector", "v4")
                em.op("vector", lambda v: v.tensor_tensor(
                    out=tl3[:], in0=tl3[:], in1=tl4[:], op=ALU.mult), "v4")
            em.wait("vector", "v4")
            em.op("vector", lambda v: v.tensor_copy(glob[:, 11:12], tl3[:]),
                  "v4")
            em.op("vector", lambda v: v.tensor_scalar_mul(
                glob[:, 12:13], glob[:, 9:10], -1.0), "v4")

            em.wait("vector", "v4")
            for i in range(17):
                sl = slice(i * D, (i + 1) * D)
                em.op("vector", lambda v, sl=sl, i=i: v.tensor_reduce(
                    out=oqmx[:, i:i + 1], in_=attnout[:, sl], axis=AX.X,
                    op=ALU.max), "v4")
                em.op("vector", lambda v, sl=sl, i=i: v.tensor_reduce(
                    out=oqmn[:, i:i + 1], in_=attnout[:, sl], axis=AX.X,
                    op=ALU.min), "v4")
            NCo = 17
            em.wait("vector", "v4")
            em.op("vector", lambda v: v.tensor_scalar(
                out=oqmx[:, 0:NCo], in0=oqmx[:, 0:NCo],
                scalar1=glob[:, 12:13], scalar2=glob[:, 11:12],
                op0=ALU.add, op1=ALU.mult), "v4")
            em.op("vector", lambda v: v.tensor_scalar(
                out=oqmn[:, 0:NCo], in0=oqmn[:, 0:NCo], scalar1=-1.0,
                scalar2=glob[:, 9:10], op0=ALU.mult, op1=ALU.add), "v4")
            em.wait("vector", "v4")
            em.op("vector", lambda v: v.tensor_scalar_mul(
                oqmn[:, 0:NCo], oqmn[:, 0:NCo], glob[:, 11:12]), "v4")
            em.wait("vector", "v4")
            em.op("vector", lambda v: v.tensor_tensor(
                out=oqmx[:, 0:NCo], in0=oqmx[:, 0:NCo], in1=oqmn[:, 0:NCo],
                op=ALU.max), "v4")
            em.wait("vector", "v4")
            em.op("vector", lambda v: v.tensor_scalar_max(
                oqmx[:, 0:NCo], oqmx[:, 0:NCo], 1e-5), "v4")
            em.wait("vector", "v4")
            em.op("vector", lambda v: v.tensor_scalar_mul(
                oqscl[:, 0:NCo], oqmx[:, 0:NCo], 1.0 / 127.0), "v4")
            em.op("vector", lambda v: v.reciprocal(osx[:, 0:NCo],
                                                   oqmx[:, 0:NCo]), "v4")
            em.wait("vector", "v4")
            em.op("vector", lambda v: v.tensor_tensor(
                out=oqmn[:, 0:NCo], in0=oqmx[:, 0:NCo], in1=osx[:, 0:NCo],
                op=ALU.mult), "v4")
            em.wait("vector", "v4")
            em.op("vector", lambda v: v.tensor_scalar(
                out=oqmn[:, 0:NCo], in0=oqmn[:, 0:NCo], scalar1=-1.0,
                scalar2=2.0, op0=ALU.mult, op1=ALU.add), "v4")
            em.wait("vector", "v4")
            em.op("vector", lambda v: v.tensor_tensor(
                out=osx[:, 0:NCo], in0=osx[:, 0:NCo], in1=oqmn[:, 0:NCo],
                op=ALU.mult), "v4")
            em.wait("vector", "v4")
            em.op("vector", lambda v: v.tensor_scalar_mul(
                osx[:, 0:NCo], osx[:, 0:NCo], 127.0), "v4")
            em.wait("vector", "v4")
            em.op("vector", lambda v: v.tensor_scalar_mul(
                osx[:, 0:NCo], osx[:, 0:NCo], glob[:, 11:12]), "v4")
            for i in range(17):
                sl = slice(i * D, (i + 1) * D)
                em.wait("vector", "v4")
                em.op("vector", lambda v, sl=sl, i=i: v.tensor_scalar(
                    out=ybb[:], in0=attnout[:, sl], scalar1=glob[:, 12:13],
                    scalar2=osx[:, i:i + 1], op0=ALU.add, op1=ALU.mult),
                    "v4")
                emit_round2("v4", ybb, kfOb, T4, extra_waits=("t4T",))
                em.wait("tensor", "v4")
                em.wait("tensor", "sc4")
                pts4 = [psum[c % 4][:, (c // 4) * 64:(c // 4) * 64 + 64]
                        .bitcast(F16) for c in range(DC)]
                for c in range(DC):
                    em.op("tensor", lambda t, c=c, pts4=pts4: t.transpose(
                        pts4[c], kfOb[:, c * P:(c + 1) * P], ident16[:]),
                        "t4T")
                em.wait("scalar", "t4T")
                for c in range(DC):
                    em.op("scalar", lambda sc, c=c, i=i, pts4=pts4:
                          sc.activation(
                              koT[:, c * TQ + i * P:c * TQ + (i + 1) * P],
                              pts4[c], AF.Identity, bias=0.0, scale=1.0),
                          "sc4")

            ternarize_t(WOT, 7, ternO, wbuf4, ybb, T4)
            em.wait("vector", "vP")
            em.wait("vector", "v4")
            for i in range(17):
                em.op("vector", lambda v, i=i: v.tensor_tensor(
                    out=oqscl[:, i:i + 1], in0=oqscl[:, i:i + 1],
                    in1=glob[:, 7:8], op=ALU.mult), "v4")
            for tt in range(5):
                W4 = QT5_W[tt]
                off4 = QT5_T0[tt] * P
                nkk = W4 // P
                em.wait("tensor", "vP")
                em.wait("tensor", "v4")
                em.wait("tensor", "sc4")
                for oc in range(DC):
                    bank = psum[oc % 2]
                    em.wait("tensor", "vOs")
                    for c in range(DC):
                        em.op("tensor", lambda t, oc=oc, c=c, W4=W4, off4=off4,
                              bank=bank: t.matmul(
                                  bank[:, 0:W4],
                                  ternO[:, c * D + oc * P:c * D + oc * P + P],
                                  koT[:, c * TQ + off4:c * TQ + off4 + W4],
                                  start=(c == 0), stop=(c == DC - 1)),
                              "t4M" if c == DC - 1 else None)
                    em.wait("vector", "t4M")
                    em.op("vector", lambda v, oc=oc, W4=W4, bank=bank:
                          v.tensor_copy(ostgT[:, oc * 512:oc * 512 + W4],
                                        bank[:, 0:W4]), "vOs")
                for kk in range(nkk):
                    i = QT5_T0[tt] + kk
                    em.wait("tensor", "vOs")
                    em.wait("tensor", "vX")
                    for c in range(DC):
                        pb4 = psum[4 + (c % 2)][:, (c // 2 % 2) * P:
                                                (c // 2 % 2) * P + P]
                        em.wait("tensor", "vX")
                        em.op("tensor", lambda t, c=c, kk=kk, pb4=pb4:
                              t.transpose(
                                  pb4, ostgT[:, c * 512 + kk * P:
                                             c * 512 + (kk + 1) * P],
                                  ident32[:]), "t4X")
                        em.wait("vector", "t4X")
                        em.op("vector", lambda v, c=c, i=i, kk=kk, pb4=pb4:
                              v.tensor_scalar_mul(
                                  o4b[:, kk * D + c * P:kk * D + (c + 1) * P],
                                  pb4, oqscl[:, i:i + 1]), "vX")
                    okk = slice(kk * D, (kk + 1) * D)
                    em.wait("vector", "vX")
                    em.op("vector", lambda v, okk=okk, kk=kk: v.tensor_reduce(
                        out=xst4[:, kk:kk + 1], in_=o4b[:, okk], axis=AX.X,
                        op=ALU.add, negate=True), "vX")
                    em.wait("vector", "vX")
                    em.op("vector", lambda v, kk=kk: v.tensor_scalar_mul(
                        xst4[:, kk:kk + 1], xst4[:, kk:kk + 1], 1.0 / D),
                        "vX")
                    em.wait("scalar", "vX")
                    em.wait("scalar", "sc4")
                    em.op("scalar", lambda sc, okk=okk, kk=kk: sc.activation(
                        scr2[:], o4b[:, okk], AF.Square,
                        bias=xst4[:, kk:kk + 1], scale=1.0,
                        accum_out=xss4[:, kk:kk + 1]), "sc4")
                em.wait("vector", "sc4")
                em.op("vector", lambda v, nkk=nkk: v.tensor_scalar(
                    out=xss4[:, 0:nkk], in0=xss4[:, 0:nkk],
                    scalar1=1.0 / D, scalar2=EPS, op0=ALU.mult,
                    op1=ALU.add), "vX")
                em.wait("scalar", "vX")
                em.op("scalar", lambda sc, nkk=nkk: sc.activation(
                    xss4[:, 4:4 + nkk], xss4[:, 0:nkk], AF.Sqrt, bias=0.0,
                    scale=1.0), "sc4")
                em.wait("vector", "sc4")
                em.op("vector", lambda v, nkk=nkk: v.reciprocal(
                    xst4[:, 4:4 + nkk], xss4[:, 4:4 + nkk]), "vX")
                for _ in range(2):
                    em.wait("vector", "vX")
                    em.op("vector", lambda v, nkk=nkk: v.tensor_tensor(
                        out=oqmn[:, 0:nkk], in0=xst4[:, 4:4 + nkk],
                        in1=xst4[:, 4:4 + nkk], op=ALU.mult), "vX")
                    em.wait("vector", "vX")
                    em.op("vector", lambda v, nkk=nkk: v.tensor_tensor(
                        out=oqmn[:, 0:nkk], in0=oqmn[:, 0:nkk],
                        in1=xss4[:, 0:nkk], op=ALU.mult), "vX")
                    em.wait("vector", "vX")
                    em.op("vector", lambda v, nkk=nkk: v.tensor_scalar(
                        out=oqmn[:, 0:nkk], in0=oqmn[:, 0:nkk],
                        scalar1=-0.5, scalar2=1.5, op0=ALU.mult,
                        op1=ALU.add), "vX")
                    em.wait("vector", "vX")
                    em.op("vector", lambda v, nkk=nkk: v.tensor_tensor(
                        out=xst4[:, 4:4 + nkk], in0=xst4[:, 4:4 + nkk],
                        in1=oqmn[:, 0:nkk], op=ALU.mult), "vX")
                for kk in range(nkk):
                    i = QT5_T0[tt] + kk
                    okk = slice(kk * D, (kk + 1) * D)
                    xrow = i * P if i < 16 else 4096
                    xb4 = xa[kk % 2]
                    em.wait("sync", "vX")
                    em.dma("sync", lambda sy, xrow=xrow, xb4=xb4:
                           sy.dma_start(
                               out=xb4[:], in_=XC[xrow:xrow + P, :]), "dx4")
                    em.wait("vector", "dx4")
                    em.wait("vector", "vX")
                    em.op("vector", lambda v, okk=okk, kk=kk: v.tensor_scalar(
                        out=o4b[:, okk], in0=o4b[:, okk],
                        scalar1=xst4[:, kk:kk + 1],
                        scalar2=xst4[:, 4 + kk:5 + kk],
                        op0=ALU.add, op1=ALU.mult), "vX")
                    em.wait("vector", "vX")
                    em.wait("vector", "dx1")
                    if DEBUG:
                        em.wait("vector", "dst")
                    dst = x1x if i == 16 else ybb
                    em.op("vector", lambda v, dst=dst, okk=okk, xb4=xb4:
                          v.tensor_tensor(
                              out=dst[:], in0=xb4[:], in1=o4b[:, okk],
                              op=ALU.add), "vX")
                    if i < 16:
                        em.wait("sync", "vX")
                        em.dma("sync", lambda sy, i=i: sy.dma_start(
                            out=X1SP[i * P:(i + 1) * P, :], in_=ybb[:]),
                            "dx1")
                    if DEBUG:
                        em.wait("sync", "vX")
                        em.dma("sync", lambda sy, i=i, dst=dst: sy.dma_start(
                            out=dbg["X1"][i * P:(i + 1) * P, :], in_=dst[:]),
                            "dst")

        if BUILD_PHASES >= 5:
            em.phase_barrier()
            p4.close()
            ao_es.close()
            open_scopes = []
            p5 = ExitStack()
            pe5 = p5.enter_context
            open_scopes.append(p5)
            lor2h = pe5(nc.sbuf_tensor("lor2h", [1, 4 * D], F16))
            lor2l = pe5(nc.sbuf_tensor("lor2l", [1, 4 * D], F16))
            x1t = pe5(nc.sbuf_tensor("x1t", [P, D], F32))
            h2x = x1t
            h2t32 = scr
            h2Th = pe5(nc.sbuf_tensor("h2Th", [P, DC * 512], F16))
            h2Tl = pe5(nc.sbuf_tensor("h2Tl", [P, DC * 512], F16))
            hTh = pe5(nc.sbuf_tensor("hTh", [P, HFC * 512], F16))
            hTl = pe5(nc.sbuf_tensor("hTl", [P, HFC * 512], F16))
            wfh = [pe5(nc.sbuf_tensor(f"wfh{i}", [P, 8 * P], F16))
                   for i in range(2)]
            wfl = [pe5(nc.sbuf_tensor(f"wfl{i}", [P, 8 * P], F16))
                   for i in range(2)]
            wf32 = xa

            prowh = pe5(nc.sbuf_tensor("prowh", [1, 4 * 512], F16))
            prowl = pe5(nc.sbuf_tensor("prowl", [1, 4 * 512], F16))
            rrow32 = pe5(nc.sbuf_tensor("rrow32", [4, 512], F32))
            rrowf = pe5(nc.sbuf_tensor("rrowf", [1, 2048], F32))
            prowf = rrowf
            rrowh = pe5(nc.sbuf_tensor("rrowh", [1, 4 * 2048], F16))
            rrowl = pe5(nc.sbuf_tensor("rrowl", [1, 4 * 2048], F16))
            outsT = pe5(nc.sbuf_tensor("outsT", [P, DC * 512], F32))
            outf = pe5(nc.sbuf_tensor("outf", [P, D], F32))
            fst5 = pe5(nc.sbuf_tensor("fst5", [P, 8], F32))
            fss5 = pe5(nc.sbuf_tensor("fss5", [P, 8], F32))
            ftm5 = pe5(nc.sbuf_tensor("ftm5", [P, 4], F32))
            hact2 = [outf[:, 0:512], outf[:, 512:1024]]
            sclb = pe5(nc.sbuf_tensor("sclb", [P, 1], F32))
            pv16a5 = pe5(nc.sbuf_tensor("pv16a5", [P, 8 * P], F16))
            pv16b5 = pe5(nc.sbuf_tensor("pv16b5", [P, 8 * P], F16))
            lor16h = pv16a5
            lor16l = pv16b5
            b1t_sb = pe5(nc.sbuf_tensor("b1t_sb", [P, HFC], F32))
            b2t_sb = pe5(nc.sbuf_tensor("b2t_sb", [P, DC], F32))
            em.dma("sync", lambda sy: sy.dma_start(out=b1t_sb[:], in_=B1T),
                   "dl5")
            em.dma("sync", lambda sy: sy.dma_start(out=b2t_sb[:], in_=B2T),
                   "dl5")

            def ln_row(src_buf, dst_buf, vsem, scsem):
                em.wait("vector", vsem)
                em.op("vector", lambda v: v.tensor_reduce(
                    out=tl1[:], in_=src_buf[:], axis=AX.X, op=ALU.add,
                    negate=True), vsem)
                em.wait("vector", vsem)
                em.op("vector", lambda v: v.tensor_scalar_mul(
                    tl1[:], tl1[:], 1.0 / D), vsem)
                em.wait("scalar", vsem)
                em.wait("scalar", scsem)
                em.op("scalar", lambda sc: sc.activation(
                    scr2[:], src_buf[:], AF.Square, bias=tl1[:], scale=1.0,
                    accum_out=tl2[:]), scsem)
                em.wait("vector", scsem)
                em.op("vector", lambda v: v.tensor_scalar(
                    out=tl2[:], in0=tl2[:], scalar1=1.0 / D, scalar2=EPS,
                    op0=ALU.mult, op1=ALU.add), vsem)
                em.wait("scalar", vsem)
                em.op("scalar", lambda sc: sc.activation(
                    tl3[:], tl2[:], AF.Sqrt, bias=0.0, scale=1.0), scsem)
                em.wait("vector", scsem)
                em.op("vector", lambda v: v.reciprocal(tl4[:], tl3[:]), vsem)
                for _ in range(2):
                    em.wait("vector", vsem)
                    em.op("vector", lambda v: v.tensor_tensor(
                        out=tl3[:], in0=tl4[:], in1=tl4[:], op=ALU.mult),
                        vsem)
                    em.wait("vector", vsem)
                    em.op("vector", lambda v: v.tensor_tensor(
                        out=tl3[:], in0=tl3[:], in1=tl2[:], op=ALU.mult),
                        vsem)
                    em.wait("vector", vsem)
                    em.op("vector", lambda v: v.tensor_scalar(
                        out=tl3[:], in0=tl3[:], scalar1=-0.5, scalar2=1.5,
                        op0=ALU.mult, op1=ALU.add), vsem)
                    em.wait("vector", vsem)
                    em.op("vector", lambda v: v.tensor_tensor(
                        out=tl4[:], in0=tl4[:], in1=tl3[:], op=ALU.mult),
                        vsem)
                em.wait("vector", vsem)
                em.op("vector", lambda v: v.tensor_scalar(
                    out=dst_buf[:], in0=src_buf[:], scalar1=tl1[:],
                    scalar2=tl4[:], op0=ALU.add, op1=ALU.mult), vsem)

            ln_row(x1x, h2x, "v5", "sc5")
            em.wait("sync", "v5")
            em.wait("sync", "v5w")
            em.dma("sync", lambda sy: sy.dma_start(
                out=scr[0:16, :], in_=h2x[1:17, :]), "dl5")
            em.wait("tensor", "dl5")
            em.wait("tensor", "v5T")
            for c in range(DC):
                em.op("tensor", lambda t, c=c: t.transpose(
                    psum[6][:, c * 16:(c + 1) * 16],
                    scr[0:16, c * P:(c + 1) * P], ident32[0:16, 0:16]),
                    "t5L")
            em.wait("vector", "t5L")
            em.op("vector", lambda v: v.tensor_copy(lorT[:], psum[6][:, 0:P]),
                  "v5")
            em.wait("vector", "v5")
            em.op("vector", lambda v: v.tensor_copy(lorTh[:], lorT[:]), "v5")
            em.wait("vector", "v5")
            em.op("vector", lambda v: v.scalar_tensor_tensor(
                out=lorTl[:], in0=lorT[:], scalar=1.0, in1=lorTh[:],
                op0=ALU.mult, op1=ALU.subtract), "v5")
            em.wait("vector", "dl5")
            em.wait("vector", "v5w")
            em.wait("vector", "dw5")
            em.op("vector", lambda v: v.tensor_copy(
                lor16h[0:16, :], scr[0:16, :]), "v5")
            em.wait("vector", "v5")
            em.op("vector", lambda v: v.scalar_tensor_tensor(
                out=lor16l[0:16, :], in0=scr[0:16, :], scalar=1.0,
                in1=lor16h[0:16, :], op0=ALU.mult, op1=ALU.subtract), "v5")
            em.wait("sync", "v5")
            em.dma("sync", lambda sy: sy.dma_start(
                out=LRSPH, in_=lor16h[0:16:2, :]), "dl5")
            em.dma("sync", lambda sy: sy.dma_start(
                out=LRSPL, in_=lor16l[0:16:2, :]), "dl5")
            em.wait("sync", "dl5")

            def load_lor2(r0):
                em.wait("sync", "t5M")
                em.wait("sync", "t5W")
                em.wait("sync", "dl5")
                em.dma("sync", lambda sy, r0=r0: sy.dma_start(
                    out=lor2h[:], in_=LRSPH[r0:r0 + 4, :]
                    .rearrange("r d -> (r d)").unsqueeze(0)), "dl5")
                em.dma("sync", lambda sy, r0=r0: sy.dma_start(
                    out=lor2l[:], in_=LRSPL[r0:r0 + 4, :]
                    .rearrange("r d -> (r d)").unsqueeze(0)), "dl5")

            load_lor2(0)
            em.dma("sync", lambda sy: sy.dma_start(
                out=sclb[0:1, :], in_=SCL), "dl5")
            bcast_col(sclb[0:1, :], sclb[:], psum[6])

            em.wait("scalar", "dl5")
            for sg in range(4):
                for kk in range(4):
                    tok = sg * 512 + kk * P
                    em.wait("sync", "v5T")
                    em.wait("sync", "dl5")
                    em.wait("vector", "t5M")
                    em.wait("vector", "t5p")
                    em.wait("vector", "t5h")
                    em.dma("sync", lambda sy, tok=tok: sy.dma_start(
                        out=x1t[:], in_=X1SP[tok:tok + P, :]), "dx5")
                    em.wait("vector", "dx5")
                    ln_row(x1t, h2t32, "v5T", "sc5T")
                    em.wait("tensor", "v5T")
                    em.wait("tensor", "v5h")
                    pts5 = [psum[c % 2][:, (c // 2) * P:(c // 2) * P + P]
                            for c in range(DC)]
                    for c in range(DC):
                        em.op("tensor", lambda t, c=c, pts5=pts5: t.transpose(
                            pts5[c], h2t32[:, c * P:(c + 1) * P],
                            ident32[:]), "t5h")
                    em.wait("vector", "t5h")
                    for c in range(DC):
                        em.op("vector", lambda v, c=c, kk=kk, pts5=pts5:
                              v.tensor_copy(
                                  h2Th[:, c * 512 + kk * P:
                                       c * 512 + (kk + 1) * P],
                                  pts5[c]), "v5h")
                        em.wait("vector", "v5h")
                        em.op("vector", lambda v, c=c, kk=kk, pts5=pts5:
                              v.scalar_tensor_tensor(
                                  out=h2Tl[:, c * 512 + kk * P:
                                           c * 512 + (kk + 1) * P],
                                  in0=pts5[c], scalar=1.0,
                                  in1=h2Th[:, c * 512 + kk * P:
                                           c * 512 + (kk + 1) * P],
                                  op0=ALU.mult, op1=ALU.subtract), "v5h")
                em.wait("tensor", "v5h")
                em.wait("tensor", "v5p")
                for c in range(DC):
                    oh = lorTh[:, c * 16 + 1:c * 16 + 8:2]
                    ol = lorTl[:, c * 16 + 1:c * 16 + 8:2]
                    hh = h2Th[:, c * 512:(c + 1) * 512]
                    hl = h2Tl[:, c * 512:(c + 1) * 512]
                    for (sa, mb, first, last) in (
                            (oh, hh, c == 0, False), (oh, hl, False, False),
                            (ol, hh, False, c == DC - 1)):
                        em.op("tensor", lambda t, sa=sa, mb=mb, first=first,
                              last=last: t.matmul(
                                  psum[4][0:4, 0:512], sa, mb,
                                  start=first, stop=last),
                              "t5p" if last else None)
                em.wait("vector", "t5p")
                em.op("vector", lambda v: v.tensor_scalar_mul(
                    rrow32[0:4, :], psum[4][0:4, 0:512], sclb[0:4, :]), "v5p")
                em.wait("sync", "v5p")
                em.dma("sync", lambda sy: sy.dma_start(
                    out=PRSP, in_=rrow32[0:4, :]), "dp5")
                em.wait("sync", "dp5")
                em.dma("sync", lambda sy: sy.dma_start(
                    out=prowf[:], in_=PRSP.rearrange("r w -> (r w)")
                    .unsqueeze(0)), "dp5")
                em.wait("vector", "dp5")
                em.op("vector", lambda v: v.tensor_copy(prowh[:], prowf[:]),
                      "v5p")
                em.wait("vector", "v5p")
                em.op("vector", lambda v: v.scalar_tensor_tensor(
                    out=prowl[:], in0=prowf[:], scalar=1.0, in1=prowh[:],
                    op0=ALU.mult, op1=ALU.subtract), "v5p")
                load_lor2(0)
                em.wait("tensor", "dl5")
                em.wait("tensor", "v5p")
                em.wait("tensor", "v5h")
                em.wait("scalar", "dout")
                h_dw = []
                h_v5s = []
                h_t5M = []
                h_g = []
                em.wait("sync", "v5u")
                em.wait("sync", "v5s")
                em.wait("gpsimd", "v5u")
                em.wait("gpsimd", "v5s")
                em.wait("tensor", "sc5o")

                def w1_dma(hc):
                    sem = "dw1a" if hc % 2 == 0 else "dw1b"
                    semg = sem + "g"
                    if hc >= 2:
                        em.wait("sync", "v5s", h_v5s[hc - 2])
                        em.wait("gpsimd", "v5s", h_v5s[hc - 2])
                    em.dma("sync", lambda sy, hc=hc: sy.dma_start(
                        out=wf32[hc % 2][:, 0:512],
                        in_=W1B[hc, :, 0:512]), sem)
                    em.dma("gpsimd", lambda g, hc=hc: g.dma_start(
                        out=wf32[hc % 2][:, 512:1024],
                        in_=W1B[hc, :, 512:1024]), semg)
                    h_dw.append((em.sem(sem).n, em.sem(semg).n))

                def emit_rrow(j):

                    em.wait("tensor", "v5g")
                    em.wait("tensor", "v5r")
                    em.wait("tensor", "vP")
                    for cc in range(8):
                        hcol = j * 8 + cc
                        oh = lorTh[:, cc * 16 + 9:cc * 16 + 16:2]
                        ol = lorTl[:, cc * 16 + 9:cc * 16 + 16:2]
                        hh = hTh[:, hcol * 512:(hcol + 1) * 512]
                        hl = hTl[:, hcol * 512:(hcol + 1) * 512]
                        for (sa, mb, first, last) in (
                                (oh, hh, cc == 0, False),
                                (oh, hl, False, False),
                                (ol, hh, False, cc == 7)):
                            em.op("tensor", lambda t, sa=sa, mb=mb,
                                  first=first, last=last: t.matmul(
                                      psum[7][0:4, 0:512], sa, mb,
                                      start=first, stop=last),
                                  "t5r" if last else None)
                    em.wait("vector", "t5r")
                    em.wait("vector", "dp5")
                    em.wait("vector", "dr5")
                    em.op("vector", lambda v: v.tensor_scalar_mul(
                        rrow32[0:4, :], psum[7][0:4, 0:512], sclb[0:4, :]),
                        "v5r")
                    em.wait("gpsimd", "v5r")
                    em.dma("gpsimd", lambda g: g.dma_start(
                        out=RRSP, in_=rrow32[0:4, :]), "dr5")
                    em.wait("gpsimd", "dr5")
                    em.wait("gpsimd", "v5r")
                    em.wait("gpsimd", "v5p")
                    em.dma("gpsimd", lambda g: g.dma_start(
                        out=rrowf[:], in_=RRSP.rearrange("r w -> (r w)")
                        .unsqueeze(0)), "dr5")
                    em.wait("vector", "dr5")
                    em.op("vector", lambda v, j=j: v.tensor_copy(
                        rrowh[0:1, j * 2048:(j + 1) * 2048], rrowf[:]),
                        "v5r")
                    em.wait("vector", "v5r")
                    em.op("vector", lambda v, j=j: v.scalar_tensor_tensor(
                        out=rrowl[0:1, j * 2048:(j + 1) * 2048],
                        in0=rrowf[:], scalar=1.0,
                        in1=rrowh[0:1, j * 2048:(j + 1) * 2048],
                        op0=ALU.mult, op1=ALU.subtract), "v5r")

                def w1_split(hc):
                    _s = "dw1a" if hc % 2 == 0 else "dw1b"
                    em.wait("vector", _s, h_dw[hc][0])
                    em.wait("vector", _s + "g", h_dw[hc][1])
                    if hc >= 2:
                        em.wait("vector", "t5M", h_t5M[hc - 2])
                    em.op("vector", lambda v, hc=hc: v.tensor_copy(
                        wfh[hc % 2][:], wf32[hc % 2][:]), "v5s")
                    em.wait("vector", "v5s")
                    em.op("vector", lambda v, hc=hc: v.scalar_tensor_tensor(
                        out=wfl[hc % 2][:], in0=wf32[hc % 2][:], scalar=1.0,
                        in1=wfh[hc % 2][:], op0=ALU.mult, op1=ALU.subtract),
                        "v5s")
                    h_v5s.append(em.sem("v5s").n)

                w1_dma(0)
                w1_dma(1)
                w1_split(0)
                for hc in range(HFC):
                    if hc + 2 < HFC:
                        w1_dma(hc + 2)
                    if hc + 1 < HFC:
                        w1_split(hc + 1)
                    bank = psum[5 + (hc % 2)]
                    wh, wl = wfh[hc % 2], wfl[hc % 2]
                    em.wait("tensor", "v5s", h_v5s[hc])
                    if len(h_g) >= 2:
                        em.wait("tensor", "sc5g", h_g[hc - 2])
                    i_blk = hc // 8
                    for c in range(DC):
                        whc = wh[:, c * P:(c + 1) * P]
                        wlc = wl[:, c * P:(c + 1) * P]
                        hh = h2Th[:, c * 512:(c + 1) * 512]
                        hl = h2Tl[:, c * 512:(c + 1) * 512]
                        for (pi, (sa, mb)) in enumerate(
                                ((whc, hh), (whc, hl), (wlc, hh))):
                            em.op("tensor", lambda t, sa=sa, mb=mb,
                                  bank=bank, first=(c == 0 and pi == 0):
                                  t.matmul(bank[:, 0:512], sa, mb,
                                           start=first, stop=False), None)
                    e0 = i_blk * D + (hc % 8) * P
                    p0_ = i_blk * 512
                    for (sa, mb, last) in (
                            (lor2h, prowh, False), (lor2h, prowl, False),
                            (lor2l, prowh, True)):
                        em.op("tensor", lambda t, sa=sa, mb=mb, e0=e0,
                              p0_=p0_, bank=bank, last=last: t.matmul(
                                  bank[:, 0:512],
                                  sa[0:1, e0:e0 + P],
                                  mb[0:1, p0_:p0_ + 512],
                                  start=False, stop=last),
                              "t5M" if last else None)
                    h_t5M.append(em.sem("t5M").n)
                    em.wait("scalar", "t5M", h_t5M[hc])
                    em.wait("scalar", "v5g")
                    em.op("scalar", lambda sc, hc=hc, bank=bank:
                          sc.activation(
                              hact2[hc % 2], bank[:, 0:512], AF.Gelu,
                              bias=b1t_sb[:, hc:hc + 1], scale=1.0), "sc5g")
                    h_g.append(em.sem("sc5g").n)
                    em.wait("vector", "sc5g", h_g[hc])
                    em.wait("vector", "t5r")
                    em.wait("vector", "t5q")
                    em.wait("vector", "t5W")
                    em.op("vector", lambda v, hc=hc: v.tensor_copy(
                        hTh[:, hc * 512:(hc + 1) * 512], hact2[hc % 2]),
                        "v5g")
                    em.wait("vector", "v5g")
                    em.op("vector", lambda v, hc=hc: v.scalar_tensor_tensor(
                        out=hTl[:, hc * 512:(hc + 1) * 512],
                        in0=hact2[hc % 2], scalar=1.0,
                        in1=hTh[:, hc * 512:(hc + 1) * 512],
                        op0=ALU.mult, op1=ALU.subtract), "v5g")
                    if hc % 8 == 7:
                        emit_rrow(hc // 8)
                load_lor2(4)
                em.wait("tensor", "dl5")
                h_dw2 = []
                h_v5u = []
                h_t5W = []
                h_t5q = []
                h_o = []
                em.wait("sync", "v5s")
                em.wait("sync", "v5u")
                em.wait("gpsimd", "v5s")
                em.wait("gpsimd", "v5u")
                em.wait("tensor", "sc5g")
                em.wait("vector", "t5M")

                def w2_dma(idx):
                    oc_, hq_ = idx // 4, idx % 4
                    sem = "dw2a" if idx % 2 == 0 else "dw2b"
                    semg = sem + "g"
                    if idx >= 2:
                        em.wait("sync", "v5u", h_v5u[idx - 2])
                        em.wait("gpsimd", "v5u", h_v5u[idx - 2])
                    h0 = hq_ * 1024
                    em.dma("sync", lambda sy, oc_=oc_, h0=h0, idx=idx:
                           sy.dma_start(
                               out=wf32[idx % 2][:, 0:512],
                               in_=W2B[oc_, :, h0:h0 + 512]), sem)
                    em.dma("gpsimd", lambda g, oc_=oc_, h0=h0, idx=idx:
                           g.dma_start(
                               out=wf32[idx % 2][:, 512:1024],
                               in_=W2B[oc_, :, h0 + 512:h0 + 1024]), semg)
                    h_dw2.append((em.sem(sem).n, em.sem(semg).n))

                w2_dma(0)
                for oc in range(DC):
                    bank = psum[5 + (oc % 2)]
                    if len(h_o) >= 2:
                        em.wait("tensor", "sc5o", h_o[oc - 2])
                    em.wait("tensor", "v5r")
                    em.wait("tensor", "v5p")
                    for hq in range(4):
                        idx = oc * 4 + hq
                        if idx + 1 < 32:
                            w2_dma(idx + 1)
                        _s2 = "dw2a" if idx % 2 == 0 else "dw2b"
                        em.wait("vector", _s2, h_dw2[idx][0])
                        em.wait("vector", _s2 + "g", h_dw2[idx][1])
                        if len(h_t5q) >= 2:
                            em.wait("vector", "t5q", h_t5q[idx - 2])
                        em.op("vector", lambda v, idx=idx: v.tensor_copy(
                            wfh[idx % 2][:], wf32[idx % 2][:]), "v5u")
                        em.wait("vector", "v5u")
                        em.op("vector", lambda v, idx=idx:
                              v.scalar_tensor_tensor(
                                  out=wfl[idx % 2][:], in0=wf32[idx % 2][:],
                                  scalar=1.0, in1=wfh[idx % 2][:],
                                  op0=ALU.mult, op1=ALU.subtract), "v5u")
                        h_v5u.append(em.sem("v5u").n)
                        em.wait("tensor", "v5u", h_v5u[idx])
                        wh, wl = wfh[idx % 2], wfl[idx % 2]
                        for c in range(DC):
                            hcol = hq * 8 + c
                            whc = wh[:, c * P:(c + 1) * P]
                            wlc = wl[:, c * P:(c + 1) * P]
                            hh = hTh[:, hcol * 512:(hcol + 1) * 512]
                            hl = hTl[:, hcol * 512:(hcol + 1) * 512]
                            for (pi, (sa, mb)) in enumerate(
                                    ((whc, hh), (whc, hl), (wlc, hh))):
                                _first = (hq == 0 and c == 0 and pi == 0)
                                _sem = ("t5q" if (c == DC - 1 and pi == 2)
                                        else None)
                                em.op("tensor", lambda t, sa=sa, mb=mb,
                                      bank=bank, first=_first:
                                      t.matmul(bank[:, 0:512], sa, mb,
                                               start=first, stop=False),
                                      _sem)
                        h_t5q.append(em.sem("t5q").n)
                    i_m2 = oc // 2
                    for j in range(4):
                        e0 = i_m2 * D
                        o0 = (oc % 2) * P
                        sl0 = e0 + 4 * o0 + j
                        r0_ = j * 2048 + i_m2 * 512
                        for (sa, mb, last) in (
                                (lor2h, rrowh, False), (lor2h, rrowl, False),
                                (lor2l, rrowh, j == 3)):
                            em.op("tensor", lambda t, sa=sa, mb=mb, sl0=sl0,
                                  r0_=r0_, bank=bank, last=last: t.matmul(
                                      bank[:, 0:512],
                                      sa[0:1, sl0:sl0 + 509:4],
                                      mb[0:1, r0_:r0_ + 512],
                                      start=False, stop=last),
                                  "t5W" if last else None)
                    h_t5W.append(em.sem("t5W").n)
                    em.wait("scalar", "t5W", h_t5W[oc])
                    em.wait("scalar", "v5o")
                    em.op("scalar", lambda sc, oc=oc, bank=bank:
                          sc.activation(
                              outsT[:, oc * 512:(oc + 1) * 512],
                              bank[:, 0:512], AF.Identity,
                              bias=b2t_sb[:, oc:oc + 1], scale=1.0), "sc5o")
                    h_o.append(em.sem("sc5o").n)
                for kk in range(4):
                    ofk = hTh[:, kk * 2048:kk * 2048 + 2048].bitcast(F32)
                    em.wait("tensor", "sc5o")
                    em.wait("tensor", "v5o")
                    for c in range(DC):
                        pb5 = psum[2 + (c % 2)][:, (c // 2 % 2) * P:
                                                (c // 2 % 2) * P + P]
                        em.wait("tensor", "v5o")
                        em.op("tensor", lambda t, c=c, kk=kk, pb5=pb5:
                              t.transpose(
                                  pb5, outsT[:, c * 512 + kk * P:
                                             c * 512 + (kk + 1) * P],
                                  ident32[:]), "t5o")
                        em.wait("vector", "t5o")
                        em.wait("vector", "t5r")
                        em.wait("vector", "t5q")
                        em.wait("vector", "t5W")
                        em.op("vector", lambda v, c=c, pb5=pb5, ofk=ofk:
                              v.tensor_copy(ofk[:, c * P:(c + 1) * P], pb5),
                              "v5o")
                    em.wait("vector", "v5o")
                    em.op("vector", lambda v, ofk=ofk, kk=kk: v.tensor_reduce(
                        out=fst5[:, kk:kk + 1], in_=ofk[:], axis=AX.X,
                        op=ALU.add, negate=True), "v5o")
                    em.wait("vector", "v5o")
                    em.op("vector", lambda v, kk=kk: v.tensor_scalar_mul(
                        fst5[:, kk:kk + 1], fst5[:, kk:kk + 1], 1.0 / D),
                        "v5o")
                    em.wait("scalar", "v5o")
                    em.wait("scalar", "sc5T")
                    em.op("scalar", lambda sc, ofk=ofk, kk=kk: sc.activation(
                        scr2[:], ofk[:], AF.Square, bias=fst5[:, kk:kk + 1],
                        scale=1.0, accum_out=fss5[:, kk:kk + 1]), "sc5T")
                em.wait("vector", "sc5T")
                em.op("vector", lambda v: v.tensor_scalar(
                    out=fss5[:, 0:4], in0=fss5[:, 0:4], scalar1=1.0 / D,
                    scalar2=EPS, op0=ALU.mult, op1=ALU.add), "v5o")
                em.wait("scalar", "v5o")
                em.op("scalar", lambda sc: sc.activation(
                    fss5[:, 4:8], fss5[:, 0:4], AF.Sqrt, bias=0.0,
                    scale=1.0), "sc5T")
                em.wait("vector", "sc5T")
                em.op("vector", lambda v: v.reciprocal(
                    fst5[:, 4:8], fss5[:, 4:8]), "v5o")
                for _ in range(2):
                    em.wait("vector", "v5o")
                    em.op("vector", lambda v: v.tensor_tensor(
                        out=ftm5[:, 0:4], in0=fst5[:, 4:8],
                        in1=fst5[:, 4:8], op=ALU.mult), "v5o")
                    em.wait("vector", "v5o")
                    em.op("vector", lambda v: v.tensor_tensor(
                        out=ftm5[:, 0:4], in0=ftm5[:, 0:4],
                        in1=fss5[:, 0:4], op=ALU.mult), "v5o")
                    em.wait("vector", "v5o")
                    em.op("vector", lambda v: v.tensor_scalar(
                        out=ftm5[:, 0:4], in0=ftm5[:, 0:4], scalar1=-0.5,
                        scalar2=1.5, op0=ALU.mult, op1=ALU.add), "v5o")
                    em.wait("vector", "v5o")
                    em.op("vector", lambda v: v.tensor_tensor(
                        out=fst5[:, 4:8], in0=fst5[:, 4:8],
                        in1=ftm5[:, 0:4], op=ALU.mult), "v5o")
                for kk in range(4):
                    i_tile = sg * 4 + kk
                    ofk = hTh[:, kk * 2048:kk * 2048 + 2048].bitcast(F32)
                    em.wait("sync", "v5o")
                    em.dma("sync", lambda sy, i_tile=i_tile: sy.dma_start(
                        out=x1t[:], in_=X1SP[i_tile * P:(i_tile + 1) * P, :]),
                        "dx5")
                    em.wait("vector", "dx5")
                    em.wait("vector", "v5o")
                    em.op("vector", lambda v, ofk=ofk, kk=kk: v.tensor_scalar(
                        out=ofk[:], in0=ofk[:], scalar1=fst5[:, kk:kk + 1],
                        scalar2=fst5[:, 4 + kk:5 + kk], op0=ALU.add,
                        op1=ALU.mult), "v5o")
                    em.wait("vector", "v5o")
                    em.wait("vector", "dout")
                    em.op("vector", lambda v, ofk=ofk: v.tensor_tensor(
                        out=outf[:], in0=ofk[:], in1=x1t[:], op=ALU.add),
                        "v5o")
                    em.wait("sync", "v5o")
                    em.dma("sync", lambda sy, i_tile=i_tile: sy.dma_start(
                        out=OUT[i_tile * P:(i_tile + 1) * P, :], in_=outf[:]),
                        "dout")

        if BUILD_PHASES < 2:
            finalize(em)
            fp_es.close()
            run_blocks(nc, em, es)
            return nc

        finalize(em)
        if BUILD_PHASES < 3:
            fp_es.close()
        elif BUILD_PHASES < 4:
            at_es.close()
            ao_es.close()
            mid_es.close()
        else:
            for _es in reversed(open_scopes):
                _es.close()
            mid_es.close()
        run_blocks(nc, em, es)
    return nc


def finalize(em):
    for sname, s in list(em.sems.items()):
        em.wait("sync", s)


def run_blocks(nc, em, es):
    block = es.enter_context(nc.Block())

    @block.sync
    def _(sy):
        for f in em.prog["sync"]:
            f(sy)

    @block.vector
    def _(v):
        for f in em.prog["vector"]:
            f(v)

    @block.scalar
    def _(sc):
        for f in em.prog["scalar"]:
            f(sc)

    @block.tensor
    def _(t):
        for f in em.prog["tensor"]:
            f(t)

    @block.gpsimd
    def _(g):
        for f in em.prog["gpsimd"]:
            f(g)


def stage_inputs(inputs):
    x = np.ascontiguousarray(inputs["L0_attn_x"], dtype=np.float32)
    W1 = np.ascontiguousarray(inputs["L0_ffn_W1"], dtype=np.float32)
    W2 = np.ascontiguousarray(inputs["L0_ffn_W2"], dtype=np.float32)
    W1B = np.ascontiguousarray(
        W1.reshape(HFC, P, DC, P).transpose(0, 3, 2, 1).reshape(HFC, P, D))
    W2B = np.ascontiguousarray(
        W2.reshape(DC, P, 4, DC, P).transpose(0, 4, 2, 3, 1)
        .reshape(DC, P, HFF))
    B1T = np.ascontiguousarray(
        inputs["L0_ffn_b1"].reshape(HFC, P).T.astype(np.float32))
    B2T = np.ascontiguousarray(
        inputs["L0_ffn_b2"].reshape(DC, P).T.astype(np.float32))
    SCLv = np.asarray(inputs["L0_ffn_scale"], np.float32).reshape(1, 1)
    idf32 = np.eye(P, dtype=np.float32)
    idf16 = np.eye(P, dtype=np.float16)
    common = dict(
        WQT=np.ascontiguousarray(inputs["L0_attn_w_q"].T.astype(np.float32)),
        WKT=np.ascontiguousarray(inputs["L0_attn_w_k"].T.astype(np.float32)),
        WVT=np.ascontiguousarray(inputs["L0_attn_w_v"].T.astype(np.float32)),
        WOT=np.ascontiguousarray(inputs["L0_attn_w_o"].T.astype(np.float32)),
        W1B=W1B, W2B=W2B, B1T=B1T, B2T=B2T, SCL=SCLv,
        IDF32=idf32, IDF16=idf16,
    )
    in_maps = []
    for c in range(N_CORES):
        b, j = c // 2, c % 2
        xb = x[b]
        xcat = np.concatenate(
            [xb[j * TOWN:(j + 1) * TOWN], xb[(1 - j) * TOWN:(2 - j) * TOWN],
             xb[0:TEXTRA]], axis=0)
        m = dict(common)
        m["XC"] = np.ascontiguousarray(xcat)
        in_maps.append(m)
    return in_maps


def kernel(**inputs):
    in_maps = stage_inputs(inputs)
    nc = build_program()
    res = run_bass_kernel_spmd(nc, in_maps, core_ids=list(range(N_CORES)))
    out = np.empty((B, S, D), dtype=np.float32)
    for c in range(N_CORES):
        b, j = c // 2, c % 2
        out[b, j * TOWN:(j + 1) * TOWN, :] = res.results[c]["OUT"]
    return out



# version 12
# speedup vs baseline: 1.0729x; 1.0729x over previous
import math
from contextlib import ExitStack

import numpy as np
import concourse.bass as bass
import concourse.mybir as mybir
from concourse.bass_utils import run_bass_kernel_spmd

F32 = mybir.dt.float32
F16 = mybir.dt.float16
I32 = mybir.dt.int32
ALU = mybir.AluOpType
AF = mybir.ActivationFunctionType
AX = mybir.AxisListType

P = 128
D = 1024
NH = 16
HD = 64
S = 4096
TOWN = 2048
TEXTRA = 128
TCAT = S + TEXTRA
TQ = TOWN + TEXTRA
NT_CAT = TCAT // P
NT_STAT = S // P
NT_OWN = TOWN // P
DC = D // P
HFF = 4096
HFC = HFF // P
EPS = 1e-5
N_CORES = 8
B = 4

BUILD_PHASES = 99
DEBUG = False


class Sem:
    def __init__(self, es, nc, name):
        self.h = es.enter_context(nc.semaphore(name))
        self.n = 0


class Emitter:

    def __init__(self, nc, es):
        self.nc = nc
        self.es = es
        self.prog = {k: [] for k in ("sync", "vector", "scalar", "tensor",
                                     "gpsimd")}
        self.sems = {}

    def sem(self, name):
        if name not in self.sems:
            self.sems[name] = Sem(self.es, self.nc, name)
        return self.sems[name]

    def op(self, eng, f, sem=None, k=1):
        if sem is not None:
            s = self.sem(sem) if isinstance(sem, str) else sem
            s.n += k
            h = s.h

            def run(e, f=f, h=h, k=k):
                f(e).then_inc(h, k)
        else:
            def run(e, f=f):
                f(e)
        self.prog[eng].append(run)

    def wait(self, eng, sem, val=None):
        s = self.sem(sem) if isinstance(sem, str) else sem
        v = s.n if val is None else val
        self.prog[eng].append(lambda e, h=s.h, v=v: e.wait_ge(h, v))

    def barrier(self, eng_wait, sems):
        for sname in sems:
            self.wait(eng_wait, sname)

    def wait_all(self, eng):
        for s in self.sems.values():
            self.wait(eng, s)

    def dma(self, eng, f, sem):
        self.wait(eng, sem)
        self.op(eng, f, sem, 16)

    def phase_barrier(self):
        for eng in self.prog:
            for s in self.sems.values():
                self.wait(eng, s)


def ceil_div(a, b):
    return (a + b - 1) // b


def build_program():
    nc = bass.Bass("TRN2", target_bir_lowering=False)

    XC = nc.dram_tensor("XC", [TCAT, D], F32, kind="ExternalInput").ap()
    WQT = nc.dram_tensor("WQT", [D, D], F32, kind="ExternalInput").ap()
    WKT = nc.dram_tensor("WKT", [D, D], F32, kind="ExternalInput").ap()
    WVT = nc.dram_tensor("WVT", [D, D], F32, kind="ExternalInput").ap()
    WOT = nc.dram_tensor("WOT", [D, D], F32, kind="ExternalInput").ap()
    W1B = nc.dram_tensor("W1B", [HFC, P, D], F32, kind="ExternalInput").ap()
    W2B = nc.dram_tensor("W2B", [DC, P, HFF], F32, kind="ExternalInput").ap()
    B1T = nc.dram_tensor("B1T", [P, HFC], F32, kind="ExternalInput").ap()
    B2T = nc.dram_tensor("B2T", [P, DC], F32, kind="ExternalInput").ap()
    SCL = nc.dram_tensor("SCL", [1, 1], F32, kind="ExternalInput").ap()
    IDF32 = nc.dram_tensor("IDF32", [P, P], F32, kind="ExternalInput").ap()
    IDF16 = nc.dram_tensor("IDF16", [P, P], F16, kind="ExternalInput").ap()
    OUT = nc.dram_tensor("OUT", [TOWN, D], F32, kind="ExternalOutput").ap()

    _sk = "ExternalOutput" if DEBUG else "Internal"
    KH = nc.dram_tensor("KHs", [D, S], F16, kind=_sk).ap()
    KL = nc.dram_tensor("KLs", [D, S], F16, kind=_sk).ap()
    VSP = nc.dram_tensor("VSPs", [NH // 2, P, 32 * 130], F32, kind=_sk).ap()
    QH = nc.dram_tensor("QHs", [D, TQ], F16, kind=_sk).ap()
    QL = nc.dram_tensor("QLs", [D, TQ], F16, kind=_sk).ap()
    X1SP = nc.dram_tensor("X1SPs", [TOWN, D], F32, kind="Internal").ap()
    W1H = nc.dram_tensor("W1Hs", [HFC, D, P], F16, kind="Internal").ap()
    W1L = nc.dram_tensor("W1Ls", [HFC, D, P], F16, kind="Internal").ap()
    W2H = nc.dram_tensor("W2Hs", [DC, HFF, P], F16, kind="Internal").ap()
    W2L = nc.dram_tensor("W2Ls", [DC, HFF, P], F16, kind="Internal").ap()
    CCI = nc.dram_tensor("CCIs", [1, 8], F32, kind="Internal")
    CCO = nc.dram_tensor("CCOs", [1, 8], F32, kind="Internal")
    LRSPH = nc.dram_tensor("LRSPH", [8, D], F16, kind="Internal").ap()
    LRSPL = nc.dram_tensor("LRSPL", [8, D], F16, kind="Internal").ap()
    PRSP = nc.dram_tensor("PRSP", [4, 512], F32, kind="Internal").ap()
    RRSP = nc.dram_tensor("RRSP", [4, 512], F32, kind="Internal").ap()

    dbg = {}
    if DEBUG:
        dbg["KT"] = nc.dram_tensor("DBG_KT", [P, DC * TCAT], F16,
                                   kind="ExternalOutput").ap()
        dbg["ST"] = nc.dram_tensor("DBG_ST", [P, 16], F32,
                                   kind="ExternalOutput").ap()
        dbg["Q"] = nc.dram_tensor("DBG_Q", [D, TQ], F32,
                                  kind="ExternalOutput").ap()
        dbg["AO"] = nc.dram_tensor("DBG_AO", [TQ, D], F32,
                                   kind="ExternalOutput").ap()
        dbg["X1"] = nc.dram_tensor("DBG_X1", [TQ, D], F32,
                                   kind="ExternalOutput").ap()

    with ExitStack() as es:
        e = es.enter_context
        em = Emitter(nc, es)

        xa = [e(nc.sbuf_tensor(f"xa{i}", [P, D], F32)) for i in range(2)]
        scr = e(nc.sbuf_tensor("scr", [P, D], F32))
        scr2 = e(nc.sbuf_tensor("scr2", [P, D], F32))
        nmean = e(nc.sbuf_tensor("nmean", [P, NT_CAT], F32))
        rstd = e(nc.sbuf_tensor("rstd", [P, NT_CAT], F32))
        vssum = e(nc.sbuf_tensor("vssum", [P, NT_CAT], F32))
        hmax = e(nc.sbuf_tensor("hmax", [P, NT_CAT], F32))
        hmin = e(nc.sbuf_tensor("hmin", [P, NT_CAT], F32))
        s1c = e(nc.sbuf_tensor("s1c", [P, NT_CAT], F32))
        s2c = e(nc.sbuf_tensor("s2c", [P, NT_CAT], F32))
        ksq = e(nc.sbuf_tensor("ksq", [P, NT_CAT], F32))
        invsx = e(nc.sbuf_tensor("invsx", [P, NT_CAT], F32))
        sxc = e(nc.sbuf_tensor("sxc", [P, NT_CAT], F32))
        tl1 = e(nc.sbuf_tensor("tl1", [P, 1], F32))
        tl2 = e(nc.sbuf_tensor("tl2", [P, 1], F32))
        tl3 = e(nc.sbuf_tensor("tl3", [P, 1], F32))
        tl4 = e(nc.sbuf_tensor("tl4", [P, 1], F32))
        glob = e(nc.sbuf_tensor("glob", [P, 16], F32))
        ones1 = e(nc.sbuf_tensor("ones1", [1, P], F32))
        kescl = e(nc.sbuf_tensor("kescl", [P, NT_STAT], F32))
        wacc = e(nc.sbuf_tensor("wacc", [P, 1], F32))
        ident32 = e(nc.sbuf_tensor("ident32", [P, P], F32))
        ident16 = e(nc.sbuf_tensor("ident16", [P, P], F16))
        psum = [e(nc.psum_tensor(f"ps{i}", [P, 512], F32)) for i in range(8)]
        fp_es = ExitStack()
        fe = fp_es.enter_context
        h1b = fe(nc.sbuf_tensor("h1b", [P, D], F32))
        yb = fe(nc.sbuf_tensor("yb", [P, D], F32))
        eb = fe(nc.sbuf_tensor("eb", [P, D], F32))
        cb = fe(nc.sbuf_tensor("cb", [P, D], F32))
        db = fe(nc.sbuf_tensor("db", [P, D], F32))
        k1b = fe(nc.sbuf_tensor("k1b", [P, D], F32))
        ki32 = fe(nc.sbuf_tensor("ki32", [P, D], I32))
        kf16 = fe(nc.sbuf_tensor("kf16", [P, D], F16))
        kT = fe(nc.sbuf_tensor("kT", [P, DC * TCAT], F16))
        tern = fe(nc.sbuf_tensor("tern", [P, DC * D], F16))
        wbuf = [fe(nc.sbuf_tensor(f"wbuf{i}", [P, D], F32)) for i in range(2)]
        pevac = fe(nc.sbuf_tensor("pevac", [P, 512], F32))
        pv16a = fe(nc.sbuf_tensor("pv16a", [P, 512], F16))
        pv16b = fe(nc.sbuf_tensor("pv16b", [P, 512], F16))
        vstg = [fe(nc.sbuf_tensor(f"vstg{i}", [P, 520], F32))
                for i in range(2)]
        vscl = fe(nc.sbuf_tensor("vscl", [P, NT_STAT], F32))
        qscl = fe(nc.sbuf_tensor("qscl", [P, NT_CAT], F32))
        qnsq = fe(nc.sbuf_tensor("qnsq", [P, 17], F32))
        vtmp = fe(nc.sbuf_tensor("vtmp", [P, P], F32))
        qstg = fe(nc.sbuf_tensor("qstg", [P, 512], F32))
        qstg2 = fe(nc.sbuf_tensor("qstg2", [P, 512], F32))

        em.op("vector", lambda v: v.memset(glob[:], 0.0), "vA")
        em.op("vector", lambda v: v.memset(ones1[:], 1.0), "vA")
        em.dma("sync", lambda sy: sy.dma_start(out=ident32[:], in_=IDF32),
               "dma")
        em.dma("sync", lambda sy: sy.dma_start(out=ident16[:], in_=IDF16),
               "dma")
        em.wait("tensor", "dma")

        def emit_round_t(vsem, src, dst_f32, T):
            _ki, _eb, _cb, _db, _k1 = (T["ki32"], T["eb"], T["cb"], T["db"],
                                       T["k1b"])
            em.wait("vector", vsem)
            em.op("vector", lambda v: v.tensor_copy(_ki[:], src[:]), vsem)
            em.wait("vector", vsem)
            em.op("vector", lambda v: v.scalar_tensor_tensor(
                out=_eb[:], in0=src[:], scalar=1.0, in1=_ki[:],
                op0=ALU.mult, op1=ALU.subtract), vsem)
            em.wait("vector", vsem)
            em.op("vector", lambda v: v.tensor_single_scalar(
                out=_cb[:], in_=_eb[:], scalar=0.5, op=ALU.is_gt), vsem)
            em.op("vector", lambda v: v.tensor_single_scalar(
                out=_db[:], in_=_eb[:], scalar=-0.5, op=ALU.is_lt), vsem)
            em.wait("vector", vsem)
            em.op("vector", lambda v: v.scalar_tensor_tensor(
                out=_k1[:], in0=_cb[:], scalar=1.0, in1=_ki[:],
                op0=ALU.mult, op1=ALU.add), vsem)
            em.wait("vector", vsem)
            em.op("vector", lambda v: v.scalar_tensor_tensor(
                out=dst_f32[:], in0=_db[:], scalar=-1.0, in1=_k1[:],
                op0=ALU.mult, op1=ALU.add), vsem)

        def emit_round(vsem, src, dst_f32):
            emit_round_t(vsem, src, dst_f32,
                         dict(ki32=ki32, eb=eb, cb=cb, db=db, k1b=k1b))

        def emit_round2(vsem, src, out_f16, T, extra_waits=()):
            _ki, _eb, _db, _k1 = T["ki32"], T["eb"], T["db"], T["k1b"]
            em.wait("vector", vsem)
            em.op("vector", lambda v: v.tensor_copy(_ki[:], src[:]), vsem)
            em.wait("vector", vsem)
            em.op("vector", lambda v: v.scalar_tensor_tensor(
                out=_eb[:], in0=src[:], scalar=1.0, in1=_ki[:],
                op0=ALU.mult, op1=ALU.subtract), vsem)
            em.wait("vector", vsem)
            em.op("vector", lambda v: v.tensor_single_scalar(
                out=_db[:], in_=_eb[:], scalar=-0.5, op=ALU.is_lt), vsem)
            em.op("vector", lambda v: v.scalar_tensor_tensor(
                out=_k1[:], in0=_eb[:], scalar=0.5, in1=_ki[:],
                op0=ALU.is_gt, op1=ALU.add), vsem)
            em.wait("vector", vsem)
            for w_ in extra_waits:
                em.wait("vector", w_)
            em.op("vector", lambda v: v.scalar_tensor_tensor(
                out=out_f16[:], in0=_db[:], scalar=-1.0, in1=_k1[:],
                op0=ALU.mult, op1=ALU.add), vsem)

        def cross_part_reduce(src_col, dst_ap, op, psum_bank):
            em.wait_all("tensor")
            em.wait_all("vector")
            em.op("vector", lambda v: v.tensor_copy(scr[:, 4:5], src_col),
                  "vP")
            em.wait("tensor", "vP")
            em.op("tensor", lambda t: t.transpose(
                psum_bank[0:1, 0:P], scr[:, 4:5], ident32[:]), "tP")
            em.wait("vector", "tP")
            em.op("vector", lambda v: v.tensor_reduce(
                out=dst_ap, in_=psum_bank[0:1, 0:P], axis=AX.X, op=op), "vP")

        def bcast_col(src11, dst_col, psum_bank):
            em.wait_all("tensor")
            em.wait_all("vector")
            em.op("tensor", lambda t: t.matmul(
                psum_bank[:, 0:1], ones1[:], src11, start=True, stop=True),
                "tP")
            em.wait("vector", "tP")
            em.op("vector", lambda v: v.tensor_copy(dst_col, psum_bank[:, 0:1]),
                  "vP")

        def recip_refined(val_col, out_col, iters=1):
            em.wait("vector", "vP")
            em.op("vector", lambda v: v.reciprocal(tl3[:], val_col), "vP")
            for _ in range(iters):
                em.wait("vector", "vP")
                em.op("vector", lambda v: v.tensor_tensor(
                    out=tl4[:], in0=val_col, in1=tl3[:], op=ALU.mult), "vP")
                em.wait("vector", "vP")
                em.op("vector", lambda v: v.tensor_scalar(
                    out=tl4[:], in0=tl4[:], scalar1=-1.0, scalar2=2.0,
                    op0=ALU.mult, op1=ALU.add), "vP")
                em.wait("vector", "vP")
                em.op("vector", lambda v: v.tensor_tensor(
                    out=tl3[:], in0=tl3[:], in1=tl4[:], op=ALU.mult), "vP")
            em.wait("vector", "vP")
            em.op("vector", lambda v: v.tensor_copy(out_col, tl3[:]), "vP")

        h_vA, h_scA = [], []
        for i in range(NT_CAT):
            buf = xa[i % 2]
            if i >= 2:
                em.wait("sync", "vA", h_vA[i - 2])
                em.wait("sync", "scA", h_scA[i - 2])
            em.dma("sync", lambda sy, b=buf, i=i: sy.dma_start(
                out=b[:], in_=XC[i * P:(i + 1) * P, :]), "dma")
            em.wait("vector", "dma")
            em.op("vector", lambda v, b=buf, i=i: v.tensor_reduce(
                out=hmax[:, i:i + 1], in_=b[:], axis=AX.X, op=ALU.max), "vA")
            em.op("vector", lambda v, b=buf, i=i: v.tensor_reduce(
                out=hmin[:, i:i + 1], in_=b[:], axis=AX.X, op=ALU.min), "vA")
            h_vA.append(em.sem("vA").n)
            em.wait("scalar", "dma")
            em.wait("scalar", "scA")
            em.op("scalar", lambda sc, b=buf, i=i: sc.activation(
                scr[:], b[:], AF.Identity, bias=0.0, scale=1.0,
                accum_out=nmean[:, i:i + 1]), "scA")
            em.op("scalar", lambda sc, b=buf, i=i: sc.activation(
                scr2[:], b[:], AF.Square, bias=0.0, scale=1.0,
                accum_out=vssum[:, i:i + 1]), "scA")
            h_scA.append(em.sem("scA").n)
        NC_ = NT_CAT
        em.wait("vector", "vA")
        em.wait("vector", "scA")
        em.op("vector", lambda v: v.tensor_scalar_mul(
            nmean[:, 0:NC_], nmean[:, 0:NC_], -1.0 / D), "vA")
        em.wait("vector", "vA")
        em.op("vector", lambda v: v.tensor_tensor(
            out=s2c[:, 0:NC_], in0=nmean[:, 0:NC_], in1=nmean[:, 0:NC_],
            op=ALU.mult), "vA")
        em.wait("vector", "vA")
        em.op("vector", lambda v: v.scalar_tensor_tensor(
            out=s1c[:, 0:NC_], in0=vssum[:, 0:NC_], scalar=1.0 / D,
            in1=s2c[:, 0:NC_], op0=ALU.mult, op1=ALU.subtract), "vA")
        em.wait("vector", "vA")
        em.op("vector", lambda v: v.tensor_scalar_add(
            s1c[:, 0:NC_], s1c[:, 0:NC_], EPS), "vA")
        em.wait("scalar", "vA")
        em.wait("scalar", "scA")
        em.op("scalar", lambda sc: sc.activation(
            scr[:, 0:NC_], s1c[:, 0:NC_], AF.Sqrt, bias=0.0, scale=1.0),
            "scA")
        em.wait("vector", "scA")
        em.op("vector", lambda v: v.reciprocal(rstd[:, 0:NC_],
                                               scr[:, 0:NC_]), "vA")
        for _ in range(2):
            em.wait("vector", "vA")
            em.op("vector", lambda v: v.tensor_tensor(
                out=s2c[:, 0:NC_], in0=rstd[:, 0:NC_], in1=rstd[:, 0:NC_],
                op=ALU.mult), "vA")
            em.wait("vector", "vA")
            em.op("vector", lambda v: v.tensor_tensor(
                out=s2c[:, 0:NC_], in0=s2c[:, 0:NC_], in1=s1c[:, 0:NC_],
                op=ALU.mult), "vA")
            em.wait("vector", "vA")
            em.op("vector", lambda v: v.tensor_scalar(
                out=s2c[:, 0:NC_], in0=s2c[:, 0:NC_], scalar1=-0.5,
                scalar2=1.5, op0=ALU.mult, op1=ALU.add), "vA")
            em.wait("vector", "vA")
            em.op("vector", lambda v: v.tensor_tensor(
                out=rstd[:, 0:NC_], in0=rstd[:, 0:NC_], in1=s2c[:, 0:NC_],
                op=ALU.mult), "vA")
        em.wait("vector", "vA")
        em.op("vector", lambda v: v.tensor_tensor(
            out=s2c[:, 0:NC_], in0=nmean[:, 0:NC_], in1=nmean[:, 0:NC_],
            op=ALU.mult), "vA")
        em.wait("vector", "vA")
        em.op("vector", lambda v: v.scalar_tensor_tensor(
            out=s2c[:, 0:NC_], in0=s2c[:, 0:NC_], scalar=-float(D),
            in1=vssum[:, 0:NC_], op0=ALU.mult, op1=ALU.add), "vA")
        em.wait("vector", "vA")
        em.op("vector", lambda v: v.tensor_tensor(
            out=scr[:, 0:NC_], in0=rstd[:, 0:NC_], in1=rstd[:, 0:NC_],
            op=ALU.mult), "vA")
        em.wait("vector", "vA")
        em.op("vector", lambda v: v.tensor_tensor(
            out=s2c[:, 0:NC_], in0=s2c[:, 0:NC_], in1=scr[:, 0:NC_],
            op=ALU.mult), "vA")
        em.wait("vector", "vA")
        em.op("vector", lambda v: v.tensor_reduce(
            out=tl2[:], in_=s2c[:, 0:NT_STAT], axis=AX.X, op=ALU.add), "vA")
        em.wait("vector", "vA")
        em.op("vector", lambda v: v.tensor_copy(scr[:, 1:2], tl2[:]), "vA")
        em.wait("tensor", "vA")
        em.wait("tensor", "dma")
        em.op("tensor", lambda t: t.transpose(
            psum[1][0:1, 0:P], scr[:, 1:2], ident32[:]), "tT")
        em.wait("vector", "tT")
        em.op("vector", lambda v: v.tensor_reduce(
            out=glob[0:1, 7:8], in_=psum[1][0:1, 0:P], axis=AX.X,
            op=ALU.add), "vA")
        em.wait("vector", "vA")
        em.op("vector", lambda v: v.tensor_scalar_mul(
            glob[0:1, 7:8], glob[0:1, 7:8], 1.0 / (S * D)), "vA")
        bcast_col(glob[0:1, 7:8], glob[:, 5:6], psum[2])
        em.wait("vector", "vP")
        em.op("vector", lambda v: v.tensor_scalar_add(
            tl1[:], glob[:, 5:6], EPS), "vA")
        em.wait("scalar", "vA")
        em.op("scalar", lambda sc: sc.activation(
            tl2[:], tl1[:], AF.Sqrt, bias=0.0, scale=1.0), "scA")
        em.wait("vector", "scA")
        em.op("vector", lambda v: v.reciprocal(tl3[:], tl2[:]), "vA")
        for _ in range(2):
            em.wait("vector", "vA")
            em.op("vector", lambda v: v.tensor_tensor(
                out=tl4[:], in0=tl3[:], in1=tl3[:], op=ALU.mult), "vA")
            em.wait("vector", "vA")
            em.op("vector", lambda v: v.tensor_tensor(
                out=tl4[:], in0=tl4[:], in1=tl1[:], op=ALU.mult), "vA")
            em.wait("vector", "vA")
            em.op("vector", lambda v: v.tensor_scalar(
                out=tl4[:], in0=tl4[:], scalar1=-0.5, scalar2=1.5,
                op0=ALU.mult, op1=ALU.add), "vA")
            em.wait("vector", "vA")
            em.op("vector", lambda v: v.tensor_tensor(
                out=tl3[:], in0=tl3[:], in1=tl4[:], op=ALU.mult), "vA")
        em.wait("vector", "vA")
        em.op("vector", lambda v: v.tensor_copy(glob[:, 2:3], tl3[:]), "vA")
        em.wait("vector", "vA")
        em.op("vector", lambda v: v.scalar_tensor_tensor(
            out=hmax[:, 0:NC_], in0=hmax[:, 0:NC_], scalar=1.0,
            in1=nmean[:, 0:NC_], op0=ALU.mult, op1=ALU.add), "vA")
        em.op("vector", lambda v: v.scalar_tensor_tensor(
            out=hmin[:, 0:NC_], in0=hmin[:, 0:NC_], scalar=-1.0,
            in1=nmean[:, 0:NC_], op0=ALU.mult, op1=ALU.subtract), "vA")
        em.wait("vector", "vA")
        em.op("vector", lambda v: v.tensor_tensor(
            out=hmax[:, 0:NC_], in0=hmax[:, 0:NC_], in1=hmin[:, 0:NC_],
            op=ALU.max), "vA")
        em.wait("vector", "vA")
        em.op("vector", lambda v: v.tensor_tensor(
            out=hmax[:, 0:NC_], in0=hmax[:, 0:NC_], in1=rstd[:, 0:NC_],
            op=ALU.mult), "vA")
        em.wait("vector", "vA")
        em.op("vector", lambda v: v.tensor_scalar_mul(
            hmax[:, 0:NC_], hmax[:, 0:NC_], glob[:, 2:3]), "vA")
        em.wait("vector", "vA")
        em.op("vector", lambda v: v.tensor_scalar_max(
            hmax[:, 0:NC_], hmax[:, 0:NC_], 1e-5), "vA")
        em.wait("vector", "vA")
        em.op("vector", lambda v: v.tensor_scalar_mul(
            invsx[:, 0:NC_], hmax[:, 0:NC_], 1.0 / 127.0), "vA")
        em.op("vector", lambda v: v.reciprocal(sxc[:, 0:NC_],
                                               hmax[:, 0:NC_]), "vA")
        em.wait("vector", "vA")
        em.op("vector", lambda v: v.tensor_tensor(
            out=scr[:, 0:NC_], in0=hmax[:, 0:NC_], in1=sxc[:, 0:NC_],
            op=ALU.mult), "vA")
        em.wait("vector", "vA")
        em.op("vector", lambda v: v.tensor_scalar(
            out=scr[:, 0:NC_], in0=scr[:, 0:NC_], scalar1=-1.0, scalar2=2.0,
            op0=ALU.mult, op1=ALU.add), "vA")
        em.wait("vector", "vA")
        em.op("vector", lambda v: v.tensor_tensor(
            out=sxc[:, 0:NC_], in0=sxc[:, 0:NC_], in1=scr[:, 0:NC_],
            op=ALU.mult), "vA")
        em.wait("vector", "vA")
        em.op("vector", lambda v: v.tensor_scalar_mul(
            sxc[:, 0:NC_], sxc[:, 0:NC_], 127.0), "vA")
        em.wait("vector", "vA")
        em.op("vector", lambda v: v.tensor_scalar_mul(
            s1c[:, 0:NC_], sxc[:, 0:NC_], glob[:, 2:3]), "vA")
        em.wait("vector", "vA")
        em.op("vector", lambda v: v.tensor_tensor(
            out=s1c[:, 0:NC_], in0=s1c[:, 0:NC_], in1=rstd[:, 0:NC_],
            op=ALU.mult), "vA")

        em.wait("sync", "dma")
        em.wait("sync", "vA")
        em.wait("sync", "scA")
        em.wait("vector", "vA")
        em.wait("vector", "vP")
        h_vB = []
        T_front = dict(ki32=ki32, eb=eb, db=db, k1b=k1b)
        for i in range(NT_CAT):
            buf = xa[i % 2]
            if i >= 2:
                em.wait("sync", "vB", h_vB[i - 2])
            em.dma("sync", lambda sy, b=buf, i=i: sy.dma_start(
                out=b[:], in_=XC[i * P:(i + 1) * P, :]), "dmb")
            em.wait("vector", "dmb")
            em.op("vector", lambda v, b=buf, i=i: v.tensor_scalar(
                out=yb[:], in0=b[:], scalar1=nmean[:, i:i + 1],
                scalar2=s1c[:, i:i + 1], op0=ALU.add, op1=ALU.mult), "vB")
            h_vB.append(em.sem("vB").n)
            emit_round2("vB", yb, kf16, T_front,
                        extra_waits=("scB", "tT"))
            em.wait("scalar", "vB")
            em.wait("scalar", "scB")
            em.op("scalar", lambda sc, i=i: sc.activation(
                scr2[:], kf16[:], AF.Square, bias=0.0, scale=1.0,
                accum_out=ksq[:, i:i + 1]), "scB")
            em.wait("tensor", "vB")
            em.wait("tensor", "scB")
            if i == 0:
                em.wait("tensor", "vA")
                em.wait("tensor", "vP")
            pts = [psum[c % 4][:, (c // 4) * 64:(c // 4) * 64 + 64]
                   .bitcast(F16) for c in range(DC)]
            for c in range(DC):
                em.op("tensor", lambda t, c=c: t.transpose(
                    pts[c], kf16[:, c * P:(c + 1) * P], ident16[:]), "tT")
            em.wait("scalar", "tT")
            for c in range(DC):
                em.op("scalar", lambda sc, c=c, i=i: sc.activation(
                    kT[:, c * TCAT + i * P: c * TCAT + (i + 1) * P], pts[c],
                    AF.Identity, bias=0.0, scale=1.0), "scB")
        if DEBUG:
            em.wait("sync", "vB")
            em.wait("sync", "scB")
            em.dma("sync", lambda sy: sy.dma_start(
                out=dbg["KT"], in_=kT[:]), "dst")

        def ternarize_t(WT, sw_glob_col, tern, wbuf, yb, T):
            em.wait("vector", "tEnd")
            for c in range(DC):
                wb = wbuf[c % 2]
                em.wait("sync", "vP")
                em.wait("gpsimd", "vP")
                em.dma("sync", lambda sy, wb=wb, c=c: sy.dma_start(
                    out=wb[:, 0:512], in_=WT[c * P:(c + 1) * P, 0:512]),
                    "dmw")
                em.dma("gpsimd", lambda g, wb=wb, c=c: g.dma_start(
                    out=wb[:, 512:D], in_=WT[c * P:(c + 1) * P, 512:D]),
                    "dmwg")
                em.wait("vector", "dmw")
                em.wait("vector", "dmwg")
                em.op("vector", lambda v, wb=wb: v.tensor_reduce(
                    out=tl1[:], in_=wb[:], axis=AX.X, op=ALU.add,
                    apply_absolute_value=True), "vP")
                em.wait("vector", "vP")
                if c == 0:
                    em.op("vector", lambda v: v.tensor_copy(wacc[:], tl1[:]),
                          "vP")
                else:
                    em.op("vector", lambda v: v.tensor_tensor(
                        out=wacc[:], in0=wacc[:], in1=tl1[:], op=ALU.add),
                        "vP")
            cross_part_reduce(wacc[:], glob[0:1, 8:9], ALU.add, psum[6])
            em.wait("vector", "vP")
            em.op("vector", lambda v: v.tensor_scalar(
                out=glob[0:1, 8:9], in0=glob[0:1, 8:9],
                scalar1=1.0 / (D * D), scalar2=1e-5, op0=ALU.mult,
                op1=ALU.max), "vP")
            bcast_col(glob[0:1, 8:9], glob[:, sw_glob_col:sw_glob_col + 1],
                      psum[6])
            em.wait("vector", "vP")
            em.op("vector", lambda v: v.tensor_scalar_mul(
                glob[:, 13:14], glob[:, sw_glob_col:sw_glob_col + 1], 0.5),
                "vP")
            em.op("vector", lambda v: v.tensor_scalar_mul(
                glob[:, 14:15], glob[:, sw_glob_col:sw_glob_col + 1], -0.5),
                "vP")
            em.wait("vector", "vP")
            for c in range(DC):
                wb = wbuf[c % 2]
                em.wait("sync", "vP")
                em.wait("gpsimd", "vP")
                em.dma("sync", lambda sy, wb=wb, c=c: sy.dma_start(
                    out=wb[:, 0:512], in_=WT[c * P:(c + 1) * P, 0:512]),
                    "dmw")
                em.dma("gpsimd", lambda g, wb=wb, c=c: g.dma_start(
                    out=wb[:, 512:D], in_=WT[c * P:(c + 1) * P, 512:D]),
                    "dmwg")
                em.wait("vector", "dmw")
                em.wait("vector", "dmwg")
                em.op("vector", lambda v, wb=wb: v.tensor_scalar(
                    out=T["cb"][:], in0=wb[:], scalar1=glob[:, 13:14],
                    scalar2=1.0, op0=ALU.is_gt, op1=ALU.mult), "vP")
                em.op("vector", lambda v, wb=wb: v.tensor_scalar(
                    out=T["db"][:], in0=wb[:], scalar1=glob[:, 14:15],
                    scalar2=1.0, op0=ALU.is_lt, op1=ALU.mult), "vP")
                em.wait("vector", "vP")
                em.op("vector", lambda v, c=c, tern=tern:
                      v.scalar_tensor_tensor(
                          out=tern[:, c * D:(c + 1) * D], in0=T["cb"][:],
                          scalar=1.0, in1=T["db"][:], op0=ALU.mult,
                          op1=ALU.subtract), "vP")

        if BUILD_PHASES >= 2:
            NKT = S // 512
            QT_W = [512, 512, 512, 512, 128]
            QT_OFF = [0, 512, 1024, 1536, 4096]

            T_fp = dict(ki32=ki32, eb=eb, cb=cb, db=db, k1b=k1b)
            ternarize_t(WVT, 6, tern, wbuf, yb, T_fp)
            em.wait("vector", "vP")
            for i in range(NT_STAT):
                em.op("vector", lambda v, i=i: v.tensor_tensor(
                    out=vscl[:, i:i + 1], in0=invsx[:, i:i + 1],
                    in1=glob[:, 6:7], op=ALU.mult), "vP")
            for b in range(2):
                for kc in range(4):
                    em.op("vector", lambda v, b=b, kc=kc: v.memset(
                        vstg[b][:, kc * 130 + 64:kc * 130 + 65], 1.0), "vP")
                    em.op("vector", lambda v, b=b, kc=kc: v.memset(
                        vstg[b][:, kc * 130 + 129:kc * 130 + 130], 1.0), "vP")
            em.wait("tensor", "vB")
            em.wait("tensor", "scB")
            h_vV_w = {0: None, 1: None}
            for oc in range(DC):
                em.wait("tensor", "vP")
                tv_mm = {}
                for w in range(2):
                    if h_vV_w[w] is not None:
                        em.wait("tensor", "vV", h_vV_w[w])
                    for c in range(DC):
                        for tt in range(w * 4, w * 4 + 4):
                            em.op("tensor", lambda t, oc=oc, c=c, tt=tt:
                                  t.matmul(
                                      psum[tt][:],
                                      tern[:, c * D + oc * P:
                                           c * D + oc * P + P],
                                      kT[:, c * TCAT + tt * 512:
                                         c * TCAT + tt * 512 + 512],
                                      start=(c == 0), stop=(c == DC - 1)),
                                  "tV" if c == DC - 1 else None)
                    tv_mm[w] = em.sem("tV").n
                for w in range(2):
                    tv_need = tv_mm[w]
                    for tt in range(w * 4, w * 4 + 4):
                        vs = vstg[tt % 2]
                        em.wait("vector", "tV", tv_need)
                        em.op("vector", lambda v, tt=tt: v.tensor_copy(
                            pevac[:], psum[tt][:]), "vV")
                        em.wait("tensor", "vV")
                        for kc in range(4):
                            em.op("tensor", lambda t, tt=tt, kc=kc:
                                  t.transpose(
                                      psum[tt][:, kc * P:(kc + 1) * P],
                                      pevac[:, kc * P:(kc + 1) * P],
                                      ident32[:]), "tV")
                        tv_need = em.sem("tV").n
                        em.wait("vector", "tV", tv_need)
                        em.wait("vector", "dsp")
                        for kc in range(4):
                            col = tt * 4 + kc
                            em.wait("vector", "vV")
                            em.op("vector", lambda v, vs=vs, tt=tt, kc=kc,
                                  col=col: v.tensor_scalar_mul(
                                      vs[:, kc * 130 + 0:kc * 130 + 64],
                                      psum[tt][:, kc * P:kc * P + 64],
                                      vscl[:, col:col + 1]), "vV")
                            em.op("vector", lambda v, vs=vs, tt=tt, kc=kc,
                                  col=col: v.tensor_scalar_mul(
                                      vs[:, kc * 130 + 65:kc * 130 + 129],
                                      psum[tt][:, kc * P + 64:(kc + 1) * P],
                                      vscl[:, col:col + 1]), "vV")
                        em.wait("gpsimd", "vV")
                        em.dma("gpsimd", lambda g, vs=vs, oc=oc, tt=tt:
                               g.dma_start(
                                   out=VSP[oc, :, tt * 4 * 130:
                                           (tt + 1) * 4 * 130],
                                   in_=vs[:]), "dsp")
                    h_vV_w[w] = em.sem("vV").n

            em.op("tensor", lambda t: t.nop(), "tEnd")
            ternarize_t(WKT, 5, tern, wbuf, yb, T_fp)
            h_vK_w = {0: None, 1: None}
            for oc in range(DC):
                em.wait("tensor", "vP")
                tk_mm = {}
                for w in range(2):
                    if h_vK_w[w] is not None:
                        em.wait("tensor", "vK", h_vK_w[w])
                    for c in range(DC):
                        for tt in range(w * 4, w * 4 + 4):
                            em.op("tensor", lambda t, oc=oc, c=c, tt=tt:
                                  t.matmul(
                                      psum[tt][:],
                                      tern[:, c * D + oc * P:
                                           c * D + oc * P + P],
                                      kT[:, c * TCAT + tt * 512:
                                         c * TCAT + tt * 512 + 512],
                                      start=(c == 0), stop=(c == DC - 1)),
                                  "tK" if c == DC - 1 else None)
                    tk_mm[w] = em.sem("tK").n
                for w in range(2):
                    for tt in range(w * 4, w * 4 + 4):
                        em.wait("vector", "tK", tk_mm[w])
                        em.wait("vector", "dsk")
                        em.op("vector", lambda v, tt=tt: v.tensor_scalar_mul(
                            pv16a[:], psum[tt][:], 0.25), "vK")
                        em.wait("vector", "vK")
                        em.op("vector", lambda v, tt=tt:
                              v.scalar_tensor_tensor(
                                  out=pv16b[:], in0=psum[tt][:], scalar=0.25,
                                  in1=pv16a[:], op0=ALU.mult,
                                  op1=ALU.subtract), "vK")
                        em.wait("gpsimd", "vK")
                        em.dma("gpsimd", lambda g, oc=oc, tt=tt: g.dma_start(
                            out=KH[oc * P:(oc + 1) * P,
                                   tt * 512:(tt + 1) * 512],
                            in_=pv16a[:]), "dsk")
                        em.dma("gpsimd", lambda g, oc=oc, tt=tt: g.dma_start(
                            out=KL[oc * P:(oc + 1) * P,
                                   tt * 512:(tt + 1) * 512],
                            in_=pv16b[:]), "dsk")
                    h_vK_w[w] = em.sem("vK").n

            em.op("tensor", lambda t: t.nop(), "tEnd")
            ternarize_t(WQT, 4, tern, wbuf, yb, T_fp)
            em.wait("vector", "vP")
            for i in list(range(16)) + [32]:
                em.op("vector", lambda v, i=i: v.tensor_scalar_mul(
                    qscl[:, i:i + 1], invsx[:, i:i + 1], 0.25), "vP")
            em.op("vector", lambda v: v.memset(qnsq[:], 0.0), "vQn")
            h_vQ_w = {0: None, 1: None}
            QW_TTS = {0: (0, 1, 2), 1: (3, 4)}
            for oc in range(DC):
                em.wait("tensor", "vP")
                tq_mm = {}
                for w in range(2):
                    if h_vQ_w[w] is not None:
                        em.wait("tensor", "vQ", h_vQ_w[w])
                    for c in range(DC):
                        for tt in QW_TTS[w]:
                            em.op("tensor", lambda t, oc=oc, c=c, tt=tt:
                                  t.matmul(
                                      psum[tt][:, 0:QT_W[tt]],
                                      tern[:, c * D + oc * P:
                                           c * D + oc * P + P],
                                      kT[:, c * TCAT + QT_OFF[tt]:
                                         c * TCAT + QT_OFF[tt] + QT_W[tt]],
                                      start=(c == 0), stop=(c == DC - 1)),
                                  "tQ" if c == DC - 1 else None)
                    tq_mm[w] = em.sem("tQ").n
                for w in range(2):
                    tq_need = tq_mm[w]
                    for tt in QW_TTS[w]:
                        W = QT_W[tt]
                        nkk = W // P
                        em.wait("vector", "tQ", tq_need)
                        em.op("vector", lambda v, tt=tt, W=W: v.tensor_copy(
                            qstg[:, 0:W], psum[tt][:, 0:W]), "vQ")
                        em.wait("tensor", "vQ")
                        for kk in range(nkk):
                            em.op("tensor", lambda t, tt=tt, kk=kk:
                                  t.transpose(
                                      psum[tt][:, kk * P:(kk + 1) * P],
                                      qstg[:, kk * P:(kk + 1) * P],
                                      ident32[:]), "tQ")
                        tq_need = em.sem("tQ").n
                        em.wait("vector", "tQ", tq_need)
                        for kk in range(nkk):
                            col = (tt * 4 + kk) if tt < 4 else 32
                            qtile = (tt * 4 + kk) if tt < 4 else 16
                            em.op("vector", lambda v, tt=tt, kk=kk, col=col:
                                  v.tensor_scalar_mul(
                                      qstg2[:, kk * P:(kk + 1) * P],
                                      psum[tt][:, kk * P:(kk + 1) * P],
                                      qscl[:, col:col + 1]), "vQ")
                            em.wait("scalar", "vQ")
                            em.op("scalar", lambda sc, kk=kk: sc.activation(
                                scr2[:, 0:P], qstg2[:, kk * P:(kk + 1) * P],
                                AF.Square, bias=0.0, scale=1.0,
                                accum_out=tl2[:]), "scQ")
                            em.wait("vector", "scQ")
                            em.wait("vector", "vQn")
                            em.op("vector", lambda v, qtile=qtile:
                                  v.tensor_tensor(
                                      out=qnsq[:, qtile:qtile + 1],
                                      in0=qnsq[:, qtile:qtile + 1],
                                      in1=tl2[:], op=ALU.add), "vQn")
                        em.wait("tensor", "vQ")
                        for kk in range(nkk):
                            em.op("tensor", lambda t, tt=tt, kk=kk:
                                  t.transpose(
                                      psum[tt][:, kk * P:(kk + 1) * P],
                                      qstg2[:, kk * P:(kk + 1) * P],
                                      ident32[:]), "tQ")
                        tq_need = em.sem("tQ").n
                        em.wait("vector", "tQ", tq_need)
                        em.wait("vector", "dsq")
                        em.wait("vector", "dsk")
                        em.op("vector", lambda v, tt=tt, W=W: v.tensor_copy(
                            pv16a[:, 0:W], psum[tt][:, 0:W]), "vQ")
                        em.wait("vector", "vQ")
                        em.op("vector", lambda v, tt=tt, W=W:
                              v.scalar_tensor_tensor(
                                  out=pv16b[:, 0:W], in0=psum[tt][:, 0:W],
                                  scalar=1.0, in1=pv16a[:, 0:W],
                                  op0=ALU.mult, op1=ALU.subtract), "vQ")
                        tok0 = tt * 512 if tt < 4 else 2048
                        em.wait("gpsimd", "vQ")
                        em.dma("gpsimd", lambda g, oc=oc, tok0=tok0, W=W:
                               g.dma_start(
                                   out=QH[oc * P:(oc + 1) * P,
                                          tok0:tok0 + W],
                                   in_=pv16a[:, 0:W]), "dsq")
                        em.dma("gpsimd", lambda g, oc=oc, tok0=tok0, W=W:
                               g.dma_start(
                                   out=QL[oc * P:(oc + 1) * P,
                                          tok0:tok0 + W],
                                   in_=pv16b[:, 0:W]), "dsq")
                    h_vQ_w[w] = em.sem("vQ").n

            em.op("tensor", lambda t: t.nop(), "tEnd")
            em.wait("vector", "vP")
            em.op("vector", lambda v: v.tensor_tensor(
                out=tl1[:], in0=glob[:, 4:5], in1=glob[:, 5:6], op=ALU.mult),
                "vP")
            em.wait("vector", "vP")
            em.op("vector", lambda v: v.tensor_scalar_mul(
                tl1[:], tl1[:], 2.0), "vP")
            em.wait("vector", "vP")
            for i in range(NT_STAT):
                em.op("vector", lambda v, i=i: v.tensor_tensor(
                    out=kescl[:, i:i + 1], in0=invsx[:, i:i + 1], in1=tl1[:],
                    op=ALU.mult), "vP")
            em.wait("scalar", "vB")
            em.wait("scalar", "scB")
            em.wait("scalar", "scQ")
            em.op("scalar", lambda sc: sc.activation(
                scr2[:, 0:NT_STAT], ksq[:, 0:NT_STAT], AF.Sqrt, bias=0.0,
                scale=1.0), "scP")
            em.wait("vector", "scP")
            em.op("vector", lambda v: v.memset(eb[:, 0:NT_STAT], 0.0), "vP")
            em.wait("vector", "vP")
            for i in range(NT_STAT):
                em.op("vector", lambda v, i=i: v.tensor_tensor(
                    out=eb[:, i:i + 1], in0=scr2[:, i:i + 1],
                    in1=kescl[:, i:i + 1], op=ALU.mult), "vP")
            em.wait("vector", "vP")
            em.op("vector", lambda v: v.tensor_scalar_mul(
                eb[:, 0:NT_STAT], eb[:, 0:NT_STAT], 0.25), "vP")
            em.wait("vector", "vP")
            em.op("vector", lambda v: v.tensor_reduce(
                out=tl1[:], in_=eb[:, 0:NT_STAT], axis=AX.X, op=ALU.max),
                "vP")
            cross_part_reduce(tl1[:], glob[0:1, 9:10], ALU.max, psum[6])
            em.wait("scalar", "vQn")
            em.wait("scalar", "scP")
            em.wait("scalar", "vP")
            em.op("scalar", lambda sc: sc.activation(
                scr2[:, 0:17], qnsq[:, 0:17], AF.Sqrt, bias=0.0,
                scale=1.0), "scP")
            em.wait("vector", "scP")
            em.op("vector", lambda v: v.tensor_reduce(
                out=tl1[:], in_=scr2[:, 0:17], axis=AX.X, op=ALU.max), "vP")
            cross_part_reduce(tl1[:], glob[0:1, 10:11], ALU.max, psum[6])
            em.wait("vector", "vP")
            em.op("vector", lambda v: v.tensor_tensor(
                out=glob[0:1, 11:12], in0=glob[0:1, 9:10],
                in1=glob[0:1, 10:11], op=ALU.mult), "vP")
            em.wait("vector", "vP")
            em.op("vector", lambda v: v.tensor_scalar_mul(
                glob[0:1, 11:12], glob[0:1, 11:12], -1.01), "vP")
            bcast_col(glob[0:1, 11:12], glob[:, 3:4], psum[6])

            if DEBUG:
                em.wait("sync", "vP")
                em.dma("sync", lambda sy: sy.dma_start(
                    out=dbg["ST"], in_=glob[:]), "dst")

        if BUILD_PHASES >= 3:
            em.phase_barrier()
            fp_es.close()
            mid_es = ExitStack()
            me = mid_es.enter_context
            x1x = me(nc.sbuf_tensor("x1x", [P, D], F32))
            oqscl = me(nc.sbuf_tensor("oqscl", [P, 17], F32))
            lorT = me(nc.sbuf_tensor("lorT", [P, DC * 16], F32))
            lorTh = me(nc.sbuf_tensor("lorTh", [P, DC * 16], F16))
            lorTl = me(nc.sbuf_tensor("lorTl", [P, DC * 16], F16))
            ao_es = ExitStack()
            attnout = ao_es.enter_context(
                nc.sbuf_tensor("attnout", [P, 17 * D], F32))
            at_es = ExitStack()
            ae = at_es.enter_context
            qbh = ae(nc.sbuf_tensor("qbh", [P, DC * 512], F16))
            qbl = ae(nc.sbuf_tensor("qbl", [P, DC * 512], F16))
            kbh = [ae(nc.sbuf_tensor(f"kbh{i}", [P, S], F16))
                   for i in range(2)]
            kbl = [ae(nc.sbuf_tensor(f"kbl{i}", [P, S], F16))
                   for i in range(2)]
            vbuf = [ae(nc.sbuf_tensor(f"vbuf{i}", [P, 32 * 130], F32))
                    for i in range(2)]
            ea32 = [ae(nc.sbuf_tensor(f"ea32{i}", [P, 512], F32))
                    for i in range(2)]
            eb32 = [ae(nc.sbuf_tensor(f"eb32{i}", [P, 512], F32))
                    for i in range(2)]
            ovst = [ae(nc.sbuf_tensor(f"ovst{i}", [P, 512], F32))
                    for i in range(2)]

            QT5_W = [512, 512, 512, 512, 128]
            QT5_T0 = [0, 4, 8, 12, 16]
            h_hp_end = []

            def emit_scores(kc, hp, W, kb, lb):
                pa = psum[2 + (kc % 2)]
                pb = psum[4 + (kc % 2)]
                kcs = slice(kc * P, (kc + 1) * P)
                for (st_, mv, first, last) in (
                        (kb, qbh, True, False), (kb, qbl, False, False),
                        (lb, qbh, False, True)):
                    em.op("tensor", lambda t, st_=st_, mv=mv, kcs=kcs, hp=hp,
                          W=W, pa=pa, first=first, last=last: t.matmul(
                              pa[0:P, 0:W], st_[0:64, kcs],
                              mv[0:64, hp * 512:hp * 512 + W],
                              start=first, stop=last),
                          "tS" if last else None)
                    em.op("tensor", lambda t, st_=st_, mv=mv, kcs=kcs, hp=hp,
                          W=W, pb=pb, first=first, last=last: t.matmul(
                              pb[0:P, 0:W], st_[64:P, kcs],
                              mv[64:P, hp * 512:hp * 512 + W],
                              start=first, stop=last,
                              tile_position=(64, 0)),
                          "tS" if last else None)

            def emit_av(kc, W, vb, bA, bB):
                first = (kc == 0)
                last = (kc == NT_STAT - 1)
                c0 = kc * 130
                ea, eb_ = ea32[kc % 2], eb32[kc % 2]
                em.op("tensor", lambda t, c0=c0, W=W, bA=bA, ea=ea,
                      first=first, last=last: t.matmul(
                          bA[0:65, 0:W], vb[:, c0:c0 + 65], ea[:, 0:W],
                          start=first, stop=last), "tAv")
                em.op("tensor", lambda t, c0=c0, W=W, bB=bB, eb_=eb_,
                      first=first, last=last: t.matmul(
                          bB[0:65, 0:W], vb[:, c0 + 65:c0 + 130],
                          eb_[:, 0:W], start=first, stop=last), "tAv")

            for qt in range(5):
                W = QT5_W[qt]
                off = qt * 512 if qt < 4 else 2048
                if h_hp_end:
                    em.wait("sync", "tAv", h_hp_end[-1])
                for c in range(DC):
                    em.dma("sync", lambda sy, c=c, off=off, W=W: sy.dma_start(
                        out=qbh[:, c * 512:c * 512 + W],
                        in_=QH[c * P:(c + 1) * P, off:off + W]), "dqb")
                    em.dma("sync", lambda sy, c=c, off=off, W=W: sy.dma_start(
                        out=qbl[:, c * 512:c * 512 + W],
                        in_=QL[c * P:(c + 1) * P, off:off + W]), "dqb")
                for hp in range(DC):
                    kb, lb, vb = kbh[hp % 2], kbl[hp % 2], vbuf[hp % 2]
                    bA = psum[0] if hp % 2 == 0 else psum[6]
                    bB = psum[1] if hp % 2 == 0 else psum[7]
                    if len(h_hp_end) >= 2:
                        em.wait("sync", "tAv", h_hp_end[-2])
                    em.dma("sync", lambda sy, kb=kb, hp=hp: sy.dma_start(
                        out=kb[:], in_=KH[hp * P:(hp + 1) * P, :]), "dkb")
                    em.dma("sync", lambda sy, lb=lb, hp=hp: sy.dma_start(
                        out=lb[:], in_=KL[hp * P:(hp + 1) * P, :]), "dkb")
                    em.dma("sync", lambda sy, vb=vb, hp=hp: sy.dma_start(
                        out=vb[:], in_=VSP[hp, :, :]), "dkb")
                    em.wait("tensor", "dkb")
                    em.wait("tensor", "dqb")
                    em.wait("tensor", "vO")
                    em.wait("tensor", "tOv")
                    em.wait("tensor", "scE")
                    h_scE, h_tAv = [], []
                    for kc in range(NT_STAT):
                        if kc >= 2:
                            em.wait("tensor", "scE", h_scE[kc - 2])
                        emit_scores(kc, hp, W, kb, lb)
                        tS_after = em.sem("tS").n
                        if kc >= 1:
                            em.wait("tensor", "scE", h_scE[kc - 1])
                            emit_av(kc - 1, W, vb, bA, bB)
                            h_tAv.append(em.sem("tAv").n)
                        pa = psum[2 + (kc % 2)]
                        pb = psum[4 + (kc % 2)]
                        ea, eb_ = ea32[kc % 2], eb32[kc % 2]
                        em.wait("scalar", "tS", tS_after)
                        if kc == 0:
                            em.wait("scalar", "tAv")
                        elif kc >= 2:
                            em.wait("scalar", "tAv", h_tAv[kc - 2])
                        em.op("scalar", lambda sc, pa=pa, ea=ea, kc=kc, W=W:
                              sc.activation(
                                  ea[:, 0:W], pa[:, 0:W], AF.Exp,
                                  bias=glob[:, 3:4],
                                  scale=kescl[:, kc:kc + 1]), "scE")
                        em.op("scalar", lambda sc, pb=pb, eb_=eb_, kc=kc, W=W:
                              sc.activation(
                                  eb_[:, 0:W], pb[:, 0:W], AF.Exp,
                                  bias=glob[:, 3:4],
                                  scale=kescl[:, kc:kc + 1]), "scE")
                        h_scE.append(em.sem("scE").n)
                    em.wait("tensor", "scE", h_scE[NT_STAT - 1])
                    emit_av(NT_STAT - 1, W, vb, bA, bB)
                    h_hp_end.append(em.sem("tAv").n)
                    nkk = W // P
                    em.wait("vector", "tAv")
                    em.wait("vector", "tOv")
                    for (hx, bk) in ((0, bA), (1, bB)):
                        ov = ovst[hx]
                        em.op("vector", lambda v, ov=ov, bk=bk, W=W:
                              v.tensor_copy(ov[0:65, 0:W],
                                            bk[0:65, 0:W]), "vO")
                    em.wait("tensor", "vO")
                    for (hx, bk) in ((0, bA), (1, bB)):
                        ov = ovst[hx]
                        for kk in range(nkk):
                            em.op("tensor", lambda t, ov=ov, bk=bk, kk=kk:
                                  t.transpose(bk[:, kk * P:kk * P + 65],
                                              ov[0:65, kk * P:(kk + 1) * P],
                                              ident32[0:65, 0:65]), "tOv")
                    nL = 2 * nkk
                    em.wait("vector", "tOv")
                    for (hx, bk) in ((0, bA), (1, bB)):
                        for kk in range(nkk):
                            idx = hx * nkk + kk
                            em.op("vector", lambda v, bk=bk, kk=kk, idx=idx:
                                  v.tensor_copy(scr[:, idx:idx + 1],
                                                bk[:, kk * P + 64:
                                                    kk * P + 65]), "vO")
                    em.wait("vector", "vO")
                    em.op("vector", lambda v, nL=nL: v.reciprocal(
                        scr2[:, 0:nL], scr[:, 0:nL]), "vO")
                    em.wait("vector", "vO")
                    em.op("vector", lambda v, nL=nL: v.tensor_tensor(
                        out=scr[:, 16:16 + nL], in0=scr[:, 0:nL],
                        in1=scr2[:, 0:nL], op=ALU.mult), "vO")
                    em.wait("vector", "vO")
                    em.op("vector", lambda v, nL=nL: v.tensor_scalar(
                        out=scr[:, 16:16 + nL], in0=scr[:, 16:16 + nL],
                        scalar1=-1.0, scalar2=2.0, op0=ALU.mult,
                        op1=ALU.add), "vO")
                    em.wait("vector", "vO")
                    em.op("vector", lambda v, nL=nL: v.tensor_tensor(
                        out=scr2[:, 0:nL], in0=scr2[:, 0:nL],
                        in1=scr[:, 16:16 + nL], op=ALU.mult), "vO")
                    em.wait("vector", "vO")
                    for (hx, bk) in ((0, bA), (1, bB)):
                        for kk in range(nkk):
                            idx = hx * nkk + kk
                            tile = QT5_T0[qt] + kk
                            hcol = tile * D + (hp * 2 + hx) * HD
                            em.op("vector", lambda v, bk=bk, kk=kk, idx=idx,
                                  hcol=hcol: v.tensor_scalar_mul(
                                      attnout[:, hcol:hcol + HD],
                                      bk[:, kk * P:kk * P + HD],
                                      scr2[:, idx:idx + 1]), "vO")

            if DEBUG:
                em.wait("sync", "vO")
                for tile in range(17):
                    em.dma("sync", lambda sy, tile=tile: sy.dma_start(
                        out=dbg["AO"][tile * P:(tile + 1) * P, :],
                        in_=attnout[:, tile * D:(tile + 1) * D]), "dst")

        open_scopes = []
        if BUILD_PHASES >= 4:
            em.phase_barrier()
            at_es.close()
            p4 = ExitStack()
            open_scopes += [ao_es, p4]
            pe4 = p4.enter_context
            koT = pe4(nc.sbuf_tensor("koT", [P, DC * TQ], F16))
            ternO = pe4(nc.sbuf_tensor("ternO", [P, DC * D], F16))
            wbuf4 = [pe4(nc.sbuf_tensor(f"wbuf4{i}", [P, D], F32))
                     for i in range(2)]
            ki32b = pe4(nc.sbuf_tensor("ki32b", [P, D], I32))
            ebb = pe4(nc.sbuf_tensor("ebb", [P, D], F32))
            cbb = pe4(nc.sbuf_tensor("cbb", [P, D], F32))
            dbb = pe4(nc.sbuf_tensor("dbb", [P, D], F32))
            k1bb = pe4(nc.sbuf_tensor("k1bb", [P, D], F32))
            ybb = pe4(nc.sbuf_tensor("ybb", [P, D], F32))
            kfOb = pe4(nc.sbuf_tensor("kfOb", [P, D], F16))
            ostgT = pe4(nc.sbuf_tensor("ostgT", [P, DC * 512], F32))
            o4b = attnout
            oqmx = pe4(nc.sbuf_tensor("oqmx", [P, 17], F32))
            oqmn = pe4(nc.sbuf_tensor("oqmn", [P, 17], F32))
            osx = pe4(nc.sbuf_tensor("osx", [P, 17], F32))
            xst4 = pe4(nc.sbuf_tensor("xst4", [P, 8], F32))
            xss4 = pe4(nc.sbuf_tensor("xss4", [P, 8], F32))
            ccis = pe4(nc.sbuf_tensor("ccis", [1, 8], F32))
            ccos = pe4(nc.sbuf_tensor("ccos", [1, 8], F32))
            T4 = dict(ki32=ki32b, eb=ebb, cb=cbb, db=dbb, k1b=k1bb)

            em.wait("vector", "v4")
            em.wait("scalar", "sc4")
            for i in range(NT_OWN):
                em.op("vector", lambda v, i=i: v.tensor_reduce(
                    out=oqmx[:, i:i + 1], in_=attnout[:, i * D:(i + 1) * D],
                    axis=AX.X, op=ALU.add), "v4")
                if i:
                    em.wait("scalar", "sc4")
                em.op("scalar", lambda sc, i=i: sc.activation(
                    scr2[:], attnout[:, i * D:(i + 1) * D], AF.Square,
                    bias=0.0, scale=1.0, accum_out=oqmn[:, i:i + 1]), "sc4")
            em.wait("vector", "v4")
            em.wait("vector", "sc4")
            em.op("vector", lambda v: v.tensor_reduce(
                out=tl1[:], in_=oqmx[:, 0:NT_OWN], axis=AX.X, op=ALU.add),
                "v4")
            em.op("vector", lambda v: v.tensor_reduce(
                out=tl2[:], in_=oqmn[:, 0:NT_OWN], axis=AX.X, op=ALU.add),
                "v4")
            cross_part_reduce(tl1[:], ccis[0:1, 0:1], ALU.add, psum[6])
            cross_part_reduce(tl2[:], ccis[0:1, 1:2], ALU.add, psum[6])
            em.wait("vector", "vP")
            em.op("vector", lambda v: v.memset(ccis[0:1, 2:8], 0.0), "v4")
            em.wait("gpsimd", "v4")
            em.wait("gpsimd", "vP")
            em.op("gpsimd", lambda g: g.dma_start(
                out=CCI.ap(), in_=ccis[:]), "dcc", 16)
            em.wait("gpsimd", "dcc")
            em.op("gpsimd", lambda g: g.collective_compute(
                "AllReduce", ALU.add,
                replica_groups=[[0, 1], [2, 3], [4, 5], [6, 7]],
                ins=[CCI.ap().opt()], outs=[CCO.ap().opt()]), "gcc", 1)
            em.wait("gpsimd", "gcc")
            em.op("gpsimd", lambda g: g.dma_start(
                out=ccos[:], in_=CCO.ap()), "dcc", 16)
            em.wait("vector", "dcc")
            em.op("vector", lambda v: v.tensor_scalar_mul(
                ccos[0:1, 2:4], ccos[0:1, 0:2], 1.0 / (S * D)), "v4")
            em.wait("vector", "v4")
            bcast_col(ccos[0:1, 2:3], glob[:, 9:10], psum[6])
            bcast_col(ccos[0:1, 3:4], glob[:, 10:11], psum[6])
            em.wait("vector", "vP")
            em.op("vector", lambda v: v.tensor_tensor(
                out=tl1[:], in0=glob[:, 9:10], in1=glob[:, 9:10],
                op=ALU.mult), "v4")
            em.wait("vector", "v4")
            em.op("vector", lambda v: v.tensor_tensor(
                out=tl2[:], in0=glob[:, 10:11], in1=tl1[:], op=ALU.subtract),
                "v4")
            em.wait("vector", "v4")
            em.op("vector", lambda v: v.tensor_scalar_add(
                tl1[:], tl2[:], EPS), "v4")
            em.wait("scalar", "v4")
            em.op("scalar", lambda sc: sc.activation(
                tl2[:], tl1[:], AF.Sqrt, bias=0.0, scale=1.0), "sc4")
            em.wait("vector", "sc4")
            em.op("vector", lambda v: v.reciprocal(tl3[:], tl2[:]), "v4")
            for _ in range(2):
                em.wait("vector", "v4")
                em.op("vector", lambda v: v.tensor_tensor(
                    out=tl4[:], in0=tl3[:], in1=tl3[:], op=ALU.mult), "v4")
                em.wait("vector", "v4")
                em.op("vector", lambda v: v.tensor_tensor(
                    out=tl4[:], in0=tl4[:], in1=tl1[:], op=ALU.mult), "v4")
                em.wait("vector", "v4")
                em.op("vector", lambda v: v.tensor_scalar(
                    out=tl4[:], in0=tl4[:], scalar1=-0.5, scalar2=1.5,
                    op0=ALU.mult, op1=ALU.add), "v4")
                em.wait("vector", "v4")
                em.op("vector", lambda v: v.tensor_tensor(
                    out=tl3[:], in0=tl3[:], in1=tl4[:], op=ALU.mult), "v4")
            em.wait("vector", "v4")
            em.op("vector", lambda v: v.tensor_copy(glob[:, 11:12], tl3[:]),
                  "v4")
            em.op("vector", lambda v: v.tensor_scalar_mul(
                glob[:, 12:13], glob[:, 9:10], -1.0), "v4")

            em.wait("vector", "v4")
            for i in range(17):
                sl = slice(i * D, (i + 1) * D)
                em.op("vector", lambda v, sl=sl, i=i: v.tensor_reduce(
                    out=oqmx[:, i:i + 1], in_=attnout[:, sl], axis=AX.X,
                    op=ALU.max), "v4")
                em.op("vector", lambda v, sl=sl, i=i: v.tensor_reduce(
                    out=oqmn[:, i:i + 1], in_=attnout[:, sl], axis=AX.X,
                    op=ALU.min), "v4")
            NCo = 17
            em.wait("vector", "v4")
            em.op("vector", lambda v: v.tensor_scalar(
                out=oqmx[:, 0:NCo], in0=oqmx[:, 0:NCo],
                scalar1=glob[:, 12:13], scalar2=glob[:, 11:12],
                op0=ALU.add, op1=ALU.mult), "v4")
            em.op("vector", lambda v: v.tensor_scalar(
                out=oqmn[:, 0:NCo], in0=oqmn[:, 0:NCo], scalar1=-1.0,
                scalar2=glob[:, 9:10], op0=ALU.mult, op1=ALU.add), "v4")
            em.wait("vector", "v4")
            em.op("vector", lambda v: v.tensor_scalar_mul(
                oqmn[:, 0:NCo], oqmn[:, 0:NCo], glob[:, 11:12]), "v4")
            em.wait("vector", "v4")
            em.op("vector", lambda v: v.tensor_tensor(
                out=oqmx[:, 0:NCo], in0=oqmx[:, 0:NCo], in1=oqmn[:, 0:NCo],
                op=ALU.max), "v4")
            em.wait("vector", "v4")
            em.op("vector", lambda v: v.tensor_scalar_max(
                oqmx[:, 0:NCo], oqmx[:, 0:NCo], 1e-5), "v4")
            em.wait("vector", "v4")
            em.op("vector", lambda v: v.tensor_scalar_mul(
                oqscl[:, 0:NCo], oqmx[:, 0:NCo], 1.0 / 127.0), "v4")
            em.op("vector", lambda v: v.reciprocal(osx[:, 0:NCo],
                                                   oqmx[:, 0:NCo]), "v4")
            em.wait("vector", "v4")
            em.op("vector", lambda v: v.tensor_tensor(
                out=oqmn[:, 0:NCo], in0=oqmx[:, 0:NCo], in1=osx[:, 0:NCo],
                op=ALU.mult), "v4")
            em.wait("vector", "v4")
            em.op("vector", lambda v: v.tensor_scalar(
                out=oqmn[:, 0:NCo], in0=oqmn[:, 0:NCo], scalar1=-1.0,
                scalar2=2.0, op0=ALU.mult, op1=ALU.add), "v4")
            em.wait("vector", "v4")
            em.op("vector", lambda v: v.tensor_tensor(
                out=osx[:, 0:NCo], in0=osx[:, 0:NCo], in1=oqmn[:, 0:NCo],
                op=ALU.mult), "v4")
            em.wait("vector", "v4")
            em.op("vector", lambda v: v.tensor_scalar_mul(
                osx[:, 0:NCo], osx[:, 0:NCo], 127.0), "v4")
            em.wait("vector", "v4")
            em.op("vector", lambda v: v.tensor_scalar_mul(
                osx[:, 0:NCo], osx[:, 0:NCo], glob[:, 11:12]), "v4")
            for i in range(17):
                sl = slice(i * D, (i + 1) * D)
                em.wait("vector", "v4")
                em.op("vector", lambda v, sl=sl, i=i: v.tensor_scalar(
                    out=ybb[:], in0=attnout[:, sl], scalar1=glob[:, 12:13],
                    scalar2=osx[:, i:i + 1], op0=ALU.add, op1=ALU.mult),
                    "v4")
                emit_round2("v4", ybb, kfOb, T4, extra_waits=("t4T",))
                em.wait("tensor", "v4")
                em.wait("tensor", "sc4")
                pts4 = [psum[c % 4][:, (c // 4) * 64:(c // 4) * 64 + 64]
                        .bitcast(F16) for c in range(DC)]
                for c in range(DC):
                    em.op("tensor", lambda t, c=c, pts4=pts4: t.transpose(
                        pts4[c], kfOb[:, c * P:(c + 1) * P], ident16[:]),
                        "t4T")
                em.wait("scalar", "t4T")
                for c in range(DC):
                    em.op("scalar", lambda sc, c=c, i=i, pts4=pts4:
                          sc.activation(
                              koT[:, c * TQ + i * P:c * TQ + (i + 1) * P],
                              pts4[c], AF.Identity, bias=0.0, scale=1.0),
                          "sc4")

            ternarize_t(WOT, 7, ternO, wbuf4, ybb, T4)
            em.wait("vector", "vP")
            em.wait("vector", "v4")
            for i in range(17):
                em.op("vector", lambda v, i=i: v.tensor_tensor(
                    out=oqscl[:, i:i + 1], in0=oqscl[:, i:i + 1],
                    in1=glob[:, 7:8], op=ALU.mult), "v4")
            for tt in range(5):
                W4 = QT5_W[tt]
                off4 = QT5_T0[tt] * P
                nkk = W4 // P
                em.wait("tensor", "vP")
                em.wait("tensor", "v4")
                em.wait("tensor", "sc4")
                for oc in range(DC):
                    bank = psum[oc % 2]
                    em.wait("tensor", "vOs")
                    for c in range(DC):
                        em.op("tensor", lambda t, oc=oc, c=c, W4=W4, off4=off4,
                              bank=bank: t.matmul(
                                  bank[:, 0:W4],
                                  ternO[:, c * D + oc * P:c * D + oc * P + P],
                                  koT[:, c * TQ + off4:c * TQ + off4 + W4],
                                  start=(c == 0), stop=(c == DC - 1)),
                              "t4M" if c == DC - 1 else None)
                    em.wait("vector", "t4M")
                    em.op("vector", lambda v, oc=oc, W4=W4, bank=bank:
                          v.tensor_copy(ostgT[:, oc * 512:oc * 512 + W4],
                                        bank[:, 0:W4]), "vOs")
                for kk in range(nkk):
                    i = QT5_T0[tt] + kk
                    em.wait("tensor", "vOs")
                    em.wait("tensor", "vX")
                    for c in range(DC):
                        pb4 = psum[4 + (c % 2)][:, (c // 2 % 2) * P:
                                                (c // 2 % 2) * P + P]
                        em.wait("tensor", "vX")
                        em.op("tensor", lambda t, c=c, kk=kk, pb4=pb4:
                              t.transpose(
                                  pb4, ostgT[:, c * 512 + kk * P:
                                             c * 512 + (kk + 1) * P],
                                  ident32[:]), "t4X")
                        em.wait("vector", "t4X")
                        em.op("vector", lambda v, c=c, i=i, kk=kk, pb4=pb4:
                              v.tensor_scalar_mul(
                                  o4b[:, kk * D + c * P:kk * D + (c + 1) * P],
                                  pb4, oqscl[:, i:i + 1]), "vX")
                    okk = slice(kk * D, (kk + 1) * D)
                    em.wait("vector", "vX")
                    em.op("vector", lambda v, okk=okk, kk=kk: v.tensor_reduce(
                        out=xst4[:, kk:kk + 1], in_=o4b[:, okk], axis=AX.X,
                        op=ALU.add, negate=True), "vX")
                    em.wait("vector", "vX")
                    em.op("vector", lambda v, kk=kk: v.tensor_scalar_mul(
                        xst4[:, kk:kk + 1], xst4[:, kk:kk + 1], 1.0 / D),
                        "vX")
                    em.wait("scalar", "vX")
                    em.wait("scalar", "sc4")
                    em.op("scalar", lambda sc, okk=okk, kk=kk: sc.activation(
                        scr2[:], o4b[:, okk], AF.Square,
                        bias=xst4[:, kk:kk + 1], scale=1.0,
                        accum_out=xss4[:, kk:kk + 1]), "sc4")
                em.wait("vector", "sc4")
                em.op("vector", lambda v, nkk=nkk: v.tensor_scalar(
                    out=xss4[:, 0:nkk], in0=xss4[:, 0:nkk],
                    scalar1=1.0 / D, scalar2=EPS, op0=ALU.mult,
                    op1=ALU.add), "vX")
                em.wait("scalar", "vX")
                em.op("scalar", lambda sc, nkk=nkk: sc.activation(
                    xss4[:, 4:4 + nkk], xss4[:, 0:nkk], AF.Sqrt, bias=0.0,
                    scale=1.0), "sc4")
                em.wait("vector", "sc4")
                em.op("vector", lambda v, nkk=nkk: v.reciprocal(
                    xst4[:, 4:4 + nkk], xss4[:, 4:4 + nkk]), "vX")
                for _ in range(2):
                    em.wait("vector", "vX")
                    em.op("vector", lambda v, nkk=nkk: v.tensor_tensor(
                        out=oqmn[:, 0:nkk], in0=xst4[:, 4:4 + nkk],
                        in1=xst4[:, 4:4 + nkk], op=ALU.mult), "vX")
                    em.wait("vector", "vX")
                    em.op("vector", lambda v, nkk=nkk: v.tensor_tensor(
                        out=oqmn[:, 0:nkk], in0=oqmn[:, 0:nkk],
                        in1=xss4[:, 0:nkk], op=ALU.mult), "vX")
                    em.wait("vector", "vX")
                    em.op("vector", lambda v, nkk=nkk: v.tensor_scalar(
                        out=oqmn[:, 0:nkk], in0=oqmn[:, 0:nkk],
                        scalar1=-0.5, scalar2=1.5, op0=ALU.mult,
                        op1=ALU.add), "vX")
                    em.wait("vector", "vX")
                    em.op("vector", lambda v, nkk=nkk: v.tensor_tensor(
                        out=xst4[:, 4:4 + nkk], in0=xst4[:, 4:4 + nkk],
                        in1=oqmn[:, 0:nkk], op=ALU.mult), "vX")
                for kk in range(nkk):
                    i = QT5_T0[tt] + kk
                    okk = slice(kk * D, (kk + 1) * D)
                    xrow = i * P if i < 16 else 4096
                    xb4 = xa[kk % 2]
                    em.wait("sync", "vX")
                    em.dma("sync", lambda sy, xrow=xrow, xb4=xb4:
                           sy.dma_start(
                               out=xb4[:], in_=XC[xrow:xrow + P, :]), "dx4")
                    em.wait("vector", "dx4")
                    em.wait("vector", "vX")
                    em.op("vector", lambda v, okk=okk, kk=kk: v.tensor_scalar(
                        out=o4b[:, okk], in0=o4b[:, okk],
                        scalar1=xst4[:, kk:kk + 1],
                        scalar2=xst4[:, 4 + kk:5 + kk],
                        op0=ALU.add, op1=ALU.mult), "vX")
                    em.wait("vector", "vX")
                    em.wait("vector", "dx1")
                    if DEBUG:
                        em.wait("vector", "dst")
                    dst = x1x if i == 16 else ybb
                    em.op("vector", lambda v, dst=dst, okk=okk, xb4=xb4:
                          v.tensor_tensor(
                              out=dst[:], in0=xb4[:], in1=o4b[:, okk],
                              op=ALU.add), "vX")
                    if i < 16:
                        em.wait("sync", "vX")
                        em.dma("sync", lambda sy, i=i: sy.dma_start(
                            out=X1SP[i * P:(i + 1) * P, :], in_=ybb[:]),
                            "dx1")
                    if DEBUG:
                        em.wait("sync", "vX")
                        em.dma("sync", lambda sy, i=i, dst=dst: sy.dma_start(
                            out=dbg["X1"][i * P:(i + 1) * P, :], in_=dst[:]),
                            "dst")

        if BUILD_PHASES >= 5:
            em.phase_barrier()
            p4.close()
            ao_es.close()
            open_scopes = []
            p5 = ExitStack()
            pe5 = p5.enter_context
            open_scopes.append(p5)
            lor2h = pe5(nc.sbuf_tensor("lor2h", [1, 4 * D], F16))
            lor2l = pe5(nc.sbuf_tensor("lor2l", [1, 4 * D], F16))
            x1t = pe5(nc.sbuf_tensor("x1t", [P, D], F32))
            h2x = x1t
            h2t32 = scr
            h2Th = pe5(nc.sbuf_tensor("h2Th", [P, DC * 512], F16))
            h2Tl = pe5(nc.sbuf_tensor("h2Tl", [P, DC * 512], F16))
            hTh = pe5(nc.sbuf_tensor("hTh", [P, HFC * 512], F16))
            hTl = pe5(nc.sbuf_tensor("hTl", [P, HFC * 512], F16))
            wfh = [pe5(nc.sbuf_tensor(f"wfh{i}", [P, 8 * P], F16))
                   for i in range(2)]
            wfl = [pe5(nc.sbuf_tensor(f"wfl{i}", [P, 8 * P], F16))
                   for i in range(2)]
            wf32 = xa

            prowh = pe5(nc.sbuf_tensor("prowh", [1, 4 * 512], F16))
            prowl = pe5(nc.sbuf_tensor("prowl", [1, 4 * 512], F16))
            rrow32 = pe5(nc.sbuf_tensor("rrow32", [4, 512], F32))
            rrowf = pe5(nc.sbuf_tensor("rrowf", [1, 2048], F32))
            prowf = rrowf
            rrowh = pe5(nc.sbuf_tensor("rrowh", [1, 4 * 2048], F16))
            rrowl = pe5(nc.sbuf_tensor("rrowl", [1, 4 * 2048], F16))
            outsT = pe5(nc.sbuf_tensor("outsT", [P, DC * 512], F32))
            outf = pe5(nc.sbuf_tensor("outf", [P, D], F32))
            fst5 = pe5(nc.sbuf_tensor("fst5", [P, 8], F32))
            fss5 = pe5(nc.sbuf_tensor("fss5", [P, 8], F32))
            ftm5 = pe5(nc.sbuf_tensor("ftm5", [P, 4], F32))
            hact2 = [outf[:, 0:512], outf[:, 512:1024]]
            sclb = pe5(nc.sbuf_tensor("sclb", [P, 1], F32))
            pv16a5 = pe5(nc.sbuf_tensor("pv16a5", [P, 8 * P], F16))
            pv16b5 = pe5(nc.sbuf_tensor("pv16b5", [P, 8 * P], F16))
            lor16h = pv16a5
            lor16l = pv16b5
            b1t_sb = pe5(nc.sbuf_tensor("b1t_sb", [P, HFC], F32))
            b2t_sb = pe5(nc.sbuf_tensor("b2t_sb", [P, DC], F32))
            em.dma("sync", lambda sy: sy.dma_start(out=b1t_sb[:], in_=B1T),
                   "dl5")
            em.dma("sync", lambda sy: sy.dma_start(out=b2t_sb[:], in_=B2T),
                   "dl5")

            def ln_row(src_buf, dst_buf, vsem, scsem):
                em.wait("vector", vsem)
                em.op("vector", lambda v: v.tensor_reduce(
                    out=tl1[:], in_=src_buf[:], axis=AX.X, op=ALU.add,
                    negate=True), vsem)
                em.wait("vector", vsem)
                em.op("vector", lambda v: v.tensor_scalar_mul(
                    tl1[:], tl1[:], 1.0 / D), vsem)
                em.wait("scalar", vsem)
                em.wait("scalar", scsem)
                em.op("scalar", lambda sc: sc.activation(
                    scr2[:], src_buf[:], AF.Square, bias=tl1[:], scale=1.0,
                    accum_out=tl2[:]), scsem)
                em.wait("vector", scsem)
                em.op("vector", lambda v: v.tensor_scalar(
                    out=tl2[:], in0=tl2[:], scalar1=1.0 / D, scalar2=EPS,
                    op0=ALU.mult, op1=ALU.add), vsem)
                em.wait("scalar", vsem)
                em.op("scalar", lambda sc: sc.activation(
                    tl3[:], tl2[:], AF.Sqrt, bias=0.0, scale=1.0), scsem)
                em.wait("vector", scsem)
                em.op("vector", lambda v: v.reciprocal(tl4[:], tl3[:]), vsem)
                for _ in range(2):
                    em.wait("vector", vsem)
                    em.op("vector", lambda v: v.tensor_tensor(
                        out=tl3[:], in0=tl4[:], in1=tl4[:], op=ALU.mult),
                        vsem)
                    em.wait("vector", vsem)
                    em.op("vector", lambda v: v.tensor_tensor(
                        out=tl3[:], in0=tl3[:], in1=tl2[:], op=ALU.mult),
                        vsem)
                    em.wait("vector", vsem)
                    em.op("vector", lambda v: v.tensor_scalar(
                        out=tl3[:], in0=tl3[:], scalar1=-0.5, scalar2=1.5,
                        op0=ALU.mult, op1=ALU.add), vsem)
                    em.wait("vector", vsem)
                    em.op("vector", lambda v: v.tensor_tensor(
                        out=tl4[:], in0=tl4[:], in1=tl3[:], op=ALU.mult),
                        vsem)
                em.wait("vector", vsem)
                em.op("vector", lambda v: v.tensor_scalar(
                    out=dst_buf[:], in0=src_buf[:], scalar1=tl1[:],
                    scalar2=tl4[:], op0=ALU.add, op1=ALU.mult), vsem)

            ln_row(x1x, h2x, "v5", "sc5")
            em.wait("sync", "v5")
            em.wait("sync", "v5w")
            em.dma("sync", lambda sy: sy.dma_start(
                out=scr[0:16, :], in_=h2x[1:17, :]), "dl5")
            em.wait("tensor", "dl5")
            em.wait("tensor", "v5T")
            for c in range(DC):
                em.op("tensor", lambda t, c=c: t.transpose(
                    psum[6][:, c * 16:(c + 1) * 16],
                    scr[0:16, c * P:(c + 1) * P], ident32[0:16, 0:16]),
                    "t5L")
            em.wait("vector", "t5L")
            em.op("vector", lambda v: v.tensor_copy(lorT[:], psum[6][:, 0:P]),
                  "v5")
            em.wait("vector", "v5")
            em.op("vector", lambda v: v.tensor_copy(lorTh[:], lorT[:]), "v5")
            em.wait("vector", "v5")
            em.op("vector", lambda v: v.scalar_tensor_tensor(
                out=lorTl[:], in0=lorT[:], scalar=1.0, in1=lorTh[:],
                op0=ALU.mult, op1=ALU.subtract), "v5")
            em.wait("vector", "dl5")
            em.wait("vector", "v5w")
            em.wait("vector", "dw5")
            em.op("vector", lambda v: v.tensor_copy(
                lor16h[0:16, :], scr[0:16, :]), "v5")
            em.wait("vector", "v5")
            em.op("vector", lambda v: v.scalar_tensor_tensor(
                out=lor16l[0:16, :], in0=scr[0:16, :], scalar=1.0,
                in1=lor16h[0:16, :], op0=ALU.mult, op1=ALU.subtract), "v5")
            em.wait("sync", "v5")
            em.dma("sync", lambda sy: sy.dma_start(
                out=LRSPH, in_=lor16h[0:16:2, :]), "dl5")
            em.dma("sync", lambda sy: sy.dma_start(
                out=LRSPL, in_=lor16l[0:16:2, :]), "dl5")
            em.wait("sync", "dl5")

            def load_lor2(r0):
                em.wait("sync", "t5M")
                em.wait("sync", "t5W")
                em.wait("sync", "dl5")
                em.dma("sync", lambda sy, r0=r0: sy.dma_start(
                    out=lor2h[:], in_=LRSPH[r0:r0 + 4, :]
                    .rearrange("r d -> (r d)").unsqueeze(0)), "dl5")
                em.dma("sync", lambda sy, r0=r0: sy.dma_start(
                    out=lor2l[:], in_=LRSPL[r0:r0 + 4, :]
                    .rearrange("r d -> (r d)").unsqueeze(0)), "dl5")

            load_lor2(0)
            em.dma("sync", lambda sy: sy.dma_start(
                out=sclb[0:1, :], in_=SCL), "dl5")
            bcast_col(sclb[0:1, :], sclb[:], psum[6])

            em.wait("scalar", "dl5")
            for sg in range(4):
                for kk in range(4):
                    tok = sg * 512 + kk * P
                    em.wait("sync", "v5T")
                    em.wait("sync", "dl5")
                    em.wait("vector", "t5M")
                    em.wait("vector", "t5p")
                    em.wait("vector", "t5h")
                    em.dma("sync", lambda sy, tok=tok: sy.dma_start(
                        out=x1t[:], in_=X1SP[tok:tok + P, :]), "dx5")
                    em.wait("vector", "dx5")
                    ln_row(x1t, h2t32, "v5T", "sc5T")
                    em.wait("tensor", "v5T")
                    em.wait("tensor", "v5h")
                    pts5 = [psum[c % 2][:, (c // 2) * P:(c // 2) * P + P]
                            for c in range(DC)]
                    for c in range(DC):
                        em.op("tensor", lambda t, c=c, pts5=pts5: t.transpose(
                            pts5[c], h2t32[:, c * P:(c + 1) * P],
                            ident32[:]), "t5h")
                    em.wait("vector", "t5h")
                    for c in range(DC):
                        em.op("vector", lambda v, c=c, kk=kk, pts5=pts5:
                              v.tensor_copy(
                                  h2Th[:, c * 512 + kk * P:
                                       c * 512 + (kk + 1) * P],
                                  pts5[c]), "v5h")
                        em.wait("vector", "v5h")
                        em.op("vector", lambda v, c=c, kk=kk, pts5=pts5:
                              v.scalar_tensor_tensor(
                                  out=h2Tl[:, c * 512 + kk * P:
                                           c * 512 + (kk + 1) * P],
                                  in0=pts5[c], scalar=1.0,
                                  in1=h2Th[:, c * 512 + kk * P:
                                           c * 512 + (kk + 1) * P],
                                  op0=ALU.mult, op1=ALU.subtract), "v5h")
                em.wait("tensor", "v5h")
                em.wait("tensor", "v5p")
                for c in range(DC):
                    oh = lorTh[:, c * 16 + 1:c * 16 + 8:2]
                    ol = lorTl[:, c * 16 + 1:c * 16 + 8:2]
                    hh = h2Th[:, c * 512:(c + 1) * 512]
                    hl = h2Tl[:, c * 512:(c + 1) * 512]
                    for (sa, mb, first, last) in (
                            (oh, hh, c == 0, False), (oh, hl, False, False),
                            (ol, hh, False, c == DC - 1)):
                        em.op("tensor", lambda t, sa=sa, mb=mb, first=first,
                              last=last: t.matmul(
                                  psum[4][0:4, 0:512], sa, mb,
                                  start=first, stop=last),
                              "t5p" if last else None)
                em.wait("vector", "t5p")
                em.op("vector", lambda v: v.tensor_scalar_mul(
                    rrow32[0:4, :], psum[4][0:4, 0:512], sclb[0:4, :]), "v5p")
                em.wait("sync", "v5p")
                em.dma("sync", lambda sy: sy.dma_start(
                    out=PRSP, in_=rrow32[0:4, :]), "dp5")
                em.wait("sync", "dp5")
                em.dma("sync", lambda sy: sy.dma_start(
                    out=prowf[:], in_=PRSP.rearrange("r w -> (r w)")
                    .unsqueeze(0)), "dp5")
                em.wait("vector", "dp5")
                em.op("vector", lambda v: v.tensor_copy(prowh[:], prowf[:]),
                      "v5p")
                em.wait("vector", "v5p")
                em.op("vector", lambda v: v.scalar_tensor_tensor(
                    out=prowl[:], in0=prowf[:], scalar=1.0, in1=prowh[:],
                    op0=ALU.mult, op1=ALU.subtract), "v5p")
                load_lor2(0)
                em.wait("tensor", "dl5")
                em.wait("tensor", "v5p")
                em.wait("tensor", "v5h")
                em.wait("scalar", "dout")
                h_dw = []
                h_v5s = []
                h_t5M = []
                h_g = []
                em.wait("sync", "v5u")
                em.wait("sync", "v5s")
                em.wait("gpsimd", "v5u")
                em.wait("gpsimd", "v5s")
                em.wait("tensor", "sc5o")

                def w1_dma(hc):
                    sem = "dw1a" if hc % 2 == 0 else "dw1b"
                    semg = sem + "g"
                    if hc >= 2:
                        em.wait("sync", "v5s", h_v5s[hc - 2])
                        em.wait("gpsimd", "v5s", h_v5s[hc - 2])
                    em.dma("sync", lambda sy, hc=hc: sy.dma_start(
                        out=wf32[hc % 2][:, 0:512],
                        in_=W1B[hc, :, 0:512]), sem)
                    em.dma("gpsimd", lambda g, hc=hc: g.dma_start(
                        out=wf32[hc % 2][:, 512:1024],
                        in_=W1B[hc, :, 512:1024]), semg)
                    h_dw.append((em.sem(sem).n, em.sem(semg).n))

                def emit_rrow(j):

                    em.wait("tensor", "v5g")
                    em.wait("tensor", "v5r")
                    em.wait("tensor", "vP")
                    for cc in range(8):
                        hcol = j * 8 + cc
                        oh = lorTh[:, cc * 16 + 9:cc * 16 + 16:2]
                        ol = lorTl[:, cc * 16 + 9:cc * 16 + 16:2]
                        hh = hTh[:, hcol * 512:(hcol + 1) * 512]
                        hl = hTl[:, hcol * 512:(hcol + 1) * 512]
                        for (sa, mb, first, last) in (
                                (oh, hh, cc == 0, False),
                                (oh, hl, False, False),
                                (ol, hh, False, cc == 7)):
                            em.op("tensor", lambda t, sa=sa, mb=mb,
                                  first=first, last=last: t.matmul(
                                      psum[7][0:4, 0:512], sa, mb,
                                      start=first, stop=last),
                                  "t5r" if last else None)
                    em.wait("vector", "t5r")
                    em.wait("vector", "dp5")
                    em.wait("vector", "dr5")
                    em.op("vector", lambda v: v.tensor_scalar_mul(
                        rrow32[0:4, :], psum[7][0:4, 0:512], sclb[0:4, :]),
                        "v5r")
                    em.wait("gpsimd", "v5r")
                    em.dma("gpsimd", lambda g: g.dma_start(
                        out=RRSP, in_=rrow32[0:4, :]), "dr5")
                    em.wait("gpsimd", "dr5")
                    em.wait("gpsimd", "v5r")
                    em.wait("gpsimd", "v5p")
                    em.dma("gpsimd", lambda g: g.dma_start(
                        out=rrowf[:], in_=RRSP.rearrange("r w -> (r w)")
                        .unsqueeze(0)), "dr5")
                    em.wait("vector", "dr5")
                    em.op("vector", lambda v, j=j: v.tensor_copy(
                        rrowh[0:1, j * 2048:(j + 1) * 2048], rrowf[:]),
                        "v5r")
                    em.wait("vector", "v5r")
                    em.op("vector", lambda v, j=j: v.scalar_tensor_tensor(
                        out=rrowl[0:1, j * 2048:(j + 1) * 2048],
                        in0=rrowf[:], scalar=1.0,
                        in1=rrowh[0:1, j * 2048:(j + 1) * 2048],
                        op0=ALU.mult, op1=ALU.subtract), "v5r")

                def w1_split(hc):
                    _s = "dw1a" if hc % 2 == 0 else "dw1b"
                    em.wait("vector", _s, h_dw[hc][0])
                    em.wait("vector", _s + "g", h_dw[hc][1])
                    if hc >= 2:
                        em.wait("vector", "t5M", h_t5M[hc - 2])
                    em.op("vector", lambda v, hc=hc: v.tensor_copy(
                        wfh[hc % 2][:], wf32[hc % 2][:]), "v5s")
                    em.wait("vector", "v5s")
                    em.op("vector", lambda v, hc=hc: v.scalar_tensor_tensor(
                        out=wfl[hc % 2][:], in0=wf32[hc % 2][:], scalar=1.0,
                        in1=wfh[hc % 2][:], op0=ALU.mult, op1=ALU.subtract),
                        "v5s")
                    h_v5s.append(em.sem("v5s").n)

                w1_dma(0)
                w1_dma(1)
                w1_split(0)
                for hc in range(HFC):
                    if hc + 2 < HFC:
                        w1_dma(hc + 2)
                    if hc + 1 < HFC:
                        w1_split(hc + 1)
                    bank = psum[5 + (hc % 2)]
                    wh, wl = wfh[hc % 2], wfl[hc % 2]
                    em.wait("tensor", "v5s", h_v5s[hc])
                    if len(h_g) >= 2:
                        em.wait("tensor", "sc5g", h_g[hc - 2])
                    i_blk = hc // 8
                    for c in range(DC):
                        whc = wh[:, c * P:(c + 1) * P]
                        wlc = wl[:, c * P:(c + 1) * P]
                        hh = h2Th[:, c * 512:(c + 1) * 512]
                        hl = h2Tl[:, c * 512:(c + 1) * 512]
                        for (pi, (sa, mb)) in enumerate(
                                ((whc, hh), (whc, hl), (wlc, hh))):
                            em.op("tensor", lambda t, sa=sa, mb=mb,
                                  bank=bank, first=(c == 0 and pi == 0):
                                  t.matmul(bank[:, 0:512], sa, mb,
                                           start=first, stop=False), None)
                    e0 = i_blk * D + (hc % 8) * P
                    p0_ = i_blk * 512
                    for (sa, mb, last) in (
                            (lor2h, prowh, False), (lor2h, prowl, False),
                            (lor2l, prowh, True)):
                        em.op("tensor", lambda t, sa=sa, mb=mb, e0=e0,
                              p0_=p0_, bank=bank, last=last: t.matmul(
                                  bank[:, 0:512],
                                  sa[0:1, e0:e0 + P],
                                  mb[0:1, p0_:p0_ + 512],
                                  start=False, stop=last),
                              "t5M" if last else None)
                    h_t5M.append(em.sem("t5M").n)
                    em.wait("scalar", "t5M", h_t5M[hc])
                    em.wait("scalar", "v5g")
                    em.op("scalar", lambda sc, hc=hc, bank=bank:
                          sc.activation(
                              hact2[hc % 2], bank[:, 0:512], AF.Gelu,
                              bias=b1t_sb[:, hc:hc + 1], scale=1.0), "sc5g")
                    h_g.append(em.sem("sc5g").n)
                    em.wait("vector", "sc5g", h_g[hc])
                    em.wait("vector", "t5r")
                    em.wait("vector", "t5q")
                    em.wait("vector", "t5W")
                    em.op("vector", lambda v, hc=hc: v.tensor_copy(
                        hTh[:, hc * 512:(hc + 1) * 512], hact2[hc % 2]),
                        "v5g")
                    em.wait("vector", "v5g")
                    em.op("vector", lambda v, hc=hc: v.scalar_tensor_tensor(
                        out=hTl[:, hc * 512:(hc + 1) * 512],
                        in0=hact2[hc % 2], scalar=1.0,
                        in1=hTh[:, hc * 512:(hc + 1) * 512],
                        op0=ALU.mult, op1=ALU.subtract), "v5g")
                    if hc % 8 == 7:
                        emit_rrow(hc // 8)
                load_lor2(4)
                em.wait("tensor", "dl5")
                h_dw2 = []
                h_v5u = []
                h_t5W = []
                h_t5q = []
                h_o = []
                em.wait("sync", "v5s")
                em.wait("sync", "v5u")
                em.wait("gpsimd", "v5s")
                em.wait("gpsimd", "v5u")
                em.wait("tensor", "sc5g")
                em.wait("vector", "t5M")

                def w2_dma(idx):
                    oc_, hq_ = idx // 4, idx % 4
                    sem = "dw2a" if idx % 2 == 0 else "dw2b"
                    semg = sem + "g"
                    if idx >= 2:
                        em.wait("sync", "v5u", h_v5u[idx - 2])
                        em.wait("gpsimd", "v5u", h_v5u[idx - 2])
                    h0 = hq_ * 1024
                    em.dma("sync", lambda sy, oc_=oc_, h0=h0, idx=idx:
                           sy.dma_start(
                               out=wf32[idx % 2][:, 0:512],
                               in_=W2B[oc_, :, h0:h0 + 512]), sem)
                    em.dma("gpsimd", lambda g, oc_=oc_, h0=h0, idx=idx:
                           g.dma_start(
                               out=wf32[idx % 2][:, 512:1024],
                               in_=W2B[oc_, :, h0 + 512:h0 + 1024]), semg)
                    h_dw2.append((em.sem(sem).n, em.sem(semg).n))

                w2_dma(0)
                for oc in range(DC):
                    bank = psum[5 + (oc % 2)]
                    if len(h_o) >= 2:
                        em.wait("tensor", "sc5o", h_o[oc - 2])
                    em.wait("tensor", "v5r")
                    em.wait("tensor", "v5p")
                    for hq in range(4):
                        idx = oc * 4 + hq
                        if idx + 1 < 32:
                            w2_dma(idx + 1)
                        _s2 = "dw2a" if idx % 2 == 0 else "dw2b"
                        em.wait("vector", _s2, h_dw2[idx][0])
                        em.wait("vector", _s2 + "g", h_dw2[idx][1])
                        if len(h_t5q) >= 2:
                            em.wait("vector", "t5q", h_t5q[idx - 2])
                        em.op("vector", lambda v, idx=idx: v.tensor_copy(
                            wfh[idx % 2][:], wf32[idx % 2][:]), "v5u")
                        em.wait("vector", "v5u")
                        em.op("vector", lambda v, idx=idx:
                              v.scalar_tensor_tensor(
                                  out=wfl[idx % 2][:], in0=wf32[idx % 2][:],
                                  scalar=1.0, in1=wfh[idx % 2][:],
                                  op0=ALU.mult, op1=ALU.subtract), "v5u")
                        h_v5u.append(em.sem("v5u").n)
                        em.wait("tensor", "v5u", h_v5u[idx])
                        wh, wl = wfh[idx % 2], wfl[idx % 2]
                        for c in range(DC):
                            hcol = hq * 8 + c
                            whc = wh[:, c * P:(c + 1) * P]
                            wlc = wl[:, c * P:(c + 1) * P]
                            hh = hTh[:, hcol * 512:(hcol + 1) * 512]
                            hl = hTl[:, hcol * 512:(hcol + 1) * 512]
                            for (pi, (sa, mb)) in enumerate(
                                    ((whc, hh), (whc, hl), (wlc, hh))):
                                _first = (hq == 0 and c == 0 and pi == 0)
                                _sem = ("t5q" if (c == DC - 1 and pi == 2)
                                        else None)
                                em.op("tensor", lambda t, sa=sa, mb=mb,
                                      bank=bank, first=_first:
                                      t.matmul(bank[:, 0:512], sa, mb,
                                               start=first, stop=False),
                                      _sem)
                        h_t5q.append(em.sem("t5q").n)
                    i_m2 = oc // 2
                    for j in range(4):
                        e0 = i_m2 * D
                        o0 = (oc % 2) * P
                        sl0 = e0 + 4 * o0 + j
                        r0_ = j * 2048 + i_m2 * 512
                        for (sa, mb, last) in (
                                (lor2h, rrowh, False), (lor2h, rrowl, False),
                                (lor2l, rrowh, j == 3)):
                            em.op("tensor", lambda t, sa=sa, mb=mb, sl0=sl0,
                                  r0_=r0_, bank=bank, last=last: t.matmul(
                                      bank[:, 0:512],
                                      sa[0:1, sl0:sl0 + 509:4],
                                      mb[0:1, r0_:r0_ + 512],
                                      start=False, stop=last),
                                  "t5W" if last else None)
                    h_t5W.append(em.sem("t5W").n)
                    em.wait("scalar", "t5W", h_t5W[oc])
                    em.wait("scalar", "v5o")
                    em.op("scalar", lambda sc, oc=oc, bank=bank:
                          sc.activation(
                              outsT[:, oc * 512:(oc + 1) * 512],
                              bank[:, 0:512], AF.Identity,
                              bias=b2t_sb[:, oc:oc + 1], scale=1.0), "sc5o")
                    h_o.append(em.sem("sc5o").n)
                for kk in range(4):
                    ofk = hTh[:, kk * 2048:kk * 2048 + 2048].bitcast(F32)
                    em.wait("tensor", "sc5o")
                    em.wait("tensor", "v5o")
                    for c in range(DC):
                        pb5 = psum[2 + (c % 2)][:, (c // 2 % 2) * P:
                                                (c // 2 % 2) * P + P]
                        em.wait("tensor", "v5o")
                        em.op("tensor", lambda t, c=c, kk=kk, pb5=pb5:
                              t.transpose(
                                  pb5, outsT[:, c * 512 + kk * P:
                                             c * 512 + (kk + 1) * P],
                                  ident32[:]), "t5o")
                        em.wait("vector", "t5o")
                        em.wait("vector", "t5r")
                        em.wait("vector", "t5q")
                        em.wait("vector", "t5W")
                        em.op("vector", lambda v, c=c, pb5=pb5, ofk=ofk:
                              v.tensor_copy(ofk[:, c * P:(c + 1) * P], pb5),
                              "v5o")
                    em.wait("vector", "v5o")
                    em.op("vector", lambda v, ofk=ofk, kk=kk: v.tensor_reduce(
                        out=fst5[:, kk:kk + 1], in_=ofk[:], axis=AX.X,
                        op=ALU.add, negate=True), "v5o")
                    em.wait("vector", "v5o")
                    em.op("vector", lambda v, kk=kk: v.tensor_scalar_mul(
                        fst5[:, kk:kk + 1], fst5[:, kk:kk + 1], 1.0 / D),
                        "v5o")
                    em.wait("scalar", "v5o")
                    em.wait("scalar", "sc5T")
                    em.op("scalar", lambda sc, ofk=ofk, kk=kk: sc.activation(
                        scr2[:], ofk[:], AF.Square, bias=fst5[:, kk:kk + 1],
                        scale=1.0, accum_out=fss5[:, kk:kk + 1]), "sc5T")
                em.wait("vector", "sc5T")
                em.op("vector", lambda v: v.tensor_scalar(
                    out=fss5[:, 0:4], in0=fss5[:, 0:4], scalar1=1.0 / D,
                    scalar2=EPS, op0=ALU.mult, op1=ALU.add), "v5o")
                em.wait("scalar", "v5o")
                em.op("scalar", lambda sc: sc.activation(
                    fss5[:, 4:8], fss5[:, 0:4], AF.Sqrt, bias=0.0,
                    scale=1.0), "sc5T")
                em.wait("vector", "sc5T")
                em.op("vector", lambda v: v.reciprocal(
                    fst5[:, 4:8], fss5[:, 4:8]), "v5o")
                for _ in range(2):
                    em.wait("vector", "v5o")
                    em.op("vector", lambda v: v.tensor_tensor(
                        out=ftm5[:, 0:4], in0=fst5[:, 4:8],
                        in1=fst5[:, 4:8], op=ALU.mult), "v5o")
                    em.wait("vector", "v5o")
                    em.op("vector", lambda v: v.tensor_tensor(
                        out=ftm5[:, 0:4], in0=ftm5[:, 0:4],
                        in1=fss5[:, 0:4], op=ALU.mult), "v5o")
                    em.wait("vector", "v5o")
                    em.op("vector", lambda v: v.tensor_scalar(
                        out=ftm5[:, 0:4], in0=ftm5[:, 0:4], scalar1=-0.5,
                        scalar2=1.5, op0=ALU.mult, op1=ALU.add), "v5o")
                    em.wait("vector", "v5o")
                    em.op("vector", lambda v: v.tensor_tensor(
                        out=fst5[:, 4:8], in0=fst5[:, 4:8],
                        in1=ftm5[:, 0:4], op=ALU.mult), "v5o")
                for kk in range(4):
                    i_tile = sg * 4 + kk
                    ofk = hTh[:, kk * 2048:kk * 2048 + 2048].bitcast(F32)
                    em.wait("sync", "v5o")
                    em.dma("sync", lambda sy, i_tile=i_tile: sy.dma_start(
                        out=x1t[:], in_=X1SP[i_tile * P:(i_tile + 1) * P, :]),
                        "dx5")
                    em.wait("vector", "dx5")
                    em.wait("vector", "v5o")
                    em.op("vector", lambda v, ofk=ofk, kk=kk: v.tensor_scalar(
                        out=ofk[:], in0=ofk[:], scalar1=fst5[:, kk:kk + 1],
                        scalar2=fst5[:, 4 + kk:5 + kk], op0=ALU.add,
                        op1=ALU.mult), "v5o")
                    em.wait("vector", "v5o")
                    em.wait("vector", "dout")
                    em.op("vector", lambda v, ofk=ofk: v.tensor_tensor(
                        out=outf[:], in0=ofk[:], in1=x1t[:], op=ALU.add),
                        "v5o")
                    em.wait("sync", "v5o")
                    em.dma("sync", lambda sy, i_tile=i_tile: sy.dma_start(
                        out=OUT[i_tile * P:(i_tile + 1) * P, :], in_=outf[:]),
                        "dout")

        if BUILD_PHASES < 2:
            finalize(em)
            fp_es.close()
            run_blocks(nc, em, es)
            return nc

        finalize(em)
        if BUILD_PHASES < 3:
            fp_es.close()
        elif BUILD_PHASES < 4:
            at_es.close()
            ao_es.close()
            mid_es.close()
        else:
            for _es in reversed(open_scopes):
                _es.close()
            mid_es.close()
        run_blocks(nc, em, es)
    return nc


def finalize(em):
    for sname, s in list(em.sems.items()):
        em.wait("sync", s)


def run_blocks(nc, em, es):
    block = es.enter_context(nc.Block())

    @block.sync
    def _(sy):
        for f in em.prog["sync"]:
            f(sy)

    @block.vector
    def _(v):
        for f in em.prog["vector"]:
            f(v)

    @block.scalar
    def _(sc):
        for f in em.prog["scalar"]:
            f(sc)

    @block.tensor
    def _(t):
        for f in em.prog["tensor"]:
            f(t)

    @block.gpsimd
    def _(g):
        for f in em.prog["gpsimd"]:
            f(g)


def stage_inputs(inputs):
    x = np.ascontiguousarray(inputs["L0_attn_x"], dtype=np.float32)
    W1 = np.ascontiguousarray(inputs["L0_ffn_W1"], dtype=np.float32)
    W2 = np.ascontiguousarray(inputs["L0_ffn_W2"], dtype=np.float32)
    W1B = np.ascontiguousarray(
        W1.reshape(HFC, P, DC, P).transpose(0, 3, 2, 1).reshape(HFC, P, D))
    W2B = np.ascontiguousarray(
        W2.reshape(DC, P, 4, DC, P).transpose(0, 4, 2, 3, 1)
        .reshape(DC, P, HFF))
    B1T = np.ascontiguousarray(
        inputs["L0_ffn_b1"].reshape(HFC, P).T.astype(np.float32))
    B2T = np.ascontiguousarray(
        inputs["L0_ffn_b2"].reshape(DC, P).T.astype(np.float32))
    SCLv = np.asarray(inputs["L0_ffn_scale"], np.float32).reshape(1, 1)
    idf32 = np.eye(P, dtype=np.float32)
    idf16 = np.eye(P, dtype=np.float16)
    common = dict(
        WQT=np.ascontiguousarray(inputs["L0_attn_w_q"].T.astype(np.float32)),
        WKT=np.ascontiguousarray(inputs["L0_attn_w_k"].T.astype(np.float32)),
        WVT=np.ascontiguousarray(inputs["L0_attn_w_v"].T.astype(np.float32)),
        WOT=np.ascontiguousarray(inputs["L0_attn_w_o"].T.astype(np.float32)),
        W1B=W1B, W2B=W2B, B1T=B1T, B2T=B2T, SCL=SCLv,
        IDF32=idf32, IDF16=idf16,
    )
    in_maps = []
    for c in range(N_CORES):
        b, j = c // 2, c % 2
        xb = x[b]
        xcat = np.concatenate(
            [xb[j * TOWN:(j + 1) * TOWN], xb[(1 - j) * TOWN:(2 - j) * TOWN],
             xb[0:TEXTRA]], axis=0)
        m = dict(common)
        m["XC"] = np.ascontiguousarray(xcat)
        in_maps.append(m)
    return in_maps


def kernel(**inputs):
    in_maps = stage_inputs(inputs)
    nc = build_program()
    res = run_bass_kernel_spmd(nc, in_maps, core_ids=list(range(N_CORES)))
    out = np.empty((B, S, D), dtype=np.float32)
    for c in range(N_CORES):
        b, j = c // 2, c % 2
        out[b, j * TOWN:(j + 1) * TOWN, :] = res.results[c]["OUT"]
    return out



# version 13
# speedup vs baseline: 1.0821x; 1.0086x over previous
import math
from contextlib import ExitStack

import numpy as np
import concourse.bass as bass
import concourse.mybir as mybir
from concourse.bass_utils import run_bass_kernel_spmd

F32 = mybir.dt.float32
F16 = mybir.dt.float16
I32 = mybir.dt.int32
ALU = mybir.AluOpType
AF = mybir.ActivationFunctionType
AX = mybir.AxisListType

P = 128
D = 1024
NH = 16
HD = 64
S = 4096
TOWN = 2048
TEXTRA = 128
TCAT = S + TEXTRA
TQ = TOWN + TEXTRA
NT_CAT = TCAT // P
NT_STAT = S // P
NT_OWN = TOWN // P
DC = D // P
HFF = 4096
HFC = HFF // P
EPS = 1e-5
N_CORES = 8
B = 4

BUILD_PHASES = 99
DEBUG = False


class Sem:
    def __init__(self, es, nc, name):
        self.h = es.enter_context(nc.semaphore(name))
        self.n = 0


class Emitter:

    def __init__(self, nc, es):
        self.nc = nc
        self.es = es
        self.prog = {k: [] for k in ("sync", "vector", "scalar", "tensor",
                                     "gpsimd")}
        self.sems = {}

    def sem(self, name):
        if name not in self.sems:
            self.sems[name] = Sem(self.es, self.nc, name)
        return self.sems[name]

    def op(self, eng, f, sem=None, k=1):
        if sem is not None:
            s = self.sem(sem) if isinstance(sem, str) else sem
            s.n += k
            h = s.h

            def run(e, f=f, h=h, k=k):
                f(e).then_inc(h, k)
        else:
            def run(e, f=f):
                f(e)
        self.prog[eng].append(run)

    def wait(self, eng, sem, val=None):
        s = self.sem(sem) if isinstance(sem, str) else sem
        v = s.n if val is None else val
        self.prog[eng].append(lambda e, h=s.h, v=v: e.wait_ge(h, v))

    def barrier(self, eng_wait, sems):
        for sname in sems:
            self.wait(eng_wait, sname)

    def wait_all(self, eng):
        for s in self.sems.values():
            self.wait(eng, s)

    def dma(self, eng, f, sem):
        self.wait(eng, sem)
        self.op(eng, f, sem, 16)

    def phase_barrier(self):
        for eng in self.prog:
            for s in self.sems.values():
                self.wait(eng, s)


def ceil_div(a, b):
    return (a + b - 1) // b


def build_program():
    nc = bass.Bass("TRN2", target_bir_lowering=False)

    XC = nc.dram_tensor("XC", [TCAT, D], F32, kind="ExternalInput").ap()
    WQT = nc.dram_tensor("WQT", [D, D], F32, kind="ExternalInput").ap()
    WKT = nc.dram_tensor("WKT", [D, D], F32, kind="ExternalInput").ap()
    WVT = nc.dram_tensor("WVT", [D, D], F32, kind="ExternalInput").ap()
    WOT = nc.dram_tensor("WOT", [D, D], F32, kind="ExternalInput").ap()
    W1B = nc.dram_tensor("W1B", [HFC, P, D], F32, kind="ExternalInput").ap()
    W2B = nc.dram_tensor("W2B", [DC, P, HFF], F32, kind="ExternalInput").ap()
    B1T = nc.dram_tensor("B1T", [P, HFC], F32, kind="ExternalInput").ap()
    B2T = nc.dram_tensor("B2T", [P, DC], F32, kind="ExternalInput").ap()
    SCL = nc.dram_tensor("SCL", [1, 1], F32, kind="ExternalInput").ap()
    IDF32 = nc.dram_tensor("IDF32", [P, P], F32, kind="ExternalInput").ap()
    IDF16 = nc.dram_tensor("IDF16", [P, P], F16, kind="ExternalInput").ap()
    OUT = nc.dram_tensor("OUT", [TOWN, D], F32, kind="ExternalOutput").ap()

    _sk = "ExternalOutput" if DEBUG else "Internal"
    KH = nc.dram_tensor("KHs", [D, S], F16, kind=_sk).ap()
    KL = nc.dram_tensor("KLs", [D, S], F16, kind=_sk).ap()
    VSP = nc.dram_tensor("VSPs", [NH // 2, P, 32 * 130], F32, kind=_sk).ap()
    QH = nc.dram_tensor("QHs", [D, TQ], F16, kind=_sk).ap()
    QL = nc.dram_tensor("QLs", [D, TQ], F16, kind=_sk).ap()
    X1SP = nc.dram_tensor("X1SPs", [TOWN, D], F32, kind="Internal").ap()
    W1H = nc.dram_tensor("W1Hs", [HFC, D, P], F16, kind="Internal").ap()
    W1L = nc.dram_tensor("W1Ls", [HFC, D, P], F16, kind="Internal").ap()
    W2H = nc.dram_tensor("W2Hs", [DC, HFF, P], F16, kind="Internal").ap()
    W2L = nc.dram_tensor("W2Ls", [DC, HFF, P], F16, kind="Internal").ap()
    CCI = nc.dram_tensor("CCIs", [1, 8], F32, kind="Internal")
    CCO = nc.dram_tensor("CCOs", [1, 8], F32, kind="Internal")
    LRSPH = nc.dram_tensor("LRSPH", [8, D], F16, kind="Internal").ap()
    LRSPL = nc.dram_tensor("LRSPL", [8, D], F16, kind="Internal").ap()
    PRSP = nc.dram_tensor("PRSP", [4, 512], F32, kind="Internal").ap()
    RRSP = nc.dram_tensor("RRSP", [4, 512], F32, kind="Internal").ap()

    dbg = {}
    if DEBUG:
        dbg["KT"] = nc.dram_tensor("DBG_KT", [P, DC * TCAT], F16,
                                   kind="ExternalOutput").ap()
        dbg["ST"] = nc.dram_tensor("DBG_ST", [P, 16], F32,
                                   kind="ExternalOutput").ap()
        dbg["Q"] = nc.dram_tensor("DBG_Q", [D, TQ], F32,
                                  kind="ExternalOutput").ap()
        dbg["AO"] = nc.dram_tensor("DBG_AO", [TQ, D], F32,
                                   kind="ExternalOutput").ap()
        dbg["X1"] = nc.dram_tensor("DBG_X1", [TQ, D], F32,
                                   kind="ExternalOutput").ap()

    with ExitStack() as es:
        e = es.enter_context
        em = Emitter(nc, es)

        xa = [e(nc.sbuf_tensor(f"xa{i}", [P, D], F32)) for i in range(2)]
        scr = e(nc.sbuf_tensor("scr", [P, D], F32))
        scr2 = e(nc.sbuf_tensor("scr2", [P, D], F32))
        nmean = e(nc.sbuf_tensor("nmean", [P, NT_CAT], F32))
        rstd = e(nc.sbuf_tensor("rstd", [P, NT_CAT], F32))
        vssum = e(nc.sbuf_tensor("vssum", [P, NT_CAT], F32))
        hmax = e(nc.sbuf_tensor("hmax", [P, NT_CAT], F32))
        hmin = e(nc.sbuf_tensor("hmin", [P, NT_CAT], F32))
        s1c = e(nc.sbuf_tensor("s1c", [P, NT_CAT], F32))
        s2c = e(nc.sbuf_tensor("s2c", [P, NT_CAT], F32))
        ksq = e(nc.sbuf_tensor("ksq", [P, NT_CAT], F32))
        invsx = e(nc.sbuf_tensor("invsx", [P, NT_CAT], F32))
        sxc = e(nc.sbuf_tensor("sxc", [P, NT_CAT], F32))
        tl1 = e(nc.sbuf_tensor("tl1", [P, 1], F32))
        tl2 = e(nc.sbuf_tensor("tl2", [P, 1], F32))
        tl3 = e(nc.sbuf_tensor("tl3", [P, 1], F32))
        tl4 = e(nc.sbuf_tensor("tl4", [P, 1], F32))
        glob = e(nc.sbuf_tensor("glob", [P, 16], F32))
        ones1 = e(nc.sbuf_tensor("ones1", [1, P], F32))
        kescl = e(nc.sbuf_tensor("kescl", [P, NT_STAT], F32))
        wacc = e(nc.sbuf_tensor("wacc", [P, 1], F32))
        ident32 = e(nc.sbuf_tensor("ident32", [P, P], F32))
        ident16 = e(nc.sbuf_tensor("ident16", [P, P], F16))
        psum = [e(nc.psum_tensor(f"ps{i}", [P, 512], F32)) for i in range(8)]
        fp_es = ExitStack()
        fe = fp_es.enter_context
        h1b = fe(nc.sbuf_tensor("h1b", [P, D], F32))
        yb = fe(nc.sbuf_tensor("yb", [P, D], F32))
        eb = fe(nc.sbuf_tensor("eb", [P, D], F32))
        cb = fe(nc.sbuf_tensor("cb", [P, D], F32))
        db = fe(nc.sbuf_tensor("db", [P, D], F32))
        k1b = fe(nc.sbuf_tensor("k1b", [P, D], F32))
        ki32 = fe(nc.sbuf_tensor("ki32", [P, D], I32))
        kf16 = fe(nc.sbuf_tensor("kf16", [P, D], F16))
        kT = fe(nc.sbuf_tensor("kT", [P, DC * TCAT], F16))
        tern = fe(nc.sbuf_tensor("tern", [P, DC * D], F16))
        wbuf = [fe(nc.sbuf_tensor(f"wbuf{i}", [P, D], F32)) for i in range(2)]
        pevac = fe(nc.sbuf_tensor("pevac", [P, 512], F32))
        pv16a = fe(nc.sbuf_tensor("pv16a", [P, 512], F16))
        pv16b = fe(nc.sbuf_tensor("pv16b", [P, 512], F16))
        vstg = [fe(nc.sbuf_tensor(f"vstg{i}", [P, 520], F32))
                for i in range(2)]
        vscl = fe(nc.sbuf_tensor("vscl", [P, NT_STAT], F32))
        qscl = fe(nc.sbuf_tensor("qscl", [P, NT_CAT], F32))
        qnsq = fe(nc.sbuf_tensor("qnsq", [P, 17], F32))
        vtmp = fe(nc.sbuf_tensor("vtmp", [P, P], F32))
        qstg = fe(nc.sbuf_tensor("qstg", [P, 512], F32))
        qstg2 = fe(nc.sbuf_tensor("qstg2", [P, 512], F32))

        em.op("vector", lambda v: v.memset(glob[:], 0.0), "vA")
        em.op("vector", lambda v: v.memset(ones1[:], 1.0), "vA")
        em.dma("sync", lambda sy: sy.dma_start(out=ident32[:], in_=IDF32),
               "dma")
        em.dma("sync", lambda sy: sy.dma_start(out=ident16[:], in_=IDF16),
               "dma")
        em.wait("tensor", "dma")

        def emit_round_t(vsem, src, dst_f32, T):
            _ki, _eb, _cb, _db, _k1 = (T["ki32"], T["eb"], T["cb"], T["db"],
                                       T["k1b"])
            em.wait("vector", vsem)
            em.op("vector", lambda v: v.tensor_copy(_ki[:], src[:]), vsem)
            em.wait("vector", vsem)
            em.op("vector", lambda v: v.scalar_tensor_tensor(
                out=_eb[:], in0=src[:], scalar=1.0, in1=_ki[:],
                op0=ALU.mult, op1=ALU.subtract), vsem)
            em.wait("vector", vsem)
            em.op("vector", lambda v: v.tensor_single_scalar(
                out=_cb[:], in_=_eb[:], scalar=0.5, op=ALU.is_gt), vsem)
            em.op("vector", lambda v: v.tensor_single_scalar(
                out=_db[:], in_=_eb[:], scalar=-0.5, op=ALU.is_lt), vsem)
            em.wait("vector", vsem)
            em.op("vector", lambda v: v.scalar_tensor_tensor(
                out=_k1[:], in0=_cb[:], scalar=1.0, in1=_ki[:],
                op0=ALU.mult, op1=ALU.add), vsem)
            em.wait("vector", vsem)
            em.op("vector", lambda v: v.scalar_tensor_tensor(
                out=dst_f32[:], in0=_db[:], scalar=-1.0, in1=_k1[:],
                op0=ALU.mult, op1=ALU.add), vsem)

        def emit_round(vsem, src, dst_f32):
            emit_round_t(vsem, src, dst_f32,
                         dict(ki32=ki32, eb=eb, cb=cb, db=db, k1b=k1b))

        def emit_round2(vsem, src, out_f16, T, extra_waits=()):
            _ki, _eb, _db, _k1 = T["ki32"], T["eb"], T["db"], T["k1b"]
            em.wait("vector", vsem)
            em.op("vector", lambda v: v.tensor_copy(_ki[:], src[:]), vsem)
            em.wait("vector", vsem)
            em.op("vector", lambda v: v.scalar_tensor_tensor(
                out=_eb[:], in0=src[:], scalar=1.0, in1=_ki[:],
                op0=ALU.mult, op1=ALU.subtract), vsem)
            em.wait("vector", vsem)
            em.op("vector", lambda v: v.tensor_single_scalar(
                out=_db[:], in_=_eb[:], scalar=-0.5, op=ALU.is_lt), vsem)
            em.op("vector", lambda v: v.scalar_tensor_tensor(
                out=_k1[:], in0=_eb[:], scalar=0.5, in1=_ki[:],
                op0=ALU.is_gt, op1=ALU.add), vsem)
            em.wait("vector", vsem)
            for w_ in extra_waits:
                em.wait("vector", w_)
            em.op("vector", lambda v: v.scalar_tensor_tensor(
                out=out_f16[:], in0=_db[:], scalar=-1.0, in1=_k1[:],
                op0=ALU.mult, op1=ALU.add), vsem)

        def cross_part_reduce(src_col, dst_ap, op, psum_bank):
            em.wait_all("tensor")
            em.wait_all("vector")
            em.op("vector", lambda v: v.tensor_copy(scr[:, 4:5], src_col),
                  "vP")
            em.wait("tensor", "vP")
            em.op("tensor", lambda t: t.transpose(
                psum_bank[0:1, 0:P], scr[:, 4:5], ident32[:]), "tP")
            em.wait("vector", "tP")
            em.op("vector", lambda v: v.tensor_reduce(
                out=dst_ap, in_=psum_bank[0:1, 0:P], axis=AX.X, op=op), "vP")

        def bcast_col(src11, dst_col, psum_bank):
            em.wait_all("tensor")
            em.wait_all("vector")
            em.op("tensor", lambda t: t.matmul(
                psum_bank[:, 0:1], ones1[:], src11, start=True, stop=True),
                "tP")
            em.wait("vector", "tP")
            em.op("vector", lambda v: v.tensor_copy(dst_col, psum_bank[:, 0:1]),
                  "vP")

        def recip_refined(val_col, out_col, iters=1):
            em.wait("vector", "vP")
            em.op("vector", lambda v: v.reciprocal(tl3[:], val_col), "vP")
            for _ in range(iters):
                em.wait("vector", "vP")
                em.op("vector", lambda v: v.tensor_tensor(
                    out=tl4[:], in0=val_col, in1=tl3[:], op=ALU.mult), "vP")
                em.wait("vector", "vP")
                em.op("vector", lambda v: v.tensor_scalar(
                    out=tl4[:], in0=tl4[:], scalar1=-1.0, scalar2=2.0,
                    op0=ALU.mult, op1=ALU.add), "vP")
                em.wait("vector", "vP")
                em.op("vector", lambda v: v.tensor_tensor(
                    out=tl3[:], in0=tl3[:], in1=tl4[:], op=ALU.mult), "vP")
            em.wait("vector", "vP")
            em.op("vector", lambda v: v.tensor_copy(out_col, tl3[:]), "vP")

        h_vA, h_scA = [], []
        for i in range(NT_CAT):
            buf = xa[i % 2]
            if i >= 2:
                em.wait("sync", "vA", h_vA[i - 2])
                em.wait("sync", "scA", h_scA[i - 2])
            em.dma("sync", lambda sy, b=buf, i=i: sy.dma_start(
                out=b[:], in_=XC[i * P:(i + 1) * P, :]), "dma")
            em.wait("vector", "dma")
            em.op("vector", lambda v, b=buf, i=i: v.tensor_reduce(
                out=hmax[:, i:i + 1], in_=b[:], axis=AX.X, op=ALU.max), "vA")
            em.op("vector", lambda v, b=buf, i=i: v.tensor_reduce(
                out=hmin[:, i:i + 1], in_=b[:], axis=AX.X, op=ALU.min), "vA")
            h_vA.append(em.sem("vA").n)
            em.wait("scalar", "dma")
            em.wait("scalar", "scA")
            em.op("scalar", lambda sc, b=buf, i=i: sc.activation(
                scr[:], b[:], AF.Identity, bias=0.0, scale=1.0,
                accum_out=nmean[:, i:i + 1]), "scA")
            em.op("scalar", lambda sc, b=buf, i=i: sc.activation(
                scr2[:], b[:], AF.Square, bias=0.0, scale=1.0,
                accum_out=vssum[:, i:i + 1]), "scA")
            h_scA.append(em.sem("scA").n)
        NC_ = NT_CAT
        em.wait("vector", "vA")
        em.wait("vector", "scA")
        em.op("vector", lambda v: v.tensor_scalar_mul(
            nmean[:, 0:NC_], nmean[:, 0:NC_], -1.0 / D), "vA")
        em.wait("vector", "vA")
        em.op("vector", lambda v: v.tensor_tensor(
            out=s2c[:, 0:NC_], in0=nmean[:, 0:NC_], in1=nmean[:, 0:NC_],
            op=ALU.mult), "vA")
        em.wait("vector", "vA")
        em.op("vector", lambda v: v.scalar_tensor_tensor(
            out=s1c[:, 0:NC_], in0=vssum[:, 0:NC_], scalar=1.0 / D,
            in1=s2c[:, 0:NC_], op0=ALU.mult, op1=ALU.subtract), "vA")
        em.wait("vector", "vA")
        em.op("vector", lambda v: v.tensor_scalar_add(
            s1c[:, 0:NC_], s1c[:, 0:NC_], EPS), "vA")
        em.wait("scalar", "vA")
        em.wait("scalar", "scA")
        em.op("scalar", lambda sc: sc.activation(
            scr[:, 0:NC_], s1c[:, 0:NC_], AF.Sqrt, bias=0.0, scale=1.0),
            "scA")
        em.wait("vector", "scA")
        em.op("vector", lambda v: v.reciprocal(rstd[:, 0:NC_],
                                               scr[:, 0:NC_]), "vA")
        for _ in range(2):
            em.wait("vector", "vA")
            em.op("vector", lambda v: v.tensor_tensor(
                out=s2c[:, 0:NC_], in0=rstd[:, 0:NC_], in1=rstd[:, 0:NC_],
                op=ALU.mult), "vA")
            em.wait("vector", "vA")
            em.op("vector", lambda v: v.tensor_tensor(
                out=s2c[:, 0:NC_], in0=s2c[:, 0:NC_], in1=s1c[:, 0:NC_],
                op=ALU.mult), "vA")
            em.wait("vector", "vA")
            em.op("vector", lambda v: v.tensor_scalar(
                out=s2c[:, 0:NC_], in0=s2c[:, 0:NC_], scalar1=-0.5,
                scalar2=1.5, op0=ALU.mult, op1=ALU.add), "vA")
            em.wait("vector", "vA")
            em.op("vector", lambda v: v.tensor_tensor(
                out=rstd[:, 0:NC_], in0=rstd[:, 0:NC_], in1=s2c[:, 0:NC_],
                op=ALU.mult), "vA")
        em.wait("vector", "vA")
        em.op("vector", lambda v: v.tensor_tensor(
            out=s2c[:, 0:NC_], in0=nmean[:, 0:NC_], in1=nmean[:, 0:NC_],
            op=ALU.mult), "vA")
        em.wait("vector", "vA")
        em.op("vector", lambda v: v.scalar_tensor_tensor(
            out=s2c[:, 0:NC_], in0=s2c[:, 0:NC_], scalar=-float(D),
            in1=vssum[:, 0:NC_], op0=ALU.mult, op1=ALU.add), "vA")
        em.wait("vector", "vA")
        em.op("vector", lambda v: v.tensor_tensor(
            out=scr[:, 0:NC_], in0=rstd[:, 0:NC_], in1=rstd[:, 0:NC_],
            op=ALU.mult), "vA")
        em.wait("vector", "vA")
        em.op("vector", lambda v: v.tensor_tensor(
            out=s2c[:, 0:NC_], in0=s2c[:, 0:NC_], in1=scr[:, 0:NC_],
            op=ALU.mult), "vA")
        em.wait("vector", "vA")
        em.op("vector", lambda v: v.tensor_reduce(
            out=tl2[:], in_=s2c[:, 0:NT_STAT], axis=AX.X, op=ALU.add), "vA")
        em.wait("vector", "vA")
        em.op("vector", lambda v: v.tensor_copy(scr[:, 1:2], tl2[:]), "vA")
        em.wait("tensor", "vA")
        em.wait("tensor", "dma")
        em.op("tensor", lambda t: t.transpose(
            psum[1][0:1, 0:P], scr[:, 1:2], ident32[:]), "tT")
        em.wait("vector", "tT")
        em.op("vector", lambda v: v.tensor_reduce(
            out=glob[0:1, 7:8], in_=psum[1][0:1, 0:P], axis=AX.X,
            op=ALU.add), "vA")
        em.wait("vector", "vA")
        em.op("vector", lambda v: v.tensor_scalar_mul(
            glob[0:1, 7:8], glob[0:1, 7:8], 1.0 / (S * D)), "vA")
        bcast_col(glob[0:1, 7:8], glob[:, 5:6], psum[2])
        em.wait("vector", "vP")
        em.op("vector", lambda v: v.tensor_scalar_add(
            tl1[:], glob[:, 5:6], EPS), "vA")
        em.wait("scalar", "vA")
        em.op("scalar", lambda sc: sc.activation(
            tl2[:], tl1[:], AF.Sqrt, bias=0.0, scale=1.0), "scA")
        em.wait("vector", "scA")
        em.op("vector", lambda v: v.reciprocal(tl3[:], tl2[:]), "vA")
        for _ in range(2):
            em.wait("vector", "vA")
            em.op("vector", lambda v: v.tensor_tensor(
                out=tl4[:], in0=tl3[:], in1=tl3[:], op=ALU.mult), "vA")
            em.wait("vector", "vA")
            em.op("vector", lambda v: v.tensor_tensor(
                out=tl4[:], in0=tl4[:], in1=tl1[:], op=ALU.mult), "vA")
            em.wait("vector", "vA")
            em.op("vector", lambda v: v.tensor_scalar(
                out=tl4[:], in0=tl4[:], scalar1=-0.5, scalar2=1.5,
                op0=ALU.mult, op1=ALU.add), "vA")
            em.wait("vector", "vA")
            em.op("vector", lambda v: v.tensor_tensor(
                out=tl3[:], in0=tl3[:], in1=tl4[:], op=ALU.mult), "vA")
        em.wait("vector", "vA")
        em.op("vector", lambda v: v.tensor_copy(glob[:, 2:3], tl3[:]), "vA")
        em.wait("vector", "vA")
        em.op("vector", lambda v: v.scalar_tensor_tensor(
            out=hmax[:, 0:NC_], in0=hmax[:, 0:NC_], scalar=1.0,
            in1=nmean[:, 0:NC_], op0=ALU.mult, op1=ALU.add), "vA")
        em.op("vector", lambda v: v.scalar_tensor_tensor(
            out=hmin[:, 0:NC_], in0=hmin[:, 0:NC_], scalar=-1.0,
            in1=nmean[:, 0:NC_], op0=ALU.mult, op1=ALU.subtract), "vA")
        em.wait("vector", "vA")
        em.op("vector", lambda v: v.tensor_tensor(
            out=hmax[:, 0:NC_], in0=hmax[:, 0:NC_], in1=hmin[:, 0:NC_],
            op=ALU.max), "vA")
        em.wait("vector", "vA")
        em.op("vector", lambda v: v.tensor_tensor(
            out=hmax[:, 0:NC_], in0=hmax[:, 0:NC_], in1=rstd[:, 0:NC_],
            op=ALU.mult), "vA")
        em.wait("vector", "vA")
        em.op("vector", lambda v: v.tensor_scalar_mul(
            hmax[:, 0:NC_], hmax[:, 0:NC_], glob[:, 2:3]), "vA")
        em.wait("vector", "vA")
        em.op("vector", lambda v: v.tensor_scalar_max(
            hmax[:, 0:NC_], hmax[:, 0:NC_], 1e-5), "vA")
        em.wait("vector", "vA")
        em.op("vector", lambda v: v.tensor_scalar_mul(
            invsx[:, 0:NC_], hmax[:, 0:NC_], 1.0 / 127.0), "vA")
        em.op("vector", lambda v: v.reciprocal(sxc[:, 0:NC_],
                                               hmax[:, 0:NC_]), "vA")
        em.wait("vector", "vA")
        em.op("vector", lambda v: v.tensor_tensor(
            out=scr[:, 0:NC_], in0=hmax[:, 0:NC_], in1=sxc[:, 0:NC_],
            op=ALU.mult), "vA")
        em.wait("vector", "vA")
        em.op("vector", lambda v: v.tensor_scalar(
            out=scr[:, 0:NC_], in0=scr[:, 0:NC_], scalar1=-1.0, scalar2=2.0,
            op0=ALU.mult, op1=ALU.add), "vA")
        em.wait("vector", "vA")
        em.op("vector", lambda v: v.tensor_tensor(
            out=sxc[:, 0:NC_], in0=sxc[:, 0:NC_], in1=scr[:, 0:NC_],
            op=ALU.mult), "vA")
        em.wait("vector", "vA")
        em.op("vector", lambda v: v.tensor_scalar_mul(
            sxc[:, 0:NC_], sxc[:, 0:NC_], 127.0), "vA")
        em.wait("vector", "vA")
        em.op("vector", lambda v: v.tensor_scalar_mul(
            s1c[:, 0:NC_], sxc[:, 0:NC_], glob[:, 2:3]), "vA")
        em.wait("vector", "vA")
        em.op("vector", lambda v: v.tensor_tensor(
            out=s1c[:, 0:NC_], in0=s1c[:, 0:NC_], in1=rstd[:, 0:NC_],
            op=ALU.mult), "vA")

        em.wait("sync", "dma")
        em.wait("sync", "vA")
        em.wait("sync", "scA")
        em.wait("vector", "vA")
        em.wait("vector", "vP")
        h_vB = []
        T_front = dict(ki32=ki32, eb=eb, db=db, k1b=k1b)
        for i in range(NT_CAT):
            buf = xa[i % 2]
            if i >= 2:
                em.wait("sync", "vB", h_vB[i - 2])
            em.dma("sync", lambda sy, b=buf, i=i: sy.dma_start(
                out=b[:], in_=XC[i * P:(i + 1) * P, :]), "dmb")
            em.wait("vector", "dmb")
            em.op("vector", lambda v, b=buf, i=i: v.tensor_scalar(
                out=yb[:], in0=b[:], scalar1=nmean[:, i:i + 1],
                scalar2=s1c[:, i:i + 1], op0=ALU.add, op1=ALU.mult), "vB")
            h_vB.append(em.sem("vB").n)
            emit_round2("vB", yb, kf16, T_front,
                        extra_waits=("scB", "tT"))
            em.wait("scalar", "vB")
            em.wait("scalar", "scB")
            em.op("scalar", lambda sc, i=i: sc.activation(
                scr2[:], kf16[:], AF.Square, bias=0.0, scale=1.0,
                accum_out=ksq[:, i:i + 1]), "scB")
            em.wait("tensor", "vB")
            em.wait("tensor", "scB")
            if i == 0:
                em.wait("tensor", "vA")
                em.wait("tensor", "vP")
            pts = [psum[c % 4][:, (c // 4) * 64:(c // 4) * 64 + 64]
                   .bitcast(F16) for c in range(DC)]
            for c in range(DC):
                em.op("tensor", lambda t, c=c: t.transpose(
                    pts[c], kf16[:, c * P:(c + 1) * P], ident16[:]), "tT")
            em.wait("scalar", "tT")
            for c in range(DC):
                em.op("scalar", lambda sc, c=c, i=i: sc.activation(
                    kT[:, c * TCAT + i * P: c * TCAT + (i + 1) * P], pts[c],
                    AF.Identity, bias=0.0, scale=1.0), "scB")
        if DEBUG:
            em.wait("sync", "vB")
            em.wait("sync", "scB")
            em.dma("sync", lambda sy: sy.dma_start(
                out=dbg["KT"], in_=kT[:]), "dst")

        def ternarize_t(WT, sw_glob_col, tern, wbuf, yb, T):
            em.wait("vector", "tEnd")
            for c in range(DC):
                wb = wbuf[c % 2]
                em.wait("sync", "vP")
                em.wait("gpsimd", "vP")
                em.dma("sync", lambda sy, wb=wb, c=c: sy.dma_start(
                    out=wb[:, 0:512], in_=WT[c * P:(c + 1) * P, 0:512]),
                    "dmw")
                em.dma("gpsimd", lambda g, wb=wb, c=c: g.dma_start(
                    out=wb[:, 512:D], in_=WT[c * P:(c + 1) * P, 512:D]),
                    "dmwg")
                em.wait("vector", "dmw")
                em.wait("vector", "dmwg")
                em.op("vector", lambda v, wb=wb: v.tensor_reduce(
                    out=tl1[:], in_=wb[:], axis=AX.X, op=ALU.add,
                    apply_absolute_value=True), "vP")
                em.wait("vector", "vP")
                if c == 0:
                    em.op("vector", lambda v: v.tensor_copy(wacc[:], tl1[:]),
                          "vP")
                else:
                    em.op("vector", lambda v: v.tensor_tensor(
                        out=wacc[:], in0=wacc[:], in1=tl1[:], op=ALU.add),
                        "vP")
            cross_part_reduce(wacc[:], glob[0:1, 8:9], ALU.add, psum[6])
            em.wait("vector", "vP")
            em.op("vector", lambda v: v.tensor_scalar(
                out=glob[0:1, 8:9], in0=glob[0:1, 8:9],
                scalar1=1.0 / (D * D), scalar2=1e-5, op0=ALU.mult,
                op1=ALU.max), "vP")
            bcast_col(glob[0:1, 8:9], glob[:, sw_glob_col:sw_glob_col + 1],
                      psum[6])
            em.wait("vector", "vP")
            em.op("vector", lambda v: v.tensor_scalar_mul(
                glob[:, 13:14], glob[:, sw_glob_col:sw_glob_col + 1], 0.5),
                "vP")
            em.op("vector", lambda v: v.tensor_scalar_mul(
                glob[:, 14:15], glob[:, sw_glob_col:sw_glob_col + 1], -0.5),
                "vP")
            em.wait("vector", "vP")
            for c in range(DC):
                wb = wbuf[c % 2]
                em.wait("sync", "vP")
                em.wait("gpsimd", "vP")
                em.dma("sync", lambda sy, wb=wb, c=c: sy.dma_start(
                    out=wb[:, 0:512], in_=WT[c * P:(c + 1) * P, 0:512]),
                    "dmw")
                em.dma("gpsimd", lambda g, wb=wb, c=c: g.dma_start(
                    out=wb[:, 512:D], in_=WT[c * P:(c + 1) * P, 512:D]),
                    "dmwg")
                em.wait("vector", "dmw")
                em.wait("vector", "dmwg")
                em.op("vector", lambda v, wb=wb: v.tensor_scalar(
                    out=T["cb"][:], in0=wb[:], scalar1=glob[:, 13:14],
                    scalar2=1.0, op0=ALU.is_gt, op1=ALU.mult), "vP")
                em.op("vector", lambda v, wb=wb: v.tensor_scalar(
                    out=T["db"][:], in0=wb[:], scalar1=glob[:, 14:15],
                    scalar2=1.0, op0=ALU.is_lt, op1=ALU.mult), "vP")
                em.wait("vector", "vP")
                em.op("vector", lambda v, c=c, tern=tern:
                      v.scalar_tensor_tensor(
                          out=tern[:, c * D:(c + 1) * D], in0=T["cb"][:],
                          scalar=1.0, in1=T["db"][:], op0=ALU.mult,
                          op1=ALU.subtract), "vP")

        if BUILD_PHASES >= 2:
            NKT = S // 512
            QT_W = [512, 512, 512, 512, 128]
            QT_OFF = [0, 512, 1024, 1536, 4096]

            T_fp = dict(ki32=ki32, eb=eb, cb=cb, db=db, k1b=k1b)
            ternarize_t(WVT, 6, tern, wbuf, yb, T_fp)
            em.wait("vector", "vP")
            for i in range(NT_STAT):
                em.op("vector", lambda v, i=i: v.tensor_tensor(
                    out=vscl[:, i:i + 1], in0=invsx[:, i:i + 1],
                    in1=glob[:, 6:7], op=ALU.mult), "vP")
            for b in range(2):
                for kc in range(4):
                    em.op("vector", lambda v, b=b, kc=kc: v.memset(
                        vstg[b][:, kc * 130 + 64:kc * 130 + 65], 1.0), "vP")
                    em.op("vector", lambda v, b=b, kc=kc: v.memset(
                        vstg[b][:, kc * 130 + 129:kc * 130 + 130], 1.0), "vP")
            em.wait("tensor", "vB")
            em.wait("tensor", "scB")
            h_vV_w = {0: None, 1: None}
            for oc in range(DC):
                em.wait("tensor", "vP")
                tv_mm = {}
                for w in range(2):
                    if h_vV_w[w] is not None:
                        em.wait("tensor", "vV", h_vV_w[w])
                    for c in range(DC):
                        for tt in range(w * 4, w * 4 + 4):
                            em.op("tensor", lambda t, oc=oc, c=c, tt=tt:
                                  t.matmul(
                                      psum[tt][:],
                                      tern[:, c * D + oc * P:
                                           c * D + oc * P + P],
                                      kT[:, c * TCAT + tt * 512:
                                         c * TCAT + tt * 512 + 512],
                                      start=(c == 0), stop=(c == DC - 1)),
                                  "tV" if c == DC - 1 else None)
                    tv_mm[w] = em.sem("tV").n
                for w in range(2):
                    tv_need = tv_mm[w]
                    for tt in range(w * 4, w * 4 + 4):
                        vs = vstg[tt % 2]
                        em.wait("vector", "tV", tv_need)
                        em.op("vector", lambda v, tt=tt: v.tensor_copy(
                            pevac[:], psum[tt][:]), "vV")
                        em.wait("tensor", "vV")
                        for kc in range(4):
                            em.op("tensor", lambda t, tt=tt, kc=kc:
                                  t.transpose(
                                      psum[tt][:, kc * P:(kc + 1) * P],
                                      pevac[:, kc * P:(kc + 1) * P],
                                      ident32[:]), "tV")
                        tv_need = em.sem("tV").n
                        em.wait("vector", "tV", tv_need)
                        em.wait("vector", "dsp")
                        for kc in range(4):
                            col = tt * 4 + kc
                            em.wait("vector", "vV")
                            em.op("vector", lambda v, vs=vs, tt=tt, kc=kc,
                                  col=col: v.tensor_scalar_mul(
                                      vs[:, kc * 130 + 0:kc * 130 + 64],
                                      psum[tt][:, kc * P:kc * P + 64],
                                      vscl[:, col:col + 1]), "vV")
                            em.op("vector", lambda v, vs=vs, tt=tt, kc=kc,
                                  col=col: v.tensor_scalar_mul(
                                      vs[:, kc * 130 + 65:kc * 130 + 129],
                                      psum[tt][:, kc * P + 64:(kc + 1) * P],
                                      vscl[:, col:col + 1]), "vV")
                        em.wait("gpsimd", "vV")
                        em.dma("gpsimd", lambda g, vs=vs, oc=oc, tt=tt:
                               g.dma_start(
                                   out=VSP[oc, :, tt * 4 * 130:
                                           (tt + 1) * 4 * 130],
                                   in_=vs[:]), "dsp")
                    h_vV_w[w] = em.sem("vV").n

            em.op("tensor", lambda t: t.nop(), "tEnd")
            ternarize_t(WKT, 5, tern, wbuf, yb, T_fp)
            h_vK_w = {0: None, 1: None}
            for oc in range(DC):
                em.wait("tensor", "vP")
                tk_mm = {}
                for w in range(2):
                    if h_vK_w[w] is not None:
                        em.wait("tensor", "vK", h_vK_w[w])
                    for c in range(DC):
                        for tt in range(w * 4, w * 4 + 4):
                            em.op("tensor", lambda t, oc=oc, c=c, tt=tt:
                                  t.matmul(
                                      psum[tt][:],
                                      tern[:, c * D + oc * P:
                                           c * D + oc * P + P],
                                      kT[:, c * TCAT + tt * 512:
                                         c * TCAT + tt * 512 + 512],
                                      start=(c == 0), stop=(c == DC - 1)),
                                  "tK" if c == DC - 1 else None)
                    tk_mm[w] = em.sem("tK").n
                for w in range(2):
                    for tt in range(w * 4, w * 4 + 4):
                        em.wait("vector", "tK", tk_mm[w])
                        em.wait("vector", "dsk")
                        em.op("vector", lambda v, tt=tt: v.tensor_scalar_mul(
                            pv16a[:], psum[tt][:], 0.25), "vK")
                        em.wait("vector", "vK")
                        em.op("vector", lambda v, tt=tt:
                              v.scalar_tensor_tensor(
                                  out=pv16b[:], in0=psum[tt][:], scalar=0.25,
                                  in1=pv16a[:], op0=ALU.mult,
                                  op1=ALU.subtract), "vK")
                        em.wait("gpsimd", "vK")
                        em.dma("gpsimd", lambda g, oc=oc, tt=tt: g.dma_start(
                            out=KH[oc * P:(oc + 1) * P,
                                   tt * 512:(tt + 1) * 512],
                            in_=pv16a[:]), "dsk")
                        em.dma("gpsimd", lambda g, oc=oc, tt=tt: g.dma_start(
                            out=KL[oc * P:(oc + 1) * P,
                                   tt * 512:(tt + 1) * 512],
                            in_=pv16b[:]), "dsk")
                    h_vK_w[w] = em.sem("vK").n

            em.op("tensor", lambda t: t.nop(), "tEnd")
            ternarize_t(WQT, 4, tern, wbuf, yb, T_fp)
            em.wait("vector", "vP")
            for i in list(range(16)) + [32]:
                em.op("vector", lambda v, i=i: v.tensor_scalar_mul(
                    qscl[:, i:i + 1], invsx[:, i:i + 1], 0.25), "vP")
            em.op("vector", lambda v: v.memset(qnsq[:], 0.0), "vQn")
            h_vQ_w = {0: None, 1: None}
            QW_TTS = {0: (0, 1, 2), 1: (3, 4)}
            for oc in range(DC):
                em.wait("tensor", "vP")
                tq_mm = {}
                for w in range(2):
                    if h_vQ_w[w] is not None:
                        em.wait("tensor", "vQ", h_vQ_w[w])
                    for c in range(DC):
                        for tt in QW_TTS[w]:
                            em.op("tensor", lambda t, oc=oc, c=c, tt=tt:
                                  t.matmul(
                                      psum[tt][:, 0:QT_W[tt]],
                                      tern[:, c * D + oc * P:
                                           c * D + oc * P + P],
                                      kT[:, c * TCAT + QT_OFF[tt]:
                                         c * TCAT + QT_OFF[tt] + QT_W[tt]],
                                      start=(c == 0), stop=(c == DC - 1)),
                                  "tQ" if c == DC - 1 else None)
                    tq_mm[w] = em.sem("tQ").n
                for w in range(2):
                    tq_need = tq_mm[w]
                    for tt in QW_TTS[w]:
                        W = QT_W[tt]
                        nkk = W // P
                        em.wait("vector", "tQ", tq_need)
                        em.op("vector", lambda v, tt=tt, W=W: v.tensor_copy(
                            qstg[:, 0:W], psum[tt][:, 0:W]), "vQ")
                        em.wait("tensor", "vQ")
                        for kk in range(nkk):
                            em.op("tensor", lambda t, tt=tt, kk=kk:
                                  t.transpose(
                                      psum[tt][:, kk * P:(kk + 1) * P],
                                      qstg[:, kk * P:(kk + 1) * P],
                                      ident32[:]), "tQ")
                        tq_need = em.sem("tQ").n
                        em.wait("vector", "tQ", tq_need)
                        for kk in range(nkk):
                            col = (tt * 4 + kk) if tt < 4 else 32
                            qtile = (tt * 4 + kk) if tt < 4 else 16
                            em.op("vector", lambda v, tt=tt, kk=kk, col=col:
                                  v.tensor_scalar_mul(
                                      qstg2[:, kk * P:(kk + 1) * P],
                                      psum[tt][:, kk * P:(kk + 1) * P],
                                      qscl[:, col:col + 1]), "vQ")
                            em.wait("scalar", "vQ")
                            em.op("scalar", lambda sc, kk=kk: sc.activation(
                                scr2[:, 0:P], qstg2[:, kk * P:(kk + 1) * P],
                                AF.Square, bias=0.0, scale=1.0,
                                accum_out=tl2[:]), "scQ")
                            em.wait("vector", "scQ")
                            em.wait("vector", "vQn")
                            em.op("vector", lambda v, qtile=qtile:
                                  v.tensor_tensor(
                                      out=qnsq[:, qtile:qtile + 1],
                                      in0=qnsq[:, qtile:qtile + 1],
                                      in1=tl2[:], op=ALU.add), "vQn")
                        em.wait("tensor", "vQ")
                        for kk in range(nkk):
                            em.op("tensor", lambda t, tt=tt, kk=kk:
                                  t.transpose(
                                      psum[tt][:, kk * P:(kk + 1) * P],
                                      qstg2[:, kk * P:(kk + 1) * P],
                                      ident32[:]), "tQ")
                        tq_need = em.sem("tQ").n
                        em.wait("vector", "tQ", tq_need)
                        em.wait("vector", "dsq")
                        em.wait("vector", "dsk")
                        em.op("vector", lambda v, tt=tt, W=W: v.tensor_copy(
                            pv16a[:, 0:W], psum[tt][:, 0:W]), "vQ")
                        em.wait("vector", "vQ")
                        em.op("vector", lambda v, tt=tt, W=W:
                              v.scalar_tensor_tensor(
                                  out=pv16b[:, 0:W], in0=psum[tt][:, 0:W],
                                  scalar=1.0, in1=pv16a[:, 0:W],
                                  op0=ALU.mult, op1=ALU.subtract), "vQ")
                        tok0 = tt * 512 if tt < 4 else 2048
                        em.wait("gpsimd", "vQ")
                        em.dma("gpsimd", lambda g, oc=oc, tok0=tok0, W=W:
                               g.dma_start(
                                   out=QH[oc * P:(oc + 1) * P,
                                          tok0:tok0 + W],
                                   in_=pv16a[:, 0:W]), "dsq")
                        em.dma("gpsimd", lambda g, oc=oc, tok0=tok0, W=W:
                               g.dma_start(
                                   out=QL[oc * P:(oc + 1) * P,
                                          tok0:tok0 + W],
                                   in_=pv16b[:, 0:W]), "dsq")
                    h_vQ_w[w] = em.sem("vQ").n

            em.op("tensor", lambda t: t.nop(), "tEnd")
            em.wait("vector", "vP")
            em.op("vector", lambda v: v.tensor_tensor(
                out=tl1[:], in0=glob[:, 4:5], in1=glob[:, 5:6], op=ALU.mult),
                "vP")
            em.wait("vector", "vP")
            em.op("vector", lambda v: v.tensor_scalar_mul(
                tl1[:], tl1[:], 2.0), "vP")
            em.wait("vector", "vP")
            for i in range(NT_STAT):
                em.op("vector", lambda v, i=i: v.tensor_tensor(
                    out=kescl[:, i:i + 1], in0=invsx[:, i:i + 1], in1=tl1[:],
                    op=ALU.mult), "vP")
            em.wait("scalar", "vB")
            em.wait("scalar", "scB")
            em.wait("scalar", "scQ")
            em.op("scalar", lambda sc: sc.activation(
                scr2[:, 0:NT_STAT], ksq[:, 0:NT_STAT], AF.Sqrt, bias=0.0,
                scale=1.0), "scP")
            em.wait("vector", "scP")
            em.op("vector", lambda v: v.memset(eb[:, 0:NT_STAT], 0.0), "vP")
            em.wait("vector", "vP")
            for i in range(NT_STAT):
                em.op("vector", lambda v, i=i: v.tensor_tensor(
                    out=eb[:, i:i + 1], in0=scr2[:, i:i + 1],
                    in1=kescl[:, i:i + 1], op=ALU.mult), "vP")
            em.wait("vector", "vP")
            em.op("vector", lambda v: v.tensor_scalar_mul(
                eb[:, 0:NT_STAT], eb[:, 0:NT_STAT], 0.25), "vP")
            em.wait("vector", "vP")
            em.op("vector", lambda v: v.tensor_reduce(
                out=tl1[:], in_=eb[:, 0:NT_STAT], axis=AX.X, op=ALU.max),
                "vP")
            cross_part_reduce(tl1[:], glob[0:1, 9:10], ALU.max, psum[6])
            em.wait("scalar", "vQn")
            em.wait("scalar", "scP")
            em.wait("scalar", "vP")
            em.op("scalar", lambda sc: sc.activation(
                scr2[:, 0:17], qnsq[:, 0:17], AF.Sqrt, bias=0.0,
                scale=1.0), "scP")
            em.wait("vector", "scP")
            em.op("vector", lambda v: v.tensor_reduce(
                out=tl1[:], in_=scr2[:, 0:17], axis=AX.X, op=ALU.max), "vP")
            cross_part_reduce(tl1[:], glob[0:1, 10:11], ALU.max, psum[6])
            em.wait("vector", "vP")
            em.op("vector", lambda v: v.tensor_tensor(
                out=glob[0:1, 11:12], in0=glob[0:1, 9:10],
                in1=glob[0:1, 10:11], op=ALU.mult), "vP")
            em.wait("vector", "vP")
            em.op("vector", lambda v: v.tensor_scalar_mul(
                glob[0:1, 11:12], glob[0:1, 11:12], -1.01), "vP")
            bcast_col(glob[0:1, 11:12], glob[:, 3:4], psum[6])

            if DEBUG:
                em.wait("sync", "vP")
                em.dma("sync", lambda sy: sy.dma_start(
                    out=dbg["ST"], in_=glob[:]), "dst")

        if BUILD_PHASES >= 3:
            em.phase_barrier()
            fp_es.close()
            mid_es = ExitStack()
            me = mid_es.enter_context
            x1x = me(nc.sbuf_tensor("x1x", [P, D], F32))
            oqscl = me(nc.sbuf_tensor("oqscl", [P, 17], F32))
            lorT = me(nc.sbuf_tensor("lorT", [P, DC * 16], F32))
            lorTh = me(nc.sbuf_tensor("lorTh", [P, DC * 16], F16))
            lorTl = me(nc.sbuf_tensor("lorTl", [P, DC * 16], F16))
            ao_es = ExitStack()
            attnout = ao_es.enter_context(
                nc.sbuf_tensor("attnout", [P, 17 * D], F32))
            at_es = ExitStack()
            ae = at_es.enter_context
            qbh = ae(nc.sbuf_tensor("qbh", [P, DC * 512], F16))
            qbl = ae(nc.sbuf_tensor("qbl", [P, DC * 512], F16))
            kbh = [ae(nc.sbuf_tensor(f"kbh{i}", [P, S], F16))
                   for i in range(2)]
            kbl = [ae(nc.sbuf_tensor(f"kbl{i}", [P, S], F16))
                   for i in range(2)]
            vbuf = [ae(nc.sbuf_tensor(f"vbuf{i}", [P, 32 * 130], F32))
                    for i in range(2)]
            ea32 = [ae(nc.sbuf_tensor(f"ea32{i}", [P, 512], F32))
                    for i in range(2)]
            eb32 = [ae(nc.sbuf_tensor(f"eb32{i}", [P, 512], F32))
                    for i in range(2)]
            ovst = [ae(nc.sbuf_tensor(f"ovst{i}", [P, 512], F32))
                    for i in range(2)]

            QT5_W = [512, 512, 512, 512, 128]
            QT5_T0 = [0, 4, 8, 12, 16]
            h_hp_end = []

            def emit_scores(kc, hp, W, kb, lb):
                pa = psum[2 + (kc % 2)]
                pb = psum[4 + (kc % 2)]
                kcs = slice(kc * P, (kc + 1) * P)
                for (st_, mv, first, last) in (
                        (kb, qbh, True, False), (kb, qbl, False, False),
                        (lb, qbh, False, True)):
                    em.op("tensor", lambda t, st_=st_, mv=mv, kcs=kcs, hp=hp,
                          W=W, pa=pa, first=first, last=last: t.matmul(
                              pa[0:P, 0:W], st_[0:64, kcs],
                              mv[0:64, hp * 512:hp * 512 + W],
                              start=first, stop=last),
                          "tS" if last else None)
                    em.op("tensor", lambda t, st_=st_, mv=mv, kcs=kcs, hp=hp,
                          W=W, pb=pb, first=first, last=last: t.matmul(
                              pb[0:P, 0:W], st_[64:P, kcs],
                              mv[64:P, hp * 512:hp * 512 + W],
                              start=first, stop=last,
                              tile_position=(64, 0)),
                          "tS" if last else None)

            def emit_av(kc, W, vb, bA, bB):
                first = (kc == 0)
                last = (kc == NT_STAT - 1)
                c0 = kc * 130
                ea, eb_ = ea32[kc % 2], eb32[kc % 2]
                em.op("tensor", lambda t, c0=c0, W=W, bA=bA, ea=ea,
                      first=first, last=last: t.matmul(
                          bA[0:65, 0:W], vb[:, c0:c0 + 65], ea[:, 0:W],
                          start=first, stop=last), "tAv")
                em.op("tensor", lambda t, c0=c0, W=W, bB=bB, eb_=eb_,
                      first=first, last=last: t.matmul(
                          bB[0:65, 0:W], vb[:, c0 + 65:c0 + 130],
                          eb_[:, 0:W], start=first, stop=last), "tAv")

            for qt in range(5):
                W = QT5_W[qt]
                off = qt * 512 if qt < 4 else 2048
                if h_hp_end:
                    em.wait("sync", "tAv", h_hp_end[-1])
                for c in range(DC):
                    em.dma("sync", lambda sy, c=c, off=off, W=W: sy.dma_start(
                        out=qbh[:, c * 512:c * 512 + W],
                        in_=QH[c * P:(c + 1) * P, off:off + W]), "dqb")
                    em.dma("sync", lambda sy, c=c, off=off, W=W: sy.dma_start(
                        out=qbl[:, c * 512:c * 512 + W],
                        in_=QL[c * P:(c + 1) * P, off:off + W]), "dqb")
                for hp in range(DC):
                    kb, lb, vb = kbh[hp % 2], kbl[hp % 2], vbuf[hp % 2]
                    bA = psum[0] if hp % 2 == 0 else psum[6]
                    bB = psum[1] if hp % 2 == 0 else psum[7]
                    if len(h_hp_end) >= 2:
                        em.wait("sync", "tAv", h_hp_end[-2])
                    em.dma("sync", lambda sy, kb=kb, hp=hp: sy.dma_start(
                        out=kb[:], in_=KH[hp * P:(hp + 1) * P, :]), "dkb")
                    em.dma("sync", lambda sy, lb=lb, hp=hp: sy.dma_start(
                        out=lb[:], in_=KL[hp * P:(hp + 1) * P, :]), "dkb")
                    em.dma("sync", lambda sy, vb=vb, hp=hp: sy.dma_start(
                        out=vb[:], in_=VSP[hp, :, :]), "dkb")
                    em.wait("tensor", "dkb")
                    em.wait("tensor", "dqb")
                    em.wait("tensor", "vO")
                    em.wait("tensor", "tOv")
                    em.wait("tensor", "scE")
                    h_scE, h_tAv = [], []
                    for kc in range(NT_STAT):
                        if kc >= 2:
                            em.wait("tensor", "scE", h_scE[kc - 2])
                        emit_scores(kc, hp, W, kb, lb)
                        tS_after = em.sem("tS").n
                        if kc >= 1:
                            em.wait("tensor", "scE", h_scE[kc - 1])
                            emit_av(kc - 1, W, vb, bA, bB)
                            h_tAv.append(em.sem("tAv").n)
                        pa = psum[2 + (kc % 2)]
                        pb = psum[4 + (kc % 2)]
                        ea, eb_ = ea32[kc % 2], eb32[kc % 2]
                        em.wait("scalar", "tS", tS_after)
                        if kc == 0:
                            em.wait("scalar", "tAv")
                        elif kc >= 2:
                            em.wait("scalar", "tAv", h_tAv[kc - 2])
                        em.op("scalar", lambda sc, pa=pa, ea=ea, kc=kc, W=W:
                              sc.activation(
                                  ea[:, 0:W], pa[:, 0:W], AF.Exp,
                                  bias=glob[:, 3:4],
                                  scale=kescl[:, kc:kc + 1]), "scE")
                        em.op("scalar", lambda sc, pb=pb, eb_=eb_, kc=kc, W=W:
                              sc.activation(
                                  eb_[:, 0:W], pb[:, 0:W], AF.Exp,
                                  bias=glob[:, 3:4],
                                  scale=kescl[:, kc:kc + 1]), "scE")
                        h_scE.append(em.sem("scE").n)
                    em.wait("tensor", "scE", h_scE[NT_STAT - 1])
                    emit_av(NT_STAT - 1, W, vb, bA, bB)
                    h_hp_end.append(em.sem("tAv").n)
                    nkk = W // P
                    em.wait("vector", "tAv")
                    em.wait("vector", "tOv")
                    for (hx, bk) in ((0, bA), (1, bB)):
                        ov = ovst[hx]
                        em.op("vector", lambda v, ov=ov, bk=bk, W=W:
                              v.tensor_copy(ov[0:65, 0:W],
                                            bk[0:65, 0:W]), "vO")
                    em.wait("tensor", "vO")
                    for (hx, bk) in ((0, bA), (1, bB)):
                        ov = ovst[hx]
                        for kk in range(nkk):
                            em.op("tensor", lambda t, ov=ov, bk=bk, kk=kk:
                                  t.transpose(bk[:, kk * P:kk * P + 65],
                                              ov[0:65, kk * P:(kk + 1) * P],
                                              ident32[0:65, 0:65]), "tOv")
                    nL = 2 * nkk
                    em.wait("vector", "tOv")
                    for (hx, bk) in ((0, bA), (1, bB)):
                        for kk in range(nkk):
                            idx = hx * nkk + kk
                            em.op("vector", lambda v, bk=bk, kk=kk, idx=idx:
                                  v.tensor_copy(scr[:, idx:idx + 1],
                                                bk[:, kk * P + 64:
                                                    kk * P + 65]), "vO")
                    em.wait("vector", "vO")
                    em.op("vector", lambda v, nL=nL: v.reciprocal(
                        scr2[:, 0:nL], scr[:, 0:nL]), "vO")
                    em.wait("vector", "vO")
                    em.op("vector", lambda v, nL=nL: v.tensor_tensor(
                        out=scr[:, 16:16 + nL], in0=scr[:, 0:nL],
                        in1=scr2[:, 0:nL], op=ALU.mult), "vO")
                    em.wait("vector", "vO")
                    em.op("vector", lambda v, nL=nL: v.tensor_scalar(
                        out=scr[:, 16:16 + nL], in0=scr[:, 16:16 + nL],
                        scalar1=-1.0, scalar2=2.0, op0=ALU.mult,
                        op1=ALU.add), "vO")
                    em.wait("vector", "vO")
                    em.op("vector", lambda v, nL=nL: v.tensor_tensor(
                        out=scr2[:, 0:nL], in0=scr2[:, 0:nL],
                        in1=scr[:, 16:16 + nL], op=ALU.mult), "vO")
                    em.wait("vector", "vO")
                    for (hx, bk) in ((0, bA), (1, bB)):
                        for kk in range(nkk):
                            idx = hx * nkk + kk
                            tile = QT5_T0[qt] + kk
                            hcol = tile * D + (hp * 2 + hx) * HD
                            em.op("vector", lambda v, bk=bk, kk=kk, idx=idx,
                                  hcol=hcol: v.tensor_scalar_mul(
                                      attnout[:, hcol:hcol + HD],
                                      bk[:, kk * P:kk * P + HD],
                                      scr2[:, idx:idx + 1]), "vO")

            if DEBUG:
                em.wait("sync", "vO")
                for tile in range(17):
                    em.dma("sync", lambda sy, tile=tile: sy.dma_start(
                        out=dbg["AO"][tile * P:(tile + 1) * P, :],
                        in_=attnout[:, tile * D:(tile + 1) * D]), "dst")

        open_scopes = []
        if BUILD_PHASES >= 4:
            em.phase_barrier()
            at_es.close()
            p4 = ExitStack()
            open_scopes += [ao_es, p4]
            pe4 = p4.enter_context
            koT = pe4(nc.sbuf_tensor("koT", [P, DC * TQ], F16))
            ternO = pe4(nc.sbuf_tensor("ternO", [P, DC * D], F16))
            wbuf4 = [pe4(nc.sbuf_tensor(f"wbuf4{i}", [P, D], F32))
                     for i in range(2)]
            ki32b = pe4(nc.sbuf_tensor("ki32b", [P, D], I32))
            ebb = pe4(nc.sbuf_tensor("ebb", [P, D], F32))
            cbb = pe4(nc.sbuf_tensor("cbb", [P, D], F32))
            dbb = pe4(nc.sbuf_tensor("dbb", [P, D], F32))
            k1bb = pe4(nc.sbuf_tensor("k1bb", [P, D], F32))
            ybb = pe4(nc.sbuf_tensor("ybb", [P, D], F32))
            kfOb = pe4(nc.sbuf_tensor("kfOb", [P, D], F16))
            ostgT = pe4(nc.sbuf_tensor("ostgT", [P, DC * 512], F32))
            o4b = attnout
            oqmx = pe4(nc.sbuf_tensor("oqmx", [P, 17], F32))
            oqmn = pe4(nc.sbuf_tensor("oqmn", [P, 17], F32))
            osx = pe4(nc.sbuf_tensor("osx", [P, 17], F32))
            xst4 = pe4(nc.sbuf_tensor("xst4", [P, 8], F32))
            xss4 = pe4(nc.sbuf_tensor("xss4", [P, 8], F32))
            ccis = pe4(nc.sbuf_tensor("ccis", [1, 8], F32))
            ccos = pe4(nc.sbuf_tensor("ccos", [1, 8], F32))
            T4 = dict(ki32=ki32b, eb=ebb, cb=cbb, db=dbb, k1b=k1bb)

            em.wait("vector", "v4")
            em.wait("scalar", "sc4")
            for i in range(NT_OWN):
                em.op("vector", lambda v, i=i: v.tensor_reduce(
                    out=oqmx[:, i:i + 1], in_=attnout[:, i * D:(i + 1) * D],
                    axis=AX.X, op=ALU.add), "v4")
                if i:
                    em.wait("scalar", "sc4")
                em.op("scalar", lambda sc, i=i: sc.activation(
                    scr2[:], attnout[:, i * D:(i + 1) * D], AF.Square,
                    bias=0.0, scale=1.0, accum_out=oqmn[:, i:i + 1]), "sc4")
            em.wait("vector", "v4")
            em.wait("vector", "sc4")
            em.op("vector", lambda v: v.tensor_reduce(
                out=tl1[:], in_=oqmx[:, 0:NT_OWN], axis=AX.X, op=ALU.add),
                "v4")
            em.op("vector", lambda v: v.tensor_reduce(
                out=tl2[:], in_=oqmn[:, 0:NT_OWN], axis=AX.X, op=ALU.add),
                "v4")
            cross_part_reduce(tl1[:], ccis[0:1, 0:1], ALU.add, psum[6])
            cross_part_reduce(tl2[:], ccis[0:1, 1:2], ALU.add, psum[6])
            em.wait("vector", "vP")
            em.op("vector", lambda v: v.memset(ccis[0:1, 2:8], 0.0), "v4")
            em.wait("gpsimd", "v4")
            em.wait("gpsimd", "vP")
            em.op("gpsimd", lambda g: g.dma_start(
                out=CCI.ap(), in_=ccis[:]), "dcc", 16)
            em.wait("gpsimd", "dcc")
            em.op("gpsimd", lambda g: g.collective_compute(
                "AllReduce", ALU.add,
                replica_groups=[[0, 1], [2, 3], [4, 5], [6, 7]],
                ins=[CCI.ap().opt()], outs=[CCO.ap().opt()]), "gcc", 1)
            em.wait("gpsimd", "gcc")
            em.op("gpsimd", lambda g: g.dma_start(
                out=ccos[:], in_=CCO.ap()), "dcc", 16)
            em.wait("vector", "dcc")
            em.op("vector", lambda v: v.tensor_scalar_mul(
                ccos[0:1, 2:4], ccos[0:1, 0:2], 1.0 / (S * D)), "v4")
            em.wait("vector", "v4")
            bcast_col(ccos[0:1, 2:3], glob[:, 9:10], psum[6])
            bcast_col(ccos[0:1, 3:4], glob[:, 10:11], psum[6])
            em.wait("vector", "vP")
            em.op("vector", lambda v: v.tensor_tensor(
                out=tl1[:], in0=glob[:, 9:10], in1=glob[:, 9:10],
                op=ALU.mult), "v4")
            em.wait("vector", "v4")
            em.op("vector", lambda v: v.tensor_tensor(
                out=tl2[:], in0=glob[:, 10:11], in1=tl1[:], op=ALU.subtract),
                "v4")
            em.wait("vector", "v4")
            em.op("vector", lambda v: v.tensor_scalar_add(
                tl1[:], tl2[:], EPS), "v4")
            em.wait("scalar", "v4")
            em.op("scalar", lambda sc: sc.activation(
                tl2[:], tl1[:], AF.Sqrt, bias=0.0, scale=1.0), "sc4")
            em.wait("vector", "sc4")
            em.op("vector", lambda v: v.reciprocal(tl3[:], tl2[:]), "v4")
            for _ in range(2):
                em.wait("vector", "v4")
                em.op("vector", lambda v: v.tensor_tensor(
                    out=tl4[:], in0=tl3[:], in1=tl3[:], op=ALU.mult), "v4")
                em.wait("vector", "v4")
                em.op("vector", lambda v: v.tensor_tensor(
                    out=tl4[:], in0=tl4[:], in1=tl1[:], op=ALU.mult), "v4")
                em.wait("vector", "v4")
                em.op("vector", lambda v: v.tensor_scalar(
                    out=tl4[:], in0=tl4[:], scalar1=-0.5, scalar2=1.5,
                    op0=ALU.mult, op1=ALU.add), "v4")
                em.wait("vector", "v4")
                em.op("vector", lambda v: v.tensor_tensor(
                    out=tl3[:], in0=tl3[:], in1=tl4[:], op=ALU.mult), "v4")
            em.wait("vector", "v4")
            em.op("vector", lambda v: v.tensor_copy(glob[:, 11:12], tl3[:]),
                  "v4")
            em.op("vector", lambda v: v.tensor_scalar_mul(
                glob[:, 12:13], glob[:, 9:10], -1.0), "v4")

            em.wait("vector", "v4")
            for i in range(17):
                sl = slice(i * D, (i + 1) * D)
                em.op("vector", lambda v, sl=sl, i=i: v.tensor_reduce(
                    out=oqmx[:, i:i + 1], in_=attnout[:, sl], axis=AX.X,
                    op=ALU.max), "v4")
                em.op("vector", lambda v, sl=sl, i=i: v.tensor_reduce(
                    out=oqmn[:, i:i + 1], in_=attnout[:, sl], axis=AX.X,
                    op=ALU.min), "v4")
            NCo = 17
            em.wait("vector", "v4")
            em.op("vector", lambda v: v.tensor_scalar(
                out=oqmx[:, 0:NCo], in0=oqmx[:, 0:NCo],
                scalar1=glob[:, 12:13], scalar2=glob[:, 11:12],
                op0=ALU.add, op1=ALU.mult), "v4")
            em.op("vector", lambda v: v.tensor_scalar(
                out=oqmn[:, 0:NCo], in0=oqmn[:, 0:NCo], scalar1=-1.0,
                scalar2=glob[:, 9:10], op0=ALU.mult, op1=ALU.add), "v4")
            em.wait("vector", "v4")
            em.op("vector", lambda v: v.tensor_scalar_mul(
                oqmn[:, 0:NCo], oqmn[:, 0:NCo], glob[:, 11:12]), "v4")
            em.wait("vector", "v4")
            em.op("vector", lambda v: v.tensor_tensor(
                out=oqmx[:, 0:NCo], in0=oqmx[:, 0:NCo], in1=oqmn[:, 0:NCo],
                op=ALU.max), "v4")
            em.wait("vector", "v4")
            em.op("vector", lambda v: v.tensor_scalar_max(
                oqmx[:, 0:NCo], oqmx[:, 0:NCo], 1e-5), "v4")
            em.wait("vector", "v4")
            em.op("vector", lambda v: v.tensor_scalar_mul(
                oqscl[:, 0:NCo], oqmx[:, 0:NCo], 1.0 / 127.0), "v4")
            em.op("vector", lambda v: v.reciprocal(osx[:, 0:NCo],
                                                   oqmx[:, 0:NCo]), "v4")
            em.wait("vector", "v4")
            em.op("vector", lambda v: v.tensor_tensor(
                out=oqmn[:, 0:NCo], in0=oqmx[:, 0:NCo], in1=osx[:, 0:NCo],
                op=ALU.mult), "v4")
            em.wait("vector", "v4")
            em.op("vector", lambda v: v.tensor_scalar(
                out=oqmn[:, 0:NCo], in0=oqmn[:, 0:NCo], scalar1=-1.0,
                scalar2=2.0, op0=ALU.mult, op1=ALU.add), "v4")
            em.wait("vector", "v4")
            em.op("vector", lambda v: v.tensor_tensor(
                out=osx[:, 0:NCo], in0=osx[:, 0:NCo], in1=oqmn[:, 0:NCo],
                op=ALU.mult), "v4")
            em.wait("vector", "v4")
            em.op("vector", lambda v: v.tensor_scalar_mul(
                osx[:, 0:NCo], osx[:, 0:NCo], 127.0), "v4")
            em.wait("vector", "v4")
            em.op("vector", lambda v: v.tensor_scalar_mul(
                osx[:, 0:NCo], osx[:, 0:NCo], glob[:, 11:12]), "v4")
            for i in range(17):
                sl = slice(i * D, (i + 1) * D)
                em.wait("vector", "v4")
                em.op("vector", lambda v, sl=sl, i=i: v.tensor_scalar(
                    out=ybb[:], in0=attnout[:, sl], scalar1=glob[:, 12:13],
                    scalar2=osx[:, i:i + 1], op0=ALU.add, op1=ALU.mult),
                    "v4")
                emit_round2("v4", ybb, kfOb, T4, extra_waits=("t4T",))
                em.wait("tensor", "v4")
                em.wait("tensor", "sc4")
                pts4 = [psum[c % 4][:, (c // 4) * 64:(c // 4) * 64 + 64]
                        .bitcast(F16) for c in range(DC)]
                for c in range(DC):
                    em.op("tensor", lambda t, c=c, pts4=pts4: t.transpose(
                        pts4[c], kfOb[:, c * P:(c + 1) * P], ident16[:]),
                        "t4T")
                em.wait("scalar", "t4T")
                for c in range(DC):
                    em.op("scalar", lambda sc, c=c, i=i, pts4=pts4:
                          sc.activation(
                              koT[:, c * TQ + i * P:c * TQ + (i + 1) * P],
                              pts4[c], AF.Identity, bias=0.0, scale=1.0),
                          "sc4")

            ternarize_t(WOT, 7, ternO, wbuf4, ybb, T4)
            em.wait("vector", "vP")
            em.wait("vector", "v4")
            for i in range(17):
                em.op("vector", lambda v, i=i: v.tensor_tensor(
                    out=oqscl[:, i:i + 1], in0=oqscl[:, i:i + 1],
                    in1=glob[:, 7:8], op=ALU.mult), "v4")
            for tt in range(5):
                W4 = QT5_W[tt]
                off4 = QT5_T0[tt] * P
                nkk = W4 // P
                em.wait("tensor", "vP")
                em.wait("tensor", "v4")
                em.wait("tensor", "sc4")
                for oc in range(DC):
                    bank = psum[oc % 2]
                    em.wait("tensor", "vOs")
                    for c in range(DC):
                        em.op("tensor", lambda t, oc=oc, c=c, W4=W4, off4=off4,
                              bank=bank: t.matmul(
                                  bank[:, 0:W4],
                                  ternO[:, c * D + oc * P:c * D + oc * P + P],
                                  koT[:, c * TQ + off4:c * TQ + off4 + W4],
                                  start=(c == 0), stop=(c == DC - 1)),
                              "t4M" if c == DC - 1 else None)
                    em.wait("vector", "t4M")
                    em.op("vector", lambda v, oc=oc, W4=W4, bank=bank:
                          v.tensor_copy(ostgT[:, oc * 512:oc * 512 + W4],
                                        bank[:, 0:W4]), "vOs")
                for kk in range(nkk):
                    i = QT5_T0[tt] + kk
                    em.wait("tensor", "vOs")
                    em.wait("tensor", "vX")
                    for c in range(DC):
                        pb4 = psum[4 + (c % 2)][:, (c // 2 % 2) * P:
                                                (c // 2 % 2) * P + P]
                        em.wait("tensor", "vX")
                        em.op("tensor", lambda t, c=c, kk=kk, pb4=pb4:
                              t.transpose(
                                  pb4, ostgT[:, c * 512 + kk * P:
                                             c * 512 + (kk + 1) * P],
                                  ident32[:]), "t4X")
                        em.wait("vector", "t4X")
                        em.op("vector", lambda v, c=c, i=i, kk=kk, pb4=pb4:
                              v.tensor_scalar_mul(
                                  o4b[:, kk * D + c * P:kk * D + (c + 1) * P],
                                  pb4, oqscl[:, i:i + 1]), "vX")
                    okk = slice(kk * D, (kk + 1) * D)
                    em.wait("vector", "vX")
                    em.op("vector", lambda v, okk=okk, kk=kk: v.tensor_reduce(
                        out=xst4[:, kk:kk + 1], in_=o4b[:, okk], axis=AX.X,
                        op=ALU.add, negate=True), "vX")
                    em.wait("vector", "vX")
                    em.op("vector", lambda v, kk=kk: v.tensor_scalar_mul(
                        xst4[:, kk:kk + 1], xst4[:, kk:kk + 1], 1.0 / D),
                        "vX")
                    em.wait("scalar", "vX")
                    em.wait("scalar", "sc4")
                    em.op("scalar", lambda sc, okk=okk, kk=kk: sc.activation(
                        scr2[:], o4b[:, okk], AF.Square,
                        bias=xst4[:, kk:kk + 1], scale=1.0,
                        accum_out=xss4[:, kk:kk + 1]), "sc4")
                em.wait("vector", "sc4")
                em.op("vector", lambda v, nkk=nkk: v.tensor_scalar(
                    out=xss4[:, 0:nkk], in0=xss4[:, 0:nkk],
                    scalar1=1.0 / D, scalar2=EPS, op0=ALU.mult,
                    op1=ALU.add), "vX")
                em.wait("scalar", "vX")
                em.op("scalar", lambda sc, nkk=nkk: sc.activation(
                    xss4[:, 4:4 + nkk], xss4[:, 0:nkk], AF.Sqrt, bias=0.0,
                    scale=1.0), "sc4")
                em.wait("vector", "sc4")
                em.op("vector", lambda v, nkk=nkk: v.reciprocal(
                    xst4[:, 4:4 + nkk], xss4[:, 4:4 + nkk]), "vX")
                for _ in range(2):
                    em.wait("vector", "vX")
                    em.op("vector", lambda v, nkk=nkk: v.tensor_tensor(
                        out=oqmn[:, 0:nkk], in0=xst4[:, 4:4 + nkk],
                        in1=xst4[:, 4:4 + nkk], op=ALU.mult), "vX")
                    em.wait("vector", "vX")
                    em.op("vector", lambda v, nkk=nkk: v.tensor_tensor(
                        out=oqmn[:, 0:nkk], in0=oqmn[:, 0:nkk],
                        in1=xss4[:, 0:nkk], op=ALU.mult), "vX")
                    em.wait("vector", "vX")
                    em.op("vector", lambda v, nkk=nkk: v.tensor_scalar(
                        out=oqmn[:, 0:nkk], in0=oqmn[:, 0:nkk],
                        scalar1=-0.5, scalar2=1.5, op0=ALU.mult,
                        op1=ALU.add), "vX")
                    em.wait("vector", "vX")
                    em.op("vector", lambda v, nkk=nkk: v.tensor_tensor(
                        out=xst4[:, 4:4 + nkk], in0=xst4[:, 4:4 + nkk],
                        in1=oqmn[:, 0:nkk], op=ALU.mult), "vX")
                for kk in range(nkk):
                    i = QT5_T0[tt] + kk
                    okk = slice(kk * D, (kk + 1) * D)
                    xrow = i * P if i < 16 else 4096
                    xb4 = xa[kk % 2]
                    em.wait("sync", "vX")
                    em.dma("sync", lambda sy, xrow=xrow, xb4=xb4:
                           sy.dma_start(
                               out=xb4[:], in_=XC[xrow:xrow + P, :]), "dx4")
                    em.wait("vector", "dx4")
                    em.wait("vector", "vX")
                    em.op("vector", lambda v, okk=okk, kk=kk: v.tensor_scalar(
                        out=o4b[:, okk], in0=o4b[:, okk],
                        scalar1=xst4[:, kk:kk + 1],
                        scalar2=xst4[:, 4 + kk:5 + kk],
                        op0=ALU.add, op1=ALU.mult), "vX")
                    em.wait("vector", "vX")
                    em.wait("vector", "dx1")
                    if DEBUG:
                        em.wait("vector", "dst")
                    dst = x1x if i == 16 else ybb
                    em.op("vector", lambda v, dst=dst, okk=okk, xb4=xb4:
                          v.tensor_tensor(
                              out=dst[:], in0=xb4[:], in1=o4b[:, okk],
                              op=ALU.add), "vX")
                    if i < 16:
                        em.wait("sync", "vX")
                        em.dma("sync", lambda sy, i=i: sy.dma_start(
                            out=X1SP[i * P:(i + 1) * P, :], in_=ybb[:]),
                            "dx1")
                    if DEBUG:
                        em.wait("sync", "vX")
                        em.dma("sync", lambda sy, i=i, dst=dst: sy.dma_start(
                            out=dbg["X1"][i * P:(i + 1) * P, :], in_=dst[:]),
                            "dst")

        if BUILD_PHASES >= 5:
            em.phase_barrier()
            p4.close()
            ao_es.close()
            open_scopes = []
            p5 = ExitStack()
            pe5 = p5.enter_context
            open_scopes.append(p5)
            lor2h = pe5(nc.sbuf_tensor("lor2h", [1, 4 * D], F16))
            lor2l = pe5(nc.sbuf_tensor("lor2l", [1, 4 * D], F16))
            x1t = pe5(nc.sbuf_tensor("x1t", [P, D], F32))
            h2x = x1t
            h2t32 = scr
            h2Th = pe5(nc.sbuf_tensor("h2Th", [P, DC * 512], F16))
            h2Tl = pe5(nc.sbuf_tensor("h2Tl", [P, DC * 512], F16))
            hTh = pe5(nc.sbuf_tensor("hTh", [P, HFC * 512], F16))
            hTl = pe5(nc.sbuf_tensor("hTl", [P, HFC * 512], F16))
            wfh = [pe5(nc.sbuf_tensor(f"wfh{i}", [P, 8 * P], F16))
                   for i in range(2)]
            wfl = [pe5(nc.sbuf_tensor(f"wfl{i}", [P, 8 * P], F16))
                   for i in range(2)]
            wf32 = xa

            prowh = pe5(nc.sbuf_tensor("prowh", [1, 4 * 512], F16))
            prowl = pe5(nc.sbuf_tensor("prowl", [1, 4 * 512], F16))
            rrow32 = pe5(nc.sbuf_tensor("rrow32", [4, 512], F32))
            rrowf = pe5(nc.sbuf_tensor("rrowf", [1, 2048], F32))
            prowf = rrowf
            rrowh = pe5(nc.sbuf_tensor("rrowh", [1, 4 * 2048], F16))
            rrowl = pe5(nc.sbuf_tensor("rrowl", [1, 4 * 2048], F16))
            outsT = pe5(nc.sbuf_tensor("outsT", [P, DC * 512], F32))
            outf = pe5(nc.sbuf_tensor("outf", [P, D], F32))
            fst5 = pe5(nc.sbuf_tensor("fst5", [P, 8], F32))
            fss5 = pe5(nc.sbuf_tensor("fss5", [P, 8], F32))
            ftm5 = pe5(nc.sbuf_tensor("ftm5", [P, 4], F32))
            hact2 = [outf[:, 0:512], outf[:, 512:1024]]
            sclb = pe5(nc.sbuf_tensor("sclb", [P, 1], F32))
            pv16a5 = pe5(nc.sbuf_tensor("pv16a5", [P, 8 * P], F16))
            pv16b5 = pe5(nc.sbuf_tensor("pv16b5", [P, 8 * P], F16))
            lor16h = pv16a5
            lor16l = pv16b5
            b1t_sb = pe5(nc.sbuf_tensor("b1t_sb", [P, HFC], F32))
            b2t_sb = pe5(nc.sbuf_tensor("b2t_sb", [P, DC], F32))
            em.dma("sync", lambda sy: sy.dma_start(out=b1t_sb[:], in_=B1T),
                   "dl5")
            em.dma("sync", lambda sy: sy.dma_start(out=b2t_sb[:], in_=B2T),
                   "dl5")

            def ln_row(src_buf, dst_buf, vsem, scsem):
                em.wait("vector", vsem)
                em.op("vector", lambda v: v.tensor_reduce(
                    out=tl1[:], in_=src_buf[:], axis=AX.X, op=ALU.add,
                    negate=True), vsem)
                em.wait("vector", vsem)
                em.op("vector", lambda v: v.tensor_scalar_mul(
                    tl1[:], tl1[:], 1.0 / D), vsem)
                em.wait("scalar", vsem)
                em.wait("scalar", scsem)
                em.op("scalar", lambda sc: sc.activation(
                    scr2[:], src_buf[:], AF.Square, bias=tl1[:], scale=1.0,
                    accum_out=tl2[:]), scsem)
                em.wait("vector", scsem)
                em.op("vector", lambda v: v.tensor_scalar(
                    out=tl2[:], in0=tl2[:], scalar1=1.0 / D, scalar2=EPS,
                    op0=ALU.mult, op1=ALU.add), vsem)
                em.wait("scalar", vsem)
                em.op("scalar", lambda sc: sc.activation(
                    tl3[:], tl2[:], AF.Sqrt, bias=0.0, scale=1.0), scsem)
                em.wait("vector", scsem)
                em.op("vector", lambda v: v.reciprocal(tl4[:], tl3[:]), vsem)
                for _ in range(2):
                    em.wait("vector", vsem)
                    em.op("vector", lambda v: v.tensor_tensor(
                        out=tl3[:], in0=tl4[:], in1=tl4[:], op=ALU.mult),
                        vsem)
                    em.wait("vector", vsem)
                    em.op("vector", lambda v: v.tensor_tensor(
                        out=tl3[:], in0=tl3[:], in1=tl2[:], op=ALU.mult),
                        vsem)
                    em.wait("vector", vsem)
                    em.op("vector", lambda v: v.tensor_scalar(
                        out=tl3[:], in0=tl3[:], scalar1=-0.5, scalar2=1.5,
                        op0=ALU.mult, op1=ALU.add), vsem)
                    em.wait("vector", vsem)
                    em.op("vector", lambda v: v.tensor_tensor(
                        out=tl4[:], in0=tl4[:], in1=tl3[:], op=ALU.mult),
                        vsem)
                em.wait("vector", vsem)
                em.op("vector", lambda v: v.tensor_scalar(
                    out=dst_buf[:], in0=src_buf[:], scalar1=tl1[:],
                    scalar2=tl4[:], op0=ALU.add, op1=ALU.mult), vsem)

            ln_row(x1x, h2x, "v5", "sc5")
            em.wait("sync", "v5")
            em.wait("sync", "v5w")
            em.dma("sync", lambda sy: sy.dma_start(
                out=scr[0:16, :], in_=h2x[1:17, :]), "dl5")
            em.wait("tensor", "dl5")
            em.wait("tensor", "v5T")
            for c in range(DC):
                em.op("tensor", lambda t, c=c: t.transpose(
                    psum[6][:, c * 16:(c + 1) * 16],
                    scr[0:16, c * P:(c + 1) * P], ident32[0:16, 0:16]),
                    "t5L")
            em.wait("vector", "t5L")
            em.op("vector", lambda v: v.tensor_copy(lorT[:], psum[6][:, 0:P]),
                  "v5")
            em.wait("vector", "v5")
            em.op("vector", lambda v: v.tensor_copy(lorTh[:], lorT[:]), "v5")
            em.wait("vector", "v5")
            em.op("vector", lambda v: v.scalar_tensor_tensor(
                out=lorTl[:], in0=lorT[:], scalar=1.0, in1=lorTh[:],
                op0=ALU.mult, op1=ALU.subtract), "v5")
            em.wait("vector", "dl5")
            em.wait("vector", "v5w")
            em.wait("vector", "dw5")
            em.op("vector", lambda v: v.tensor_copy(
                lor16h[0:16, :], scr[0:16, :]), "v5")
            em.wait("vector", "v5")
            em.op("vector", lambda v: v.scalar_tensor_tensor(
                out=lor16l[0:16, :], in0=scr[0:16, :], scalar=1.0,
                in1=lor16h[0:16, :], op0=ALU.mult, op1=ALU.subtract), "v5")
            em.wait("sync", "v5")
            em.dma("sync", lambda sy: sy.dma_start(
                out=LRSPH, in_=lor16h[0:16:2, :]), "dl5")
            em.dma("sync", lambda sy: sy.dma_start(
                out=LRSPL, in_=lor16l[0:16:2, :]), "dl5")
            em.wait("sync", "dl5")

            def load_lor2(r0):
                em.wait("sync", "t5M")
                em.wait("sync", "t5W")
                em.wait("sync", "dl5")
                em.dma("sync", lambda sy, r0=r0: sy.dma_start(
                    out=lor2h[:], in_=LRSPH[r0:r0 + 4, :]
                    .rearrange("r d -> (r d)").unsqueeze(0)), "dl5")
                em.dma("sync", lambda sy, r0=r0: sy.dma_start(
                    out=lor2l[:], in_=LRSPL[r0:r0 + 4, :]
                    .rearrange("r d -> (r d)").unsqueeze(0)), "dl5")

            load_lor2(0)
            em.dma("sync", lambda sy: sy.dma_start(
                out=sclb[0:1, :], in_=SCL), "dl5")
            bcast_col(sclb[0:1, :], sclb[:], psum[6])

            em.wait("scalar", "dl5")
            x1t4 = hTh[:, 8192:16384].bitcast(F32)
            for sg in range(4):
                em.wait("sync", "v5T")
                em.wait("sync", "dl5")
                em.wait("sync", "v5g")
                em.wait("sync", "t5q")
                em.wait("sync", "t5W")
                em.wait("sync", "t5r")
                em.dma("sync", lambda sy, sg=sg: sy.dma_start(
                    out=x1t4[:].rearrange("p (k d) -> p k d", k=4),
                    in_=X1SP[sg * 512:(sg + 1) * 512, :]
                    .rearrange("(k p) d -> p k d", p=P)), "dx5")
                em.wait("vector", "dx5")
                em.wait("vector", "t5M")
                em.wait("vector", "t5p")
                em.wait("vector", "t5h")
                for kk in range(4):
                    em.op("vector", lambda v, kk=kk: v.tensor_reduce(
                        out=fst5[:, kk:kk + 1],
                        in_=x1t4[:, kk * D:(kk + 1) * D], axis=AX.X,
                        op=ALU.add, negate=True), "v5T")
                em.wait("vector", "v5T")
                em.op("vector", lambda v: v.tensor_scalar_mul(
                    fst5[:, 0:4], fst5[:, 0:4], 1.0 / D), "v5T")
                for kk in range(4):
                    em.wait("scalar", "v5T")
                    em.wait("scalar", "sc5T")
                    em.op("scalar", lambda sc, kk=kk: sc.activation(
                        scr2[:], x1t4[:, kk * D:(kk + 1) * D], AF.Square,
                        bias=fst5[:, kk:kk + 1], scale=1.0,
                        accum_out=fss5[:, kk:kk + 1]), "sc5T")
                em.wait("vector", "sc5T")
                em.op("vector", lambda v: v.tensor_scalar(
                    out=fss5[:, 0:4], in0=fss5[:, 0:4], scalar1=1.0 / D,
                    scalar2=EPS, op0=ALU.mult, op1=ALU.add), "v5T")
                em.wait("scalar", "v5T")
                em.op("scalar", lambda sc: sc.activation(
                    fss5[:, 4:8], fss5[:, 0:4], AF.Sqrt, bias=0.0,
                    scale=1.0), "sc5T")
                em.wait("vector", "sc5T")
                em.op("vector", lambda v: v.reciprocal(
                    fst5[:, 4:8], fss5[:, 4:8]), "v5T")
                for _ in range(2):
                    em.wait("vector", "v5T")
                    em.op("vector", lambda v: v.tensor_tensor(
                        out=ftm5[:, 0:4], in0=fst5[:, 4:8],
                        in1=fst5[:, 4:8], op=ALU.mult), "v5T")
                    em.wait("vector", "v5T")
                    em.op("vector", lambda v: v.tensor_tensor(
                        out=ftm5[:, 0:4], in0=ftm5[:, 0:4],
                        in1=fss5[:, 0:4], op=ALU.mult), "v5T")
                    em.wait("vector", "v5T")
                    em.op("vector", lambda v: v.tensor_scalar(
                        out=ftm5[:, 0:4], in0=ftm5[:, 0:4], scalar1=-0.5,
                        scalar2=1.5, op0=ALU.mult, op1=ALU.add), "v5T")
                    em.wait("vector", "v5T")
                    em.op("vector", lambda v: v.tensor_tensor(
                        out=fst5[:, 4:8], in0=fst5[:, 4:8],
                        in1=ftm5[:, 0:4], op=ALU.mult), "v5T")
                for kk in range(4):
                    em.wait("vector", "v5T")
                    em.wait("vector", "t5h")
                    em.op("vector", lambda v, kk=kk: v.tensor_scalar(
                        out=h2t32[:], in0=x1t4[:, kk * D:(kk + 1) * D],
                        scalar1=fst5[:, kk:kk + 1],
                        scalar2=fst5[:, 4 + kk:5 + kk],
                        op0=ALU.add, op1=ALU.mult), "v5T")
                    em.wait("tensor", "v5T")
                    em.wait("tensor", "v5h")
                    em.wait("tensor", "sc5h")
                    pts5 = [psum[c % 2][:, (c // 2) * P:(c // 2) * P + P]
                            for c in range(DC)]
                    for c in range(DC):
                        em.op("tensor", lambda t, c=c, pts5=pts5: t.transpose(
                            pts5[c], h2t32[:, c * P:(c + 1) * P],
                            ident32[:]), "t5h")
                    em.wait("scalar", "t5h")
                    em.wait("scalar", "v5h")
                    for c in range(DC):
                        em.op("scalar", lambda sc, c=c, kk=kk, pts5=pts5:
                              sc.activation(
                                  h2Th[:, c * 512 + kk * P:
                                       c * 512 + (kk + 1) * P],
                                  pts5[c], AF.Identity, bias=0.0,
                                  scale=1.0), "sc5h")
                    em.wait("vector", "t5h")
                    em.wait("vector", "sc5h")
                    for c in range(DC):
                        em.op("vector", lambda v, c=c, kk=kk, pts5=pts5:
                              v.scalar_tensor_tensor(
                                  out=h2Tl[:, c * 512 + kk * P:
                                           c * 512 + (kk + 1) * P],
                                  in0=pts5[c], scalar=1.0,
                                  in1=h2Th[:, c * 512 + kk * P:
                                           c * 512 + (kk + 1) * P],
                                  op0=ALU.mult, op1=ALU.subtract), "v5h")
                em.wait("tensor", "v5h")
                em.wait("tensor", "sc5h")
                em.wait("tensor", "v5p")
                for c in range(DC):
                    oh = lorTh[:, c * 16 + 1:c * 16 + 8:2]
                    ol = lorTl[:, c * 16 + 1:c * 16 + 8:2]
                    hh = h2Th[:, c * 512:(c + 1) * 512]
                    hl = h2Tl[:, c * 512:(c + 1) * 512]
                    for (sa, mb, first, last) in (
                            (oh, hh, c == 0, False), (oh, hl, False, False),
                            (ol, hh, False, c == DC - 1)):
                        em.op("tensor", lambda t, sa=sa, mb=mb, first=first,
                              last=last: t.matmul(
                                  psum[4][0:4, 0:512], sa, mb,
                                  start=first, stop=last),
                              "t5p" if last else None)
                em.wait("vector", "t5p")
                em.op("vector", lambda v: v.tensor_scalar_mul(
                    rrow32[0:4, :], psum[4][0:4, 0:512], sclb[0:4, :]), "v5p")
                em.wait("sync", "v5p")
                em.dma("sync", lambda sy: sy.dma_start(
                    out=PRSP, in_=rrow32[0:4, :]), "dp5")
                em.wait("sync", "dp5")
                em.dma("sync", lambda sy: sy.dma_start(
                    out=prowf[:], in_=PRSP.rearrange("r w -> (r w)")
                    .unsqueeze(0)), "dp5")
                em.wait("vector", "dp5")
                em.op("vector", lambda v: v.tensor_copy(prowh[:], prowf[:]),
                      "v5p")
                em.wait("vector", "v5p")
                em.op("vector", lambda v: v.scalar_tensor_tensor(
                    out=prowl[:], in0=prowf[:], scalar=1.0, in1=prowh[:],
                    op0=ALU.mult, op1=ALU.subtract), "v5p")
                load_lor2(0)
                em.wait("tensor", "dl5")
                em.wait("tensor", "v5p")
                em.wait("tensor", "v5h")
                em.wait("tensor", "sc5h")
                em.wait("scalar", "dout")
                h_dw = []
                h_v5s = []
                h_t5M = []
                h_g = []
                em.wait("sync", "v5u")
                em.wait("sync", "v5s")
                em.wait("gpsimd", "v5u")
                em.wait("gpsimd", "v5s")
                em.wait("tensor", "sc5o")

                def w1_dma(hc):
                    sem = "dw1a" if hc % 2 == 0 else "dw1b"
                    semg = sem + "g"
                    if hc >= 2:
                        em.wait("sync", "v5s", h_v5s[hc - 2])
                        em.wait("gpsimd", "v5s", h_v5s[hc - 2])
                    em.dma("sync", lambda sy, hc=hc: sy.dma_start(
                        out=wf32[hc % 2][:, 0:512],
                        in_=W1B[hc, :, 0:512]), sem)
                    em.dma("gpsimd", lambda g, hc=hc: g.dma_start(
                        out=wf32[hc % 2][:, 512:1024],
                        in_=W1B[hc, :, 512:1024]), semg)
                    h_dw.append((em.sem(sem).n, em.sem(semg).n))

                def emit_rrow(j):

                    em.wait("tensor", "v5g")
                    em.wait("tensor", "v5r")
                    em.wait("tensor", "vP")
                    for cc in range(8):
                        hcol = j * 8 + cc
                        oh = lorTh[:, cc * 16 + 9:cc * 16 + 16:2]
                        ol = lorTl[:, cc * 16 + 9:cc * 16 + 16:2]
                        hh = hTh[:, hcol * 512:(hcol + 1) * 512]
                        hl = hTl[:, hcol * 512:(hcol + 1) * 512]
                        for (sa, mb, first, last) in (
                                (oh, hh, cc == 0, False),
                                (oh, hl, False, False),
                                (ol, hh, False, cc == 7)):
                            em.op("tensor", lambda t, sa=sa, mb=mb,
                                  first=first, last=last: t.matmul(
                                      psum[7][0:4, 0:512], sa, mb,
                                      start=first, stop=last),
                                  "t5r" if last else None)
                    em.wait("vector", "t5r")
                    em.wait("vector", "dp5")
                    em.wait("vector", "dr5")
                    em.op("vector", lambda v: v.tensor_scalar_mul(
                        rrow32[0:4, :], psum[7][0:4, 0:512], sclb[0:4, :]),
                        "v5r")
                    em.wait("gpsimd", "v5r")
                    em.dma("gpsimd", lambda g: g.dma_start(
                        out=RRSP, in_=rrow32[0:4, :]), "dr5")
                    em.wait("gpsimd", "dr5")
                    em.wait("gpsimd", "v5r")
                    em.wait("gpsimd", "v5p")
                    em.dma("gpsimd", lambda g: g.dma_start(
                        out=rrowf[:], in_=RRSP.rearrange("r w -> (r w)")
                        .unsqueeze(0)), "dr5")
                    em.wait("vector", "dr5")
                    em.op("vector", lambda v, j=j: v.tensor_copy(
                        rrowh[0:1, j * 2048:(j + 1) * 2048], rrowf[:]),
                        "v5r")
                    em.wait("vector", "v5r")
                    em.op("vector", lambda v, j=j: v.scalar_tensor_tensor(
                        out=rrowl[0:1, j * 2048:(j + 1) * 2048],
                        in0=rrowf[:], scalar=1.0,
                        in1=rrowh[0:1, j * 2048:(j + 1) * 2048],
                        op0=ALU.mult, op1=ALU.subtract), "v5r")

                def w1_split(hc):
                    _s = "dw1a" if hc % 2 == 0 else "dw1b"
                    em.wait("vector", _s, h_dw[hc][0])
                    em.wait("vector", _s + "g", h_dw[hc][1])
                    if hc >= 2:
                        em.wait("vector", "t5M", h_t5M[hc - 2])
                    em.op("vector", lambda v, hc=hc: v.tensor_copy(
                        wfh[hc % 2][:], wf32[hc % 2][:]), "v5s")
                    em.wait("vector", "v5s")
                    em.op("vector", lambda v, hc=hc: v.scalar_tensor_tensor(
                        out=wfl[hc % 2][:], in0=wf32[hc % 2][:], scalar=1.0,
                        in1=wfh[hc % 2][:], op0=ALU.mult, op1=ALU.subtract),
                        "v5s")
                    h_v5s.append(em.sem("v5s").n)

                w1_dma(0)
                w1_dma(1)
                w1_split(0)
                for hc in range(HFC):
                    if hc + 2 < HFC:
                        w1_dma(hc + 2)
                    if hc + 1 < HFC:
                        w1_split(hc + 1)
                    bank = psum[5 + (hc % 2)]
                    wh, wl = wfh[hc % 2], wfl[hc % 2]
                    em.wait("tensor", "v5s", h_v5s[hc])
                    if len(h_g) >= 2:
                        em.wait("tensor", "sc5g", h_g[hc - 2])
                    i_blk = hc // 8
                    for c in range(DC):
                        whc = wh[:, c * P:(c + 1) * P]
                        wlc = wl[:, c * P:(c + 1) * P]
                        hh = h2Th[:, c * 512:(c + 1) * 512]
                        hl = h2Tl[:, c * 512:(c + 1) * 512]
                        for (pi, (sa, mb)) in enumerate(
                                ((whc, hh), (whc, hl), (wlc, hh))):
                            em.op("tensor", lambda t, sa=sa, mb=mb,
                                  bank=bank, first=(c == 0 and pi == 0):
                                  t.matmul(bank[:, 0:512], sa, mb,
                                           start=first, stop=False), None)
                    e0 = i_blk * D + (hc % 8) * P
                    p0_ = i_blk * 512
                    for (sa, mb, last) in (
                            (lor2h, prowh, False), (lor2h, prowl, False),
                            (lor2l, prowh, True)):
                        em.op("tensor", lambda t, sa=sa, mb=mb, e0=e0,
                              p0_=p0_, bank=bank, last=last: t.matmul(
                                  bank[:, 0:512],
                                  sa[0:1, e0:e0 + P],
                                  mb[0:1, p0_:p0_ + 512],
                                  start=False, stop=last),
                              "t5M" if last else None)
                    h_t5M.append(em.sem("t5M").n)
                    em.wait("scalar", "t5M", h_t5M[hc])
                    em.wait("scalar", "v5g")
                    em.op("scalar", lambda sc, hc=hc, bank=bank:
                          sc.activation(
                              hact2[hc % 2], bank[:, 0:512], AF.Gelu,
                              bias=b1t_sb[:, hc:hc + 1], scale=1.0), "sc5g")
                    h_g.append(em.sem("sc5g").n)
                    em.wait("vector", "sc5g", h_g[hc])
                    em.wait("vector", "t5r")
                    em.wait("vector", "t5q")
                    em.wait("vector", "t5W")
                    em.op("vector", lambda v, hc=hc: v.tensor_copy(
                        hTh[:, hc * 512:(hc + 1) * 512], hact2[hc % 2]),
                        "v5g")
                    em.wait("vector", "v5g")
                    em.op("vector", lambda v, hc=hc: v.scalar_tensor_tensor(
                        out=hTl[:, hc * 512:(hc + 1) * 512],
                        in0=hact2[hc % 2], scalar=1.0,
                        in1=hTh[:, hc * 512:(hc + 1) * 512],
                        op0=ALU.mult, op1=ALU.subtract), "v5g")
                    if hc % 8 == 7:
                        emit_rrow(hc // 8)
                load_lor2(4)
                em.wait("tensor", "dl5")
                h_dw2 = []
                h_v5u = []
                h_t5W = []
                h_t5q = []
                h_o = []
                em.wait("sync", "v5s")
                em.wait("sync", "v5u")
                em.wait("gpsimd", "v5s")
                em.wait("gpsimd", "v5u")
                em.wait("tensor", "sc5g")
                em.wait("vector", "t5M")

                def w2_dma(idx):
                    oc_, hq_ = idx // 4, idx % 4
                    sem = "dw2a" if idx % 2 == 0 else "dw2b"
                    semg = sem + "g"
                    if idx >= 2:
                        em.wait("sync", "v5u", h_v5u[idx - 2])
                        em.wait("gpsimd", "v5u", h_v5u[idx - 2])
                    h0 = hq_ * 1024
                    em.dma("sync", lambda sy, oc_=oc_, h0=h0, idx=idx:
                           sy.dma_start(
                               out=wf32[idx % 2][:, 0:512],
                               in_=W2B[oc_, :, h0:h0 + 512]), sem)
                    em.dma("gpsimd", lambda g, oc_=oc_, h0=h0, idx=idx:
                           g.dma_start(
                               out=wf32[idx % 2][:, 512:1024],
                               in_=W2B[oc_, :, h0 + 512:h0 + 1024]), semg)
                    h_dw2.append((em.sem(sem).n, em.sem(semg).n))

                w2_dma(0)
                for oc in range(DC):
                    bank = psum[5 + (oc % 2)]
                    if len(h_o) >= 2:
                        em.wait("tensor", "sc5o", h_o[oc - 2])
                    em.wait("tensor", "v5r")
                    em.wait("tensor", "v5p")
                    for hq in range(4):
                        idx = oc * 4 + hq
                        if idx + 1 < 32:
                            w2_dma(idx + 1)
                        _s2 = "dw2a" if idx % 2 == 0 else "dw2b"
                        em.wait("vector", _s2, h_dw2[idx][0])
                        em.wait("vector", _s2 + "g", h_dw2[idx][1])
                        if len(h_t5q) >= 2:
                            em.wait("vector", "t5q", h_t5q[idx - 2])
                        em.op("vector", lambda v, idx=idx: v.tensor_copy(
                            wfh[idx % 2][:], wf32[idx % 2][:]), "v5u")
                        em.wait("vector", "v5u")
                        em.op("vector", lambda v, idx=idx:
                              v.scalar_tensor_tensor(
                                  out=wfl[idx % 2][:], in0=wf32[idx % 2][:],
                                  scalar=1.0, in1=wfh[idx % 2][:],
                                  op0=ALU.mult, op1=ALU.subtract), "v5u")
                        h_v5u.append(em.sem("v5u").n)
                        em.wait("tensor", "v5u", h_v5u[idx])
                        wh, wl = wfh[idx % 2], wfl[idx % 2]
                        for c in range(DC):
                            hcol = hq * 8 + c
                            whc = wh[:, c * P:(c + 1) * P]
                            wlc = wl[:, c * P:(c + 1) * P]
                            hh = hTh[:, hcol * 512:(hcol + 1) * 512]
                            hl = hTl[:, hcol * 512:(hcol + 1) * 512]
                            for (pi, (sa, mb)) in enumerate(
                                    ((whc, hh), (whc, hl), (wlc, hh))):
                                _first = (hq == 0 and c == 0 and pi == 0)
                                _sem = ("t5q" if (c == DC - 1 and pi == 2)
                                        else None)
                                em.op("tensor", lambda t, sa=sa, mb=mb,
                                      bank=bank, first=_first:
                                      t.matmul(bank[:, 0:512], sa, mb,
                                               start=first, stop=False),
                                      _sem)
                        h_t5q.append(em.sem("t5q").n)
                    i_m2 = oc // 2
                    for j in range(4):
                        e0 = i_m2 * D
                        o0 = (oc % 2) * P
                        sl0 = e0 + 4 * o0 + j
                        r0_ = j * 2048 + i_m2 * 512
                        for (sa, mb, last) in (
                                (lor2h, rrowh, False), (lor2h, rrowl, False),
                                (lor2l, rrowh, j == 3)):
                            em.op("tensor", lambda t, sa=sa, mb=mb, sl0=sl0,
                                  r0_=r0_, bank=bank, last=last: t.matmul(
                                      bank[:, 0:512],
                                      sa[0:1, sl0:sl0 + 509:4],
                                      mb[0:1, r0_:r0_ + 512],
                                      start=False, stop=last),
                                  "t5W" if last else None)
                    h_t5W.append(em.sem("t5W").n)
                    em.wait("scalar", "t5W", h_t5W[oc])
                    em.wait("scalar", "v5o")
                    em.op("scalar", lambda sc, oc=oc, bank=bank:
                          sc.activation(
                              outsT[:, oc * 512:(oc + 1) * 512],
                              bank[:, 0:512], AF.Identity,
                              bias=b2t_sb[:, oc:oc + 1], scale=1.0), "sc5o")
                    h_o.append(em.sem("sc5o").n)
                for kk in range(4):
                    ofk = hTh[:, kk * 2048:kk * 2048 + 2048].bitcast(F32)
                    em.wait("tensor", "sc5o")
                    em.wait("tensor", "v5o")
                    for c in range(DC):
                        pb5 = psum[2 + (c % 2)][:, (c // 2 % 2) * P:
                                                (c // 2 % 2) * P + P]
                        em.wait("tensor", "v5o")
                        em.op("tensor", lambda t, c=c, kk=kk, pb5=pb5:
                              t.transpose(
                                  pb5, outsT[:, c * 512 + kk * P:
                                             c * 512 + (kk + 1) * P],
                                  ident32[:]), "t5o")
                        em.wait("vector", "t5o")
                        em.wait("vector", "t5r")
                        em.wait("vector", "t5q")
                        em.wait("vector", "t5W")
                        em.op("vector", lambda v, c=c, pb5=pb5, ofk=ofk:
                              v.tensor_copy(ofk[:, c * P:(c + 1) * P], pb5),
                              "v5o")
                    em.wait("vector", "v5o")
                    em.op("vector", lambda v, ofk=ofk, kk=kk: v.tensor_reduce(
                        out=fst5[:, kk:kk + 1], in_=ofk[:], axis=AX.X,
                        op=ALU.add, negate=True), "v5o")
                    em.wait("vector", "v5o")
                    em.op("vector", lambda v, kk=kk: v.tensor_scalar_mul(
                        fst5[:, kk:kk + 1], fst5[:, kk:kk + 1], 1.0 / D),
                        "v5o")
                    em.wait("scalar", "v5o")
                    em.wait("scalar", "sc5T")
                    em.op("scalar", lambda sc, ofk=ofk, kk=kk: sc.activation(
                        scr2[:], ofk[:], AF.Square, bias=fst5[:, kk:kk + 1],
                        scale=1.0, accum_out=fss5[:, kk:kk + 1]), "sc5T")
                em.wait("vector", "sc5T")
                em.op("vector", lambda v: v.tensor_scalar(
                    out=fss5[:, 0:4], in0=fss5[:, 0:4], scalar1=1.0 / D,
                    scalar2=EPS, op0=ALU.mult, op1=ALU.add), "v5o")
                em.wait("scalar", "v5o")
                em.op("scalar", lambda sc: sc.activation(
                    fss5[:, 4:8], fss5[:, 0:4], AF.Sqrt, bias=0.0,
                    scale=1.0), "sc5T")
                em.wait("vector", "sc5T")
                em.op("vector", lambda v: v.reciprocal(
                    fst5[:, 4:8], fss5[:, 4:8]), "v5o")
                for _ in range(2):
                    em.wait("vector", "v5o")
                    em.op("vector", lambda v: v.tensor_tensor(
                        out=ftm5[:, 0:4], in0=fst5[:, 4:8],
                        in1=fst5[:, 4:8], op=ALU.mult), "v5o")
                    em.wait("vector", "v5o")
                    em.op("vector", lambda v: v.tensor_tensor(
                        out=ftm5[:, 0:4], in0=ftm5[:, 0:4],
                        in1=fss5[:, 0:4], op=ALU.mult), "v5o")
                    em.wait("vector", "v5o")
                    em.op("vector", lambda v: v.tensor_scalar(
                        out=ftm5[:, 0:4], in0=ftm5[:, 0:4], scalar1=-0.5,
                        scalar2=1.5, op0=ALU.mult, op1=ALU.add), "v5o")
                    em.wait("vector", "v5o")
                    em.op("vector", lambda v: v.tensor_tensor(
                        out=fst5[:, 4:8], in0=fst5[:, 4:8],
                        in1=ftm5[:, 0:4], op=ALU.mult), "v5o")
                for kk in range(4):
                    i_tile = sg * 4 + kk
                    ofk = hTh[:, kk * 2048:kk * 2048 + 2048].bitcast(F32)
                    em.wait("sync", "v5o")
                    em.dma("sync", lambda sy, i_tile=i_tile: sy.dma_start(
                        out=x1t[:], in_=X1SP[i_tile * P:(i_tile + 1) * P, :]),
                        "dx5")
                    em.wait("vector", "dx5")
                    em.wait("vector", "v5o")
                    em.op("vector", lambda v, ofk=ofk, kk=kk: v.tensor_scalar(
                        out=ofk[:], in0=ofk[:], scalar1=fst5[:, kk:kk + 1],
                        scalar2=fst5[:, 4 + kk:5 + kk], op0=ALU.add,
                        op1=ALU.mult), "v5o")
                    em.wait("vector", "v5o")
                    em.wait("vector", "dout")
                    em.op("vector", lambda v, ofk=ofk: v.tensor_tensor(
                        out=outf[:], in0=ofk[:], in1=x1t[:], op=ALU.add),
                        "v5o")
                    em.wait("sync", "v5o")
                    em.dma("sync", lambda sy, i_tile=i_tile: sy.dma_start(
                        out=OUT[i_tile * P:(i_tile + 1) * P, :], in_=outf[:]),
                        "dout")

        if BUILD_PHASES < 2:
            finalize(em)
            fp_es.close()
            run_blocks(nc, em, es)
            return nc

        finalize(em)
        if BUILD_PHASES < 3:
            fp_es.close()
        elif BUILD_PHASES < 4:
            at_es.close()
            ao_es.close()
            mid_es.close()
        else:
            for _es in reversed(open_scopes):
                _es.close()
            mid_es.close()
        run_blocks(nc, em, es)
    return nc


def finalize(em):
    for sname, s in list(em.sems.items()):
        em.wait("sync", s)


def run_blocks(nc, em, es):
    block = es.enter_context(nc.Block())

    @block.sync
    def _(sy):
        for f in em.prog["sync"]:
            f(sy)

    @block.vector
    def _(v):
        for f in em.prog["vector"]:
            f(v)

    @block.scalar
    def _(sc):
        for f in em.prog["scalar"]:
            f(sc)

    @block.tensor
    def _(t):
        for f in em.prog["tensor"]:
            f(t)

    @block.gpsimd
    def _(g):
        for f in em.prog["gpsimd"]:
            f(g)


def stage_inputs(inputs):
    x = np.ascontiguousarray(inputs["L0_attn_x"], dtype=np.float32)
    W1 = np.ascontiguousarray(inputs["L0_ffn_W1"], dtype=np.float32)
    W2 = np.ascontiguousarray(inputs["L0_ffn_W2"], dtype=np.float32)
    W1B = np.ascontiguousarray(
        W1.reshape(HFC, P, DC, P).transpose(0, 3, 2, 1).reshape(HFC, P, D))
    W2B = np.ascontiguousarray(
        W2.reshape(DC, P, 4, DC, P).transpose(0, 4, 2, 3, 1)
        .reshape(DC, P, HFF))
    B1T = np.ascontiguousarray(
        inputs["L0_ffn_b1"].reshape(HFC, P).T.astype(np.float32))
    B2T = np.ascontiguousarray(
        inputs["L0_ffn_b2"].reshape(DC, P).T.astype(np.float32))
    SCLv = np.asarray(inputs["L0_ffn_scale"], np.float32).reshape(1, 1)
    idf32 = np.eye(P, dtype=np.float32)
    idf16 = np.eye(P, dtype=np.float16)
    common = dict(
        WQT=np.ascontiguousarray(inputs["L0_attn_w_q"].T.astype(np.float32)),
        WKT=np.ascontiguousarray(inputs["L0_attn_w_k"].T.astype(np.float32)),
        WVT=np.ascontiguousarray(inputs["L0_attn_w_v"].T.astype(np.float32)),
        WOT=np.ascontiguousarray(inputs["L0_attn_w_o"].T.astype(np.float32)),
        W1B=W1B, W2B=W2B, B1T=B1T, B2T=B2T, SCL=SCLv,
        IDF32=idf32, IDF16=idf16,
    )
    in_maps = []
    for c in range(N_CORES):
        b, j = c // 2, c % 2
        xb = x[b]
        xcat = np.concatenate(
            [xb[j * TOWN:(j + 1) * TOWN], xb[(1 - j) * TOWN:(2 - j) * TOWN],
             xb[0:TEXTRA]], axis=0)
        m = dict(common)
        m["XC"] = np.ascontiguousarray(xcat)
        in_maps.append(m)
    return in_maps


def kernel(**inputs):
    in_maps = stage_inputs(inputs)
    nc = build_program()
    res = run_bass_kernel_spmd(nc, in_maps, core_ids=list(range(N_CORES)))
    out = np.empty((B, S, D), dtype=np.float32)
    for c in range(N_CORES):
        b, j = c // 2, c % 2
        out[b, j * TOWN:(j + 1) * TOWN, :] = res.results[c]["OUT"]
    return out

